# version 17
# baseline (speedup 1.0000x reference)
import numpy as np
import ml_dtypes
from contextlib import ExitStack

import concourse.bass as bass
import concourse.mybir as mybir
from concourse.bass_utils import run_bass_kernel_spmd

F32 = mybir.dt.float32
BF16 = mybir.dt.bfloat16
I32 = mybir.dt.int32
ALU = mybir.AluOpType
AF = mybir.ActivationFunctionType

EPS_ATT = 1e-6
EPS_LN = 1e-5
EPS_TOP = 1e-8

ENGINES = ("pe", "dve", "act", "pool", "sp")


def real_cfg():
    return dict(S=8192, NH=16, M=4, E=8, EI=512)


def small_cfg():
    return dict(S=512, NH=4, M=4, E=8, EI=128)


def derive(cin):
    c = dict(cin)
    c["D"] = 64
    c["H"] = c["NH"] * 64
    c["FC"] = c["H"] // 128
    c["CH"] = 128
    c["SO"] = c["S"] // 2
    c["TILE"] = min(512, c["SO"])
    c["CPT"] = c["TILE"] // c["CH"]
    c["NT"] = c["SO"] // c["TILE"]
    c["NB"] = 1
    c["NBS"] = c["SO"]
    c["OWN"] = c["SO"]
    c["OH"] = c["OWN"] // 2
    c["CT"] = min(512, c["OH"])
    c["TPH"] = c["OH"] // c["CT"]
    c["FOW"] = min(512, c["H"])
    c["NFO"] = c["H"] // c["FOW"]
    c["SI"] = 2 * c["EI"]
    c["EIC"] = c["EI"] // 128
    c["SIC"] = c["SI"] // 128
    return c



class Plan:
    def __init__(self):
        self.items = {e: [] for e in ENGINES}
        self.needed = set()

    def add(self, eng, kind, fn, deps, inc):
        toks = [d for d in deps if d is not None]
        self.items[eng].append([kind, fn, toks, inc])
        return (eng, len(self.items[eng]) - 1)


class Rs:
    def __init__(self, name):
        self.name = name
        self.w = None
        self.readers = []

    def rd(self):
        return [self.w] if self.w is not None else []

    def wd(self):
        d = list(self.readers)
        if self.w is not None:
            d.append(self.w)
        return d

    def did_read(self, tok):
        self.readers.append(tok)

    def did_write(self, tok):
        self.w = tok
        self.readers = []


class K:
    def __init__(self, plan):
        self.p = plan

    def _deps(self, reads, writes, deps):
        d = list(deps)
        for r in reads:
            d += r.rd()
        for w in writes:
            d += w.wd()
        return d

    def op(self, eng, fn, reads=(), writes=(), deps=(), inc=1):
        tok = self.p.add(eng, "op", fn, self._deps(reads, writes, deps), inc)
        for r in reads:
            r.did_read(tok)
        for w in writes:
            w.did_write(tok)
        return tok

    def raw(self, eng, fn, reads=(), writes=(), deps=(), incs=0):
        tok = self.p.add(eng, "raw", fn, self._deps(reads, writes, deps), incs)
        if incs:
            self.p.needed.add(tok)
        for r in reads:
            r.did_read(tok)
        for w in writes:
            w.did_write(tok)
        return tok


def emit_all(nc, plan, es):
    NQ = 8
    for e in ENGINES:
        for i, it in enumerate(plan.items[e]):
            for (de, di) in it[2]:
                if de == e == "pe":
                    continue
                plan.needed.add((de, di))
            if it[0] == "op" and it[3] == 16:
                plan.needed.add((e, i))
    csem = {e: es.enter_context(nc.semaphore(f"sem_{e}_c")) for e in ENGINES}
    dsem = {}
    for e in ("pool", "sp"):
        for q in range(NQ):
            dsem[(e, q)] = es.enter_context(nc.semaphore(f"sem_{e}_d{q}"))
    psem = es.enter_context(nc.semaphore("sem_poll"))
    rawsems = {}

    msval = {}
    pre_wait = {}
    ccnt = {e: 0 for e in ENGINES}
    dcnt = {}
    dn = {e: 0 for e in ENGINES}
    for e in ENGINES:
        for i, it in enumerate(plan.items[e]):
            kind, fn, deps, inc = it
            is_dma = (kind == "op" and inc == 16)
            if kind == "raw" and inc > 0:
                rawsems[(e, i)] = es.enter_context(
                    nc.semaphore(f"sem_raw_{e}_{i}"))
                msval[(e, i)] = ("r", (e, i), inc)
                continue
            if (e, i) not in plan.needed and not is_dma:
                continue
            if is_dma:
                q = dn[e] % NQ
                dn[e] += 1
                prev = dcnt.get((e, q), 0)
                if prev > 0:
                    pre_wait[(e, i)] = (q, prev)
                dcnt[(e, q)] = prev + 16
                msval[(e, i)] = ("d", e, q, dcnt[(e, q)])
            else:
                ccnt[e] += inc
                msval[(e, i)] = ("c", e, ccnt[e])

    block = es.enter_context(nc.Block())
    eng_attr = {"pe": "tensor", "dve": "vector", "act": "scalar",
                "pool": "gpsimd", "sp": "sync"}

    def sem_of(mv):
        if mv[0] == "c":
            return csem[mv[1]], mv[2]
        if mv[0] == "d":
            return dsem[(mv[1], mv[2])], mv[3]
        return rawsems[mv[1]], mv[2]

    def make_body(e):
        def body(eng):
            hw = {}
            def wait_to(sem, val):
                if hw.get(sem.name if hasattr(sem, "name") else id(sem), 0) < val:
                    eng.wait_ge(sem, val)
                    hw[sem.name if hasattr(sem, "name") else id(sem)] = val
            for i, (kind, fn, deps, inc) in enumerate(plan.items[e]):
                want = {}
                for (de, di) in deps:
                    if de == e == "pe":
                        continue
                    mv = msval.get((de, di))
                    if mv is None:
                        continue
                    sem, v = sem_of(mv)
                    kkey = sem.name if hasattr(sem, "name") else id(sem)
                    if v > want.get(kkey, (None, 0))[1]:
                        want[kkey] = (sem, v)
                for kkey in sorted(want):
                    wait_to(*want[kkey])
                if (e, i) in pre_wait:
                    q, v = pre_wait[(e, i)]
                    wait_to(dsem[(e, q)], v)
                if kind == "op":
                    inst = fn(eng)
                    mv = msval.get((e, i))
                    if mv is not None:
                        sem, _ = sem_of(mv)
                        inst.then_inc(sem, inc)
                elif kind == "raw" and inc > 0:
                    fn(eng, rawsems[(e, i)], psem)
                else:
                    fn(eng, None, psem)
            if e == "sp":
                for q in range(NQ):
                    if dcnt.get((e, q), 0) > 0:
                        eng.wait_ge(dsem[(e, q)], dcnt[(e, q)])
        return body

    for e in ENGINES:
        getattr(block, eng_attr[e])(make_body(e))



def build(cin, dbg=False, races=True, phases=None, mparts=None):
    c = derive(cin)
    S, NH, M, E, EI = c["S"], c["NH"], c["M"], c["E"], c["EI"]
    D, H, FC = c["D"], c["H"], c["FC"]
    CH, TILE, CPT, NT = c["CH"], c["TILE"], c["CPT"], c["NT"]
    NB, NBS, OWN, OH = c["NB"], c["NBS"], c["OWN"], c["OH"]
    SO = c["SO"]
    CT, TPH = c["CT"], c["TPH"]
    SI, EIC, SIC = c["SI"], c["EIC"], c["SIC"]
    NFO, FOW = c["NFO"], c["FOW"]
    M65 = M * 65
    NH325 = NH * 325
    NOC = OWN // 128
    CTC = CT // 128

    PH = set(phases) if phases is not None else {"pq0", "state", "main0",
                                                  "mbox", "c"}
    MP = set(mparts) if mparts is not None else {"kv", "skt", "heads", "epi",
                                                 "wo", "ln", "x1t", "rout",
                                                 "hat", "hatm", "hcm", "hni", "hst", "x"}
    nc = bass.Bass("TRN2", target_bir_lowering=False,
                   detect_race_conditions=races)

    hsT = [nc.dram_tensor("hsT_own", [H, SO], F32, kind="ExternalInput").ap()]
    hsT_state = nc.dram_tensor("hsT_state", [H, SO], F32,
                               kind="ExternalInput").ap()
    hstok = [nc.dram_tensor("hs_own", [SO, H], F32, kind="ExternalInput").ap()]
    wT_in = [nc.dram_tensor(n, [H, H], F32, kind="ExternalInput").ap()
             for n in ("wqT", "wkT", "wvT", "woT")]
    memcat = nc.dram_tensor("memcat", [NH, D, M65], F32, kind="ExternalInput").ap()
    wmn = nc.dram_tensor("wmn", [H, NH * M], F32, kind="ExternalInput").ap()
    gate_in = nc.dram_tensor("gate_in", [NH, 1], F32, kind="ExternalInput").ap()
    gwT = nc.dram_tensor("gwT", [H, E], F32, kind="ExternalInput").ap()
    lbb = nc.dram_tensor("lbb", [1, E], F32, kind="ExternalInput").ap()
    egT = nc.dram_tensor("egT", [E, H, EI], F32, kind="ExternalInput").ap()
    euT = nc.dram_tensor("euT", [E, H, EI], F32, kind="ExternalInput").ap()
    edT = nc.dram_tensor("edT", [E, EI, H], F32, kind="ExternalInput").ap()
    sgT = nc.dram_tensor("sgT", [H, SI], F32, kind="ExternalInput").ap()
    suT = nc.dram_tensor("suT", [H, SI], F32, kind="ExternalInput").ap()
    sdT = nc.dram_tensor("sdT", [SI, H], F32, kind="ExternalInput").ap()
    cfgv = nc.dram_tensor("cfgv", [1, 8], I32, kind="ExternalInput").ap()

    out = nc.dram_tensor("out", [OWN, H], F32, kind="ExternalOutput").ap()

    sq_scr_t = nc.dram_tensor("sq_scr", [FC, 128, NBS], BF16, kind="Internal")
    x1t_scr_t = nc.dram_tensor("x1t_scr", [FC, 128, NBS], BF16, kind="Internal")
    sel_scr_t = nc.dram_tensor("sel_scr", [NBS, E], BF16, kind="Internal")
    SSW = (NH // 2) * 65
    sdump_t = nc.dram_tensor("sdump", [2, 128, SSW], F32, kind="Internal")
    crel_in_t = nc.dram_tensor("crel_in", [1, NH * M], F32, kind="Internal")
    crel_out_t = nc.dram_tensor("crel_out", [1, NH * M], F32, kind="Internal")
    cstat_in_t = nc.dram_tensor("cstat_in", [1, 16], F32, kind="Internal")
    cstat_out_t = nc.dram_tensor("cstat_out", [1, 16], F32, kind="Internal")

    dbg_outs = {}
    if dbg:
        dbg_outs["dbg_x1"] = nc.dram_tensor("dbg_x1", [NBS, H], F32,
                                            kind="ExternalOutput").ap()
        dbg_outs["dbg_stats"] = nc.dram_tensor("dbg_stats", [1, 16], F32,
                                               kind="ExternalOutput").ap()
        dbg_outs["dbg_w"] = nc.dram_tensor("dbg_w", [NB, NH, 5], F32,
                                           kind="ExternalOutput").ap()
        dbg_outs["dbg_wmean"] = nc.dram_tensor("dbg_wmean", [1, E], F32,
                                               kind="ExternalOutput").ap()

    with ExitStack() as es:
        ctx = es.enter_context

        W_AR = ctx(nc.sbuf_tensor("W_AR", [128, 4 * FC * H], BF16))
        EXPCOLS = 2 * FC * EI + EIC * H
        W2 = (NH // 2) * 325

        _off = [0]
        _c_marks = {}
        def _carve(n):
            c0 = _off[0]
            _off[0] += (n + 1) & ~1
            return c0
        O_hsT = [_carve(FC * TILE) for _ in range(2)]
        O_sqT = [_carve(FC * TILE) for _ in range(2)]
        O_x1tt = [_carve(FC * TILE) for _ in range(2)]
        O_cb = [_carve(NH325) for _ in range(2)]
        O_rhs = _carve(W2)
        O_skt = [_carve(H) for _ in range(2)]
        O_skT = [_carve(H) for _ in range(2)]
        O_va = [_carve(NH * 65) for _ in range(2)]
        O_am = [_carve(512) for _ in range(2)]
        O_ep = [_carve(H) for _ in range(4)]
        O_outpre = _carve(H)
        O_opT = _carve(H)
        O_x1c = _carve(H)
        O_xsq = _carve(H)
        O_x1r = _carve(2 * H)
        O_hs = [_carve(2 * H) for _ in range(2)]
        O_elu = [_carve(1024) for _ in range(2)]
        O_ssb = _carve((NH // 2) * 65 * 2)
        att_end = _off[0]
        _off[0] = 0
        O_exp = _carve(2 * EXPCOLS)
        O_acc = _carve(2 * (OH // 128) * H)
        O_hb = _carve(SIC * CT)
        O_silu = _carve(CT)
        O_selown = _carve(NOC * E)
        O_xout = _carve(2 * H)
        O_x2r = _carve(2 * H)
        O_x1tok = _carve(H)
        UNW = max(att_end, _off[0])
        UN = ctx(nc.sbuf_tensor("UN", [128, UNW], BF16))
        def _v(o, n):
            return UN[:, o:o + n]
        def _vf(o, n):
            return UN[:, o:o + 2 * n].bitcast(F32)
        hsT_b = [_v(O_hsT[i], FC * TILE) for i in range(2)]
        sqT_b = [_v(O_sqT[i], FC * TILE) for i in range(2)]
        x1t_tile = [_v(O_x1tt[i], FC * TILE) for i in range(2)]
        cb_b = [_v(O_cb[i], NH325) for i in range(2)]
        rhscat = _v(O_rhs, W2)
        skt_b = [_v(O_skt[i], H) for i in range(2)]
        skT_b = [_v(O_skT[i], H) for i in range(2)]
        vaug_b = [_v(O_va[i], NH * 65) for i in range(2)]
        am_b = [_v(O_am[i], 512) for i in range(2)]
        ep_t = [_v(O_ep[i], H) for i in range(4)]
        outpre_b = _v(O_outpre, H)
        opT_b = _v(O_opT, H)
        x1c_b = _v(O_x1c, H)
        xsq_scr = _v(O_xsq, H)
        x1r_b = _vf(O_x1r, H)
        hs_b = [_vf(O_hs[i], H) for i in range(2)]
        elu_t1 = _vf(O_elu[0], 512)
        elu_t2 = _vf(O_elu[1], 512)
        ssb = _vf(O_ssb, (NH // 2) * 65)
        EXP_AR = _v(O_exp, 2 * EXPCOLS)
        accv_full = _vf(O_acc, (OH // 128) * H)
        _hbv = _v(O_hb, SIC * CT)
        h_b = [_hbv, _hbv]
        silu_b = _v(O_silu, CT)
        sel_own = _v(O_selown, NOC * E)
        xout = _vf(O_xout, H)
        x2r_b = _vf(O_x2r, H)
        x1tok_b = _v(O_x1tok, H)
        sqs_tile = ctx(nc.sbuf_tensor("sqs_tile", [128, FC], F32))
        sqsum = ctx(nc.sbuf_tensor("sqsum", [128, FC], F32))
        sqsum_bf = ctx(nc.sbuf_tensor("sqsum_bf", [128, FC], BF16))
        wmn_sb = ctx(nc.sbuf_tensor("wmn_sb", [128, FC * NH * M], BF16))
        gwt_sb = ctx(nc.sbuf_tensor("gwt_sb", [128, FC * E], BF16))
        biasb = ctx(nc.sbuf_tensor("biasb", [128, E], F32))
        coefb = ctx(nc.sbuf_tensor("coefb", [128, NH * 5], F32))
        wmeanb = ctx(nc.sbuf_tensor("wmeanb", [128, E], F32))
        sc_raw = ctx(nc.sbuf_tensor("sc_raw", [128, NH * 5], F32))
        sc_clip = ctx(nc.sbuf_tensor("sc_clip", [128, NH * 5], F32))
        sc_scale = ctx(nc.sbuf_tensor("sc_scale", [128, NH * 5], F32))
        lncol = ctx(nc.sbuf_tensor("lncol", [128, 8], F32))
        sco_p = ctx(nc.sbuf_tensor("sco_p", [128, E], F32))
        sco_e = ctx(nc.sbuf_tensor("sco_e", [128, E], F32))
        scores = ctx(nc.sbuf_tensor("scores", [128, E], F32))
        rmx = ctx(nc.sbuf_tensor("rmx", [128, 4], F32))
        ismax = ctx(nc.sbuf_tensor("ismax", [128, E], F32))
        smask = ctx(nc.sbuf_tensor("smask", [128, E], F32))
        self32 = ctx(nc.sbuf_tensor("self32", [128, E], F32))
        wsel = ctx(nc.sbuf_tensor("wsel", [128, 16], F32))
        wselacc = ctx(nc.sbuf_tensor("wselacc", [128, 16], F32))
        ones_col_f = ctx(nc.sbuf_tensor("ones_col_f", [128, 1], F32))
        sel_b = [ctx(nc.sbuf_tensor(f"sel_b{i}", [128, CPT * E], BF16)) for i in range(2)]
        statacc = ctx(nc.sbuf_tensor("statacc", [16, 1], F32))
        g16 = ctx(nc.sbuf_tensor("g16", [NH, 1], F32))
        omg16 = ctx(nc.sbuf_tensor("omg16", [NH, 1], F32))
        gtmp = ctx(nc.sbuf_tensor("gtmp", [NH, 4], F32))
        relrow = ctx(nc.sbuf_tensor("relrow", [1, NH * M], F32))
        relt16 = ctx(nc.sbuf_tensor("relt16", [NH, M], F32))
        coef16 = ctx(nc.sbuf_tensor("coef16", [NH, 5], F32))
        rowbuf = ctx(nc.sbuf_tensor("rowbuf", [1, NH * 5], F32))
        rowbuf_bf = ctx(nc.sbuf_tensor("rowbuf_bf", [1, NH * 5], BF16))
        ones_row = ctx(nc.sbuf_tensor("ones_row", [1, 128], BF16))
        ones_col = ctx(nc.sbuf_tensor("ones_col", [128, 1], BF16))
        identb = ctx(nc.sbuf_tensor("identb", [128, 128], BF16))
        trib = ctx(nc.sbuf_tensor("trib", [128, 512], BF16))
        lbb_sb = ctx(nc.sbuf_tensor("lbb_sb", [1, E], F32))
        lbb_bf = ctx(nc.sbuf_tensor("lbb_bf", [1, E], BF16))
        epsb = ctx(nc.sbuf_tensor("epsb", [128, 1], F32))
        cfg_sb = ctx(nc.sbuf_tensor("cfg_sb", [1, 8], I32))
        mrow = ctx(nc.sbuf_tensor("mrow", [1, 64], F32))
        wm_row = ctx(nc.sbuf_tensor("wm_row", [1, E], F32))
        wm_bf = ctx(nc.sbuf_tensor("wm_bf", [1, E], BF16))
        wtil = ctx(nc.sbuf_tensor("wtil", [128, 1], F32))
        dbg_xout = (ctx(nc.sbuf_tensor("dbg_xout", [128, H], F32))
                    if dbg else None)

        PW = ctx(nc.psum_tensor("PW", [128, 1024], F32))
        PN = ctx(nc.psum_tensor("PN", [128, 1024], F32))
        PA = ctx(nc.psum_tensor("PA", [128, 512], F32))
        PT = ctx(nc.psum_tensor("PT", [128, 1024], BF16))
        PS = ctx(nc.psum_tensor("PS", [128, 512], F32))
        PS2 = ctx(nc.psum_tensor("PS2", [128, 512], F32))

        plan = Plan()
        k = K(plan)

        r_war = Rs("W_AR")
        r_exp = [Rs(f"EXP{i}") for i in range(2)]
        r_acc = Rs("ACC")
        r_hsT = [Rs(f"hsT{i}") for i in range(2)]
        r_sqT = [Rs(f"sqT{i}") for i in range(2)]
        r_hs = [Rs(f"hs{i}") for i in range(2)]
        r_skt = [Rs(f"skt{i}") for i in range(2)]
        r_skT = [Rs(f"skT{i}") for i in range(2)]
        r_vaug = [Rs(f"vaug{i}") for i in range(2)]
        r_am = [Rs(f"am{i}") for i in range(2)]
        r_cb = [Rs(f"cb{i}") for i in range(2)]
        r_rhs_mem = Rs("rhs_mem")
        r_rhs_S = Rs("rhs_S")
        r_ssb = Rs("ssb")
        r_outpre = Rs("outpre")
        r_opT = Rs("opT")
        r_x1r = Rs("x1r")
        r_x1c = Rs("x1c")
        r_x1tt = [Rs(f"x1tt{i}") for i in range(2)]
        r_elu1 = Rs("elu1")
        r_elu2 = Rs("elu2")
        r_sqs = Rs("sqs")
        r_sqsum = Rs("sqsum")
        r_sqsum_bf = Rs("sqsum_bf")
        r_consts = Rs("consts")
        r_coefb = Rs("coefb")
        r_scraw = Rs("scraw")
        r_scclip = Rs("scclip")
        r_scscale = Rs("scscale")
        r_ept = [Rs(f"ept{i}") for i in range(4)]
        r_xsq = Rs("xsq")
        r_lncol = Rs("lncol")
        r_scop = Rs("scop")
        r_scoe = Rs("scoe")
        r_scores = Rs("scores")
        r_rmx = Rs("rmx")
        r_ismax = Rs("ismax")
        r_smask = Rs("smask")
        r_self32 = Rs("self32")
        r_wsel = Rs("wsel")
        r_wselacc = Rs("wselacc")
        r_sel_b = [Rs(f"selb{i}") for i in range(2)]
        r_statacc = Rs("statacc")
        r_small = Rs("small16")
        r_relrow = Rs("relrow")
        r_selown = Rs("selown")
        _rhb = Rs("hb")
        r_hb = [_rhb, _rhb]
        r_silu = Rs("silu")
        r_wtil = Rs("wtil")
        r_xout = Rs("xout")
        r_x2r = Rs("x2r")
        r_x1tok = Rs("x1tok")
        r_mrow = Rs("mrow")
        r_both = Rs("both")
        r_wmrow = Rs("wmrow")
        r_wmeanb = Rs("wmeanb")
        r_PW = [Rs("PW0"), Rs("PW1")]
        r_PN = [Rs("PN0"), Rs("PN1")]
        r_PA = Rs("PA")
        r_PT = Rs("PT")
        r_PS = Rs("PS")
        r_PS2 = Rs("PS2")
        r_sqscr = Rs("sq_scr")
        r_x1scr_h = [Rs("x1_scr0"), Rs("x1_scr1")]
        r_war_h = [Rs("W_AR_a"), Rs("W_AR_b")]
        r_selscr = Rs("sel_scr")
        r_sdump = Rs("sdump")
        r_crel = Rs("crel")
        r_cstat = Rs("cstat")

        def fn_wl4(g, sem, psem):
            for wi in range(4):
                g.dma_start(
                    out=bass.AP(W_AR, wi * FC * H, [[4 * FC * H, 128], [H, FC], [1, H]]),
                    in_=bass.AP(wT_in[wi].tensor, 0, [[H, 128], [128 * H, FC], [1, H]])
                ).then_inc(sem, 16)
        k.raw("pool", fn_wl4, writes=[r_war], incs=64)

        for h2 in range(2):
            def fn_memcat(g, h2=h2):
                return g.dma_start(
                    out=rhscat[h2 * 64:h2 * 64 + 64, :].rearrange(
                        "p (a b) -> p a b", b=325)[:, :, 65:325],
                    in_=bass.AP(memcat.tensor, h2 * D * M65,
                                [[M65, 64], [2 * D * M65, NH // 2], [1, M65]]))
            k.op("pool", fn_memcat, writes=[r_rhs_mem], inc=16)

        def fn_wmn(g):
            return g.dma_start(
                out=bass.AP(wmn_sb, 0, [[FC * NH * M, 128], [NH * M, FC], [1, NH * M]]),
                in_=bass.AP(wmn.tensor, 0, [[NH * M, 128], [128 * NH * M, FC], [1, NH * M]]))
        k.op("pool", fn_wmn, writes=[r_consts], inc=16)

        def fn_gwt(g):
            return g.dma_start(
                out=bass.AP(gwt_sb, 0, [[FC * E, 128], [E, FC], [1, E]]),
                in_=bass.AP(gwT.tensor, 0, [[E, 128], [128 * E, FC], [1, E]]))
        k.op("pool", fn_gwt, writes=[r_consts], inc=16)

        t_gate = k.op("pool", lambda g: g.dma_start(out=g16[:], in_=gate_in),
                      writes=[r_small], inc=16)
        t_lbb = k.op("pool", lambda g: g.dma_start(out=lbb_sb[:], in_=lbb),
                     writes=[r_small], inc=16)
        k.op("pool", lambda g: g.dma_start(out=cfg_sb[:], in_=cfgv),
             writes=[r_small], inc=16)

        k.op("pool", lambda g: g.memset(ones_row[:], 1.0), writes=[r_consts])
        k.op("pool", lambda g: g.memset(ones_col[:], 1.0), writes=[r_consts])
        k.op("pool", lambda g: g.memset(ones_col_f[:], 1.0), writes=[r_consts])
        k.op("pool", lambda g: g.memset(identb[:], 1.0), writes=[r_consts])
        k.op("pool", lambda g: g.affine_select(identb[:], identb[:], [[1, 128]],
                                               ALU.is_equal, 0.0, base=0,
                                               channel_multiplier=-1),
             writes=[r_consts])
        k.op("pool", lambda g: g.memset(trib[:], 1.0), writes=[r_consts])
        for kk in range(4):
            def fn_tri(g, kk=kk):
                v = trib[:, kk * 128:(kk + 1) * 128]
                return g.affine_select(v, v, [[1, 128]], ALU.is_ge, 0.0,
                                       base=0, channel_multiplier=-1)
            k.op("pool", fn_tri, writes=[r_consts])
        for i in range(2):
            def fn_vone(g, i=i):
                return g.memset(
                    vaug_b[i].rearrange("p (h c) -> p h c", c=65)[:, :, 64:65], 1.0)
            k.op("pool", fn_vone, writes=[r_vaug[i]])

        k.op("pool", lambda g: g.memset(statacc[:], 0.0), writes=[r_statacc])
        k.op("pool", lambda g: g.memset(wselacc[:], 0.0), writes=[r_wselacc])
        k.op("pool", lambda g: g.memset(epsb[:], EPS_LN), writes=[r_consts])

        k.op("act", lambda s_: s_.activation(gtmp[:, 0:1], g16[:], AF.Exp, scale=-1.0),
             reads=[r_small], writes=[r_small], deps=[t_gate])
        k.op("dve", lambda v: v.tensor_scalar_add(gtmp[:, 1:2], gtmp[:, 0:1], 1.0),
             reads=[r_small], writes=[r_small])
        k.op("dve", lambda v: v.reciprocal(g16[:], gtmp[:, 1:2]),
             reads=[r_small], writes=[r_small])
        k.op("dve", lambda v: v.tensor_scalar(omg16[:], g16[:], -1.0, 1.0,
                                              ALU.mult, ALU.add),
             reads=[r_small], writes=[r_small])

        k.op("dve", lambda v: v.tensor_copy(lbb_bf[:], lbb_sb[:]),
             reads=[r_small], writes=[r_small], deps=[t_lbb])
        k.op("pe", lambda t_: t_.matmul(PA[:, 0:E], ones_row[:], lbb_bf[:],
                                        start=True, stop=True),
             reads=[r_consts, r_small], writes=[r_PA])
        k.op("act", lambda s_: s_.copy(biasb[:], PA[:, 0:E]),
             reads=[r_PA], writes=[r_consts])

        def sq_scr_ap(bi, t0):
            return bass.AP(sq_scr_t, bi * S + t0,
                           [[NBS, 128], [128 * NBS, FC], [1, TILE]])

        def x1_scr_ap(bi, t0):
            return bass.AP(x1t_scr_t, bi * S + t0,
                           [[NBS, 128], [128 * NBS, FC], [1, TILE]])

        pending_tail = [None, None]

        def emit_tail(idx):
            fn, pending_tail[idx] = pending_tail[idx], None
            if fn is not None:
                fn()

        rel_tail = [None]
        for bi in range(NB):
            if f"pq{bi}" not in PH:
                continue
            k.op("pool", lambda g: g.memset(sqsum[:], 0.0), writes=[r_sqsum])
            def emit_pq_load(t, bi=bi):
                tb = t % 2
                t0 = t * TILE
                def fn_hsT(g, bi=bi, t0=t0, tb=tb):
                    return g.dma_start(
                        out=hsT_b[tb][:],
                        in_=bass.AP(hsT[bi].tensor, t0,
                                    [[SO, 128], [128 * SO, FC], [1, TILE]]))
                k.op("pool", fn_hsT, writes=[r_hsT[tb]], inc=16)
            emit_pq_load(0)
            for t in range(NT):
                tb = t % 2
                t0 = t * TILE
                if t + 1 < NT:
                    emit_pq_load(t + 1)

                for foc in range(FC):
                    pw = foc % 2
                    for fic in range(FC):
                        def fn_mm(t_, foc=foc, fic=fic, tb=tb, pw=pw):
                            return t_.matmul(
                                PW[:, pw * 512:pw * 512 + 512],
                                W_AR[:, fic * H + foc * 128:fic * H + foc * 128 + 128],
                                hsT_b[tb][:, fic * TILE:fic * TILE + TILE],
                                start=(fic == 0), stop=(fic == FC - 1))
                        k.op("pe", fn_mm, reads=[r_war, r_hsT[tb]], writes=[r_PW[pw]])
                    def fn_nrelu(s_, pw=pw):
                        return s_.activation(elu_t1[:],
                                             PW[:, pw * 512:pw * 512 + 512],
                                             AF.Relu, scale=-1.0)
                    k.op("act", fn_nrelu, reads=[r_PW[pw]], writes=[r_elu1])
                    k.op("act", lambda s_: s_.activation(elu_t2[:], elu_t1[:],
                                                         AF.Exp, scale=-1.0),
                         reads=[r_elu1], writes=[r_elu2])
                    def fn_comb(v, pw=pw, foc=foc, tb=tb):
                        return v.scalar_tensor_tensor(
                            sqT_b[tb][:, foc * TILE:foc * TILE + TILE],
                            PW[:, pw * 512:pw * 512 + 512], 0.0, elu_t2[:],
                            ALU.max, ALU.add,
                            accum_out=sqs_tile[:, foc:foc + 1])
                    k.op("dve", fn_comb, reads=[r_PW[pw], r_elu2],
                         writes=[r_sqT[tb], r_sqs])
                k.op("dve", lambda v: v.tensor_add(sqsum[:], sqsum[:], sqs_tile[:]),
                     reads=[r_sqs], writes=[r_sqsum])
                def fn_store(g, bi=bi, t0=t0, tb=tb):
                    return g.dma_start(out=sq_scr_ap(bi, t0), in_=sqT_b[tb][:])
                k.op("sp", fn_store, reads=[r_sqT[tb]], writes=[r_sqscr], inc=16)

            if "state" in PH:
                k.op("pool", lambda g: g.memset(ssb[:], 0.0), writes=[r_ssb])
                def fn_zdump(g):
                    return g.dma_start(
                        out=bass.AP(sdump_t, 0, [[SSW, 128], [1, SSW]]),
                        in_=ssb[:])
                k.op("sp", fn_zdump, reads=[r_ssb], writes=[r_sdump], inc=16)
                def emit_state_load(t):
                    tb = t % 2
                    t0 = t * TILE
                    def fn_hsTs(g, t0=t0, tb=tb):
                        return g.dma_start(
                            out=hsT_b[tb][:],
                            in_=bass.AP(hsT_state.tensor, t0,
                                        [[SO, 128], [128 * SO, FC], [1, TILE]]))
                    k.op("pool", fn_hsTs, writes=[r_hsT[tb]], inc=16)
                emit_state_load(0)
                for t in range(NT):
                    tb = t % 2
                    t0 = t * TILE
                    if t + 1 < NT:
                        emit_state_load(t + 1)
                    for ci in range(CPT):
                        cbp = ci % 2
                        for wofs in (1, 2):
                            for half in range(NFO):
                                pw = half % 2
                                for fic in range(FC):
                                    def fn_mm(t_, wofs=wofs, half=half, fic=fic,
                                              tb=tb, ci=ci, pw=pw):
                                        return t_.matmul(
                                            PW[:, pw * 512:pw * 512 + FOW],
                                            hsT_b[tb][:, fic * TILE + ci * 128:
                                                      fic * TILE + ci * 128 + 128],
                                            W_AR[:, wofs * FC * H + fic * H + half * FOW:
                                                 wofs * FC * H + fic * H + half * FOW + FOW],
                                            start=(fic == 0), stop=(fic == FC - 1))
                                    k.op("pe", fn_mm, reads=[r_war, r_hsT[tb]],
                                         writes=[r_PW[pw]])
                                if wofs == 1:
                                    def fn_nrelu(s_, pw=pw):
                                        return s_.activation(
                                            elu_t1[:, 0:FOW],
                                            PW[:, pw * 512:pw * 512 + FOW],
                                            AF.Relu, scale=-1.0)
                                    k.op("act", fn_nrelu, reads=[r_PW[pw]],
                                         writes=[r_elu1])
                                    k.op("act",
                                         lambda s_: s_.activation(elu_t2[:, 0:FOW],
                                                                  elu_t1[:, 0:FOW],
                                                                  AF.Exp,
                                                                  scale=-1.0),
                                         reads=[r_elu1], writes=[r_elu2])
                                    def fn_comb(v, pw=pw, half=half, cbp=cbp):
                                        return v.scalar_tensor_tensor(
                                            skt_b[cbp][:, half * FOW:half * FOW + FOW],
                                            PW[:, pw * 512:pw * 512 + FOW], 0.0,
                                            elu_t2[:, 0:FOW],
                                            ALU.max, ALU.add)
                                    k.op("dve", fn_comb, reads=[r_PW[pw], r_elu2],
                                         writes=[r_skt[cbp]])
                                else:
                                    def fn_ve(v, pw=pw, half=half, cbp=cbp):
                                        nh_g = FOW // 64
                                        return v.tensor_copy(
                                            vaug_b[cbp].rearrange(
                                                "p (h c) -> p h c", c=65)[
                                                :, half * nh_g:(half + 1) * nh_g, 0:64],
                                            PW[:, pw * 512:pw * 512 + FOW])
                                    k.op("dve", fn_ve, reads=[r_PW[pw]],
                                         writes=[r_vaug[cbp]])
                        evens = [h for h in range(NH) if h % 2 == 0]
                        odds = [h for h in range(NH) if h % 2 == 1]
                        groups = []
                        for lst in (evens, odds):
                            for gi in range(0, len(lst), 4):
                                groups.append(lst[gi:gi + 4])
                        for hg, heads in enumerate(groups):
                            p2 = heads[0] % 2
                            gl = len(heads)
                            sup = heads[0] // 8
                            for j, h in enumerate(heads):
                                def fn_st(t_, h=h, j=j, p2=p2, cbp=cbp, gl=gl):
                                    kw = {}
                                    bank = PS if p2 == 0 else PS2
                                    if p2 == 1:
                                        kw["tile_position"] = (0, 64)
                                    return t_.matmul(
                                        bank[p2 * 64:p2 * 64 + 64,
                                             j * 65:j * 65 + 65],
                                        skt_b[cbp][:, h * 64:h * 64 + 64],
                                        vaug_b[cbp][:, h * 65:h * 65 + 65],
                                        start=(j == 0), stop=(j == gl - 1), **kw)
                                k.op("pe", fn_st, reads=[r_skt[cbp], r_vaug[cbp]],
                                     writes=[r_PS if p2 == 0 else r_PS2])
                            def fn_sa(v, p2=p2, sup=sup, gl=gl):
                                bank = PS if p2 == 0 else PS2
                                return v.tensor_add(
                                    ssb[p2 * 64:p2 * 64 + 64,
                                        sup * 260:sup * 260 + gl * 65],
                                    ssb[p2 * 64:p2 * 64 + 64,
                                        sup * 260:sup * 260 + gl * 65],
                                    bank[p2 * 64:p2 * 64 + 64, 0:gl * 65])
                            k.op("dve", fn_sa, reads=[r_PS if p2 == 0 else r_PS2],
                                 writes=[r_ssb])
                def fn_sdump(g):
                    return g.dma_start(
                        out=bass.AP(sdump_t, 128 * SSW, [[SSW, 128], [1, SSW]]),
                        in_=ssb[:])
                k.op("sp", fn_sdump, reads=[r_ssb], writes=[r_sdump], inc=16)
                def fn_sreload(g, sem, psem):
                    with g.register("sr") as sr, g.register("soff") as soff:
                        g.reg_load(sr, cfg_sb[:1, 0:1])
                        g.reg_mul(soff, sr, 128 * SSW)
                        g.dma_start(
                            out=ssb[:],
                            in_=bass.AP(sdump_t, soff, [[SSW, 128], [1, SSW]])
                        ).then_inc(sem, 16)
                k.raw("pool", fn_sreload, reads=[r_small, r_sdump],
                      writes=[r_ssb], incs=16)
                def fn_sc0(v):
                    return v.tensor_copy(
                        rhscat.rearrange("p (a b) -> p a b", b=325)[:, :, 0:65],
                        ssb.rearrange("p (a b) -> p a b", b=65))
                k.op("dve", fn_sc0, reads=[r_ssb], writes=[r_rhs_S])

            k.op("dve", lambda v: v.tensor_copy(sqsum_bf[:], sqsum[:]),
                 reads=[r_sqsum], writes=[r_sqsum_bf])
            for fic in range(FC):
                def fn_rel(t_, fic=fic):
                    return t_.matmul(PA[0:1, 0:NH * M],
                                     sqsum_bf[:, fic:fic + 1],
                                     wmn_sb[:, fic * NH * M:(fic + 1) * NH * M],
                                     start=(fic == 0), stop=(fic == FC - 1))
                k.op("pe", fn_rel, reads=[r_sqsum_bf, r_consts], writes=[r_PA])
            k.op("act", lambda s_: s_.copy(relrow[:], PA[0:1, 0:NH * M]),
                 reads=[r_PA], writes=[r_relrow])
            def fn_relcc_a(g, sem, psem):
                g.dma_start(out=crel_in_t[:, :], in_=relrow[:]).then_inc(sem, 16)
                g.wait_ge(sem, 16)
                g.collective_compute(
                    "AllReduce", ALU.add,
                    replica_groups=[[0, 1], [2, 3], [4, 5], [6, 7]],
                    ins=[crel_in_t.ap().opt()], outs=[crel_out_t.ap().opt()],
                ).then_inc(sem)
            k.raw("pool", fn_relcc_a, reads=[r_relrow, r_crel],
                  writes=[r_crel], incs=17)
            def emit_rel_tail():
                def fn_relcc_b(g, sem, psem):
                    g.dma_start(out=relrow[:], in_=crel_out_t[:, :]).then_inc(sem, 16)
                k.raw("pool", fn_relcc_b, reads=[r_crel],
                      writes=[r_relrow], incs=16)
                def fn_r16(g):
                    return g.dma_start(
                        out=bass.AP(relt16, 0, [[M, NH], [1, M]]),
                        in_=bass.AP(relrow, 0, [[NH * M, 1], [M, NH], [1, M]]))
                k.op("pool", fn_r16, reads=[r_relrow], writes=[r_small], inc=16)
                k.op("dve", lambda v: v.reduce_max(gtmp[:, 0:1], relt16[:],
                                                   axis=mybir.AxisListType.X),
                     reads=[r_small], writes=[r_small])
                k.op("dve", lambda v: v.tensor_scalar(gtmp[:, 1:2], gtmp[:, 0:1],
                                                      -1.0 / S, None, ALU.mult),
                     reads=[r_small], writes=[r_small])
                k.op("act", lambda s_: s_.activation(relt16[:], relt16[:], AF.Exp,
                                                     bias=gtmp[:, 1:2], scale=1.0 / S),
                     reads=[r_small], writes=[r_small])
                k.op("dve", lambda v: v.reduce_sum(gtmp[:, 2:3], relt16[:],
                                                   axis=mybir.AxisListType.X),
                     reads=[r_small], writes=[r_small])
                k.op("dve", lambda v: v.reciprocal(gtmp[:, 3:4], gtmp[:, 2:3]),
                     reads=[r_small], writes=[r_small])
                k.op("dve", lambda v: v.tensor_scalar(relt16[:], relt16[:], gtmp[:, 3:4],
                                                      None, ALU.mult),
                     reads=[r_small], writes=[r_small])
                k.op("dve", lambda v: v.tensor_copy(coef16[:, 0:1], omg16[:]),
                     reads=[r_small], writes=[r_small])
                k.op("dve", lambda v: v.tensor_scalar(coef16[:, 1:1 + M], relt16[:],
                                                      g16[:], None, ALU.mult),
                     reads=[r_small], writes=[r_small])
                if dbg:
                    def fn_dw(g, bi=bi):
                        return g.dma_start(
                            out=bass.AP(dbg_outs["dbg_w"].tensor, bi * NH * 5,
                                        [[5, NH], [1, 5]]),
                            in_=bass.AP(coef16, 0, [[5, NH], [1, 5]]))
                    k.op("sp", fn_dw, reads=[r_small], inc=16)
                def fn_row(g):
                    return g.dma_start(
                        out=bass.AP(rowbuf, 0, [[NH * 5, 1], [5, NH], [1, 5]]),
                        in_=bass.AP(coef16, 0, [[5, NH], [1, 5]]))
                k.op("pool", fn_row, reads=[r_small], writes=[r_small], inc=16)
                k.op("dve", lambda v: v.tensor_copy(rowbuf_bf[:], rowbuf[:]),
                     reads=[r_small], writes=[r_small])
                k.op("pe", lambda t_: t_.matmul(PA[:, 0:NH * 5], ones_row[:], rowbuf_bf[:],
                                                start=True, stop=True),
                     reads=[r_consts, r_small], writes=[r_PA])
                k.op("act", lambda s_: s_.copy(coefb[:], PA[:, 0:NH * 5]),
                     reads=[r_PA], writes=[r_coefb])
            rel_tail[0] = emit_rel_tail

            if f"main{bi}" not in PH:
                continue
            if "state" not in PH:
                k.op("pool", lambda g: g.memset(ssb[:], 0.0), writes=[r_ssb])
                def fn_srst(g):
                    return g.memset(
                        rhscat.rearrange("p (a b) -> p a b", b=325)[:, :, 0:65], 0.0)
                k.op("pool", fn_srst, writes=[r_rhs_S])

            def emit_main_loads(t, bi=bi):
                tb = t % 2
                t0 = t * TILE
                def fn_hsT(g, bi=bi, t0=t0, tb=tb):
                    return g.dma_start(
                        out=hsT_b[tb][:],
                        in_=bass.AP(hsT[bi].tensor, t0,
                                    [[SO, 128], [128 * SO, FC], [1, TILE]]))
                k.op("pool", fn_hsT, writes=[r_hsT[tb]], inc=16)
                def fn_sqTl(g, bi=bi, t0=t0, tb=tb):
                    return g.dma_start(out=sqT_b[tb][:], in_=sq_scr_ap(bi, t0))
                k.op("sp", fn_sqTl, reads=[r_sqscr], writes=[r_sqT[tb]], inc=16)

            emit_main_loads(0)
            for t in range(NT):
                tb = t % 2
                t0 = t * TILE
                if t + 1 < NT:
                    emit_main_loads(t + 1)

                for ci in range(CPT):
                    cbp = ci % 2
                    ch0 = t0 + ci * CH
                    def fn_hs(g, bi=bi, ch0=ch0, cbp=cbp):
                        return g.dma_start(
                            out=hs_b[cbp][:],
                            in_=bass.AP(hstok[bi].tensor, ch0 * H, [[H, 128], [1, H]]))
                    k.op("sp", fn_hs, writes=[r_hs[cbp]], inc=16)

                    for wofs in ((1, 2) if "kv" in MP else ()):
                        for half in range(NFO):
                            pw = half % 2
                            bank = PW if wofs == 1 else PN
                            rbank = r_PW[pw] if wofs == 1 else r_PN[pw]
                            for fic in range(FC):
                                def fn_mm(t_, wofs=wofs, half=half, fic=fic,
                                          tb=tb, ci=ci, pw=pw, bank=bank):
                                    return t_.matmul(
                                        bank[:, pw * 512:pw * 512 + FOW],
                                        hsT_b[tb][:, fic * TILE + ci * 128:
                                                  fic * TILE + ci * 128 + 128],
                                        W_AR[:, wofs * FC * H + fic * H + half * FOW:
                                             wofs * FC * H + fic * H + half * FOW + FOW],
                                        start=(fic == 0), stop=(fic == FC - 1))
                                k.op("pe", fn_mm, reads=[r_war, r_hsT[tb]],
                                     writes=[rbank])
                            if wofs == 1:
                                def fn_nrelu(s_, pw=pw):
                                    return s_.activation(
                                        elu_t1[:, 0:FOW],
                                        PW[:, pw * 512:pw * 512 + FOW],
                                        AF.Relu, scale=-1.0)
                                k.op("act", fn_nrelu, reads=[r_PW[pw]],
                                     writes=[r_elu1])
                                k.op("act",
                                     lambda s_: s_.activation(elu_t2[:, 0:FOW],
                                                              elu_t1[:, 0:FOW],
                                                              AF.Exp, scale=-1.0),
                                     reads=[r_elu1], writes=[r_elu2])
                                def fn_comb(v, pw=pw, half=half, cbp=cbp):
                                    return v.scalar_tensor_tensor(
                                        skt_b[cbp][:, half * FOW:half * FOW + FOW],
                                        PW[:, pw * 512:pw * 512 + FOW], 0.0,
                                        elu_t2[:, 0:FOW],
                                        ALU.max, ALU.add)
                                k.op("dve", fn_comb, reads=[r_PW[pw], r_elu2],
                                     writes=[r_skt[cbp]])
                            else:
                                def fn_ve(s_, pw=pw, half=half, cbp=cbp):
                                    nh_g = FOW // 64
                                    return s_.copy(
                                        vaug_b[cbp].rearrange(
                                            "p (h c) -> p h c", c=65)[
                                            :, half * nh_g:(half + 1) * nh_g, 0:64],
                                        PN[:, pw * 512:pw * 512 + FOW])
                                k.op("act", fn_ve, reads=[r_PN[pw]], writes=[r_vaug[cbp]])

                    for fic in (range(FC) if "skt" in MP else ()):
                        def fn_tp(t_, fic=fic, cbp=cbp):
                            return t_.transpose(PT[:, fic * 128:fic * 128 + 128],
                                                skt_b[cbp][:, fic * 128:fic * 128 + 128],
                                                identb[:])
                        k.op("pe", fn_tp, reads=[r_skt[cbp], r_consts], writes=[r_PT])
                    if "skt" in MP:
                        def fn_tpe(s_, cbp=cbp):
                            return s_.copy(skT_b[cbp][:, 0:H], PT[:, 0:H])
                        k.op("act", fn_tpe, reads=[r_PT], writes=[r_skT[cbp]])

                    emit_tail(0)

                    evens = [h for h in range(NH) if h % 2 == 0]
                    odds = [h for h in range(NH) if h % 2 == 1]
                    groups = []
                    for lst in (evens, odds):
                        for gi in range(0, len(lst), 4):
                            groups.append(lst[gi:gi + 4])
                    if "heads" not in MP:
                        groups = []
                    for hg, heads in enumerate(groups):
                        p2 = heads[0] % 2
                        pr = p2 * 64
                        gl = len(heads)
                        sup = heads[0] // 8
                        hat_list = list(enumerate(heads)) if "hat" in MP else []
                        if "hat1" in MP:
                            hat_list = hat_list[:1]
                        for j, h in hat_list:
                            fic_h = h // 2
                            def fn_at(t_, j=j, fic_h=fic_h, pr=pr, cbp=cbp,
                                      tb=tb, ci=ci):
                                return t_.matmul(
                                    PA[:, j * 128:j * 128 + 128],
                                    skT_b[cbp][pr:pr + 64,
                                               fic_h * 128:fic_h * 128 + 128],
                                    sqT_b[tb][pr:pr + 64,
                                              fic_h * TILE + ci * 128:
                                              fic_h * TILE + ci * 128 + 128],
                                    start=True, stop=True)
                            k.op("pe", fn_at, reads=[r_skT[cbp], r_sqT[tb]],
                                 writes=[r_PA])
                        if "hatm" in MP:
                            def fn_mask(v, hg=hg, gl=gl):
                                return v.tensor_tensor(am_b[hg % 2][:, 0:gl * 128],
                                                       PA[:, 0:gl * 128],
                                                       trib[:, 0:gl * 128], ALU.mult)
                            k.op("dve", fn_mask, reads=[r_PA, r_consts],
                                 writes=[r_am[hg % 2]])

                        for j, h in enumerate(heads):
                            hb = j % 2
                            fic_h = h // 2
                            if "hcm" in MP:
                                def fn_cm(t_, h=h, hb=hb, fic_h=fic_h, pr=pr,
                                          tb=tb, ci=ci):
                                    return t_.matmul(
                                        PN[:, hb * 512:hb * 512 + 325],
                                        sqT_b[tb][pr:pr + 64,
                                                  fic_h * TILE + ci * 128:
                                                  fic_h * TILE + ci * 128 + 128],
                                        rhscat[pr:pr + 64,
                                               (h // 2) * 325:(h // 2) * 325 + 325],
                                        start=True, stop=("hni" not in MP))
                                k.op("pe", fn_cm, reads=[r_sqT[tb], r_rhs_S, r_rhs_mem],
                                     writes=[r_PN[hb]])
                            if "hni" in MP:
                                def fn_ni(t_, h=h, j=j, hb=hb, hg=hg, cbp=cbp):
                                    return t_.matmul(
                                        PN[:, hb * 512:hb * 512 + 65],
                                        am_b[hg % 2][:, j * 128:j * 128 + 128],
                                        vaug_b[cbp][:, h * 65:h * 65 + 65],
                                        start=("hcm" not in MP), stop=True)
                                k.op("pe", fn_ni, reads=[r_am[hg % 2], r_vaug[cbp]],
                                     writes=[r_PN[hb]])
                            if "hcm" in MP or "hni" in MP:
                                def fn_ne(s_, h=h, hb=hb, cbp=cbp):
                                    return s_.copy(
                                        cb_b[cbp][:, h * 325:h * 325 + 325],
                                        PN[:, hb * 512:hb * 512 + 325])
                                k.op("act", fn_ne, reads=[r_PN[hb]],
                                     writes=[r_cb[cbp]])
                            if "hst" not in MP:
                                continue
                            def fn_st(t_, h=h, j=j, p2=p2, cbp=cbp, gl=gl):
                                kw = {}
                                bank = PS if p2 == 0 else PS2
                                if p2 == 1:
                                    kw["tile_position"] = (0, 64)
                                return t_.matmul(
                                    bank[p2 * 64:p2 * 64 + 64,
                                         j * 65:j * 65 + 65],
                                    skt_b[cbp][:, h * 64:h * 64 + 64],
                                    vaug_b[cbp][:, h * 65:h * 65 + 65],
                                    start=(j == 0), stop=(j == gl - 1), **kw)
                            k.op("pe", fn_st, reads=[r_skt[cbp], r_vaug[cbp]],
                                 writes=[r_PS if p2 == 0 else r_PS2])
                        if "hst" not in MP:
                            continue
                        def fn_sa(v, p2=p2, sup=sup, gl=gl):
                            bank = PS if p2 == 0 else PS2
                            return v.tensor_add(
                                ssb[p2 * 64:p2 * 64 + 64,
                                    sup * 260:sup * 260 + gl * 65],
                                ssb[p2 * 64:p2 * 64 + 64,
                                    sup * 260:sup * 260 + gl * 65],
                                bank[p2 * 64:p2 * 64 + 64, 0:gl * 65])
                        k.op("dve", fn_sa, reads=[r_PS if p2 == 0 else r_PS2],
                             writes=[r_ssb])
                    if "hst" not in MP:
                        continue
                    def fn_sc(s_):
                        return s_.copy(
                            rhscat.rearrange("p (a b) -> p a b", b=325)[:, :, 0:65],
                            ssb.rearrange("p (a b) -> p a b", b=65))
                    k.op("act", fn_sc, reads=[r_ssb], writes=[r_rhs_S])

                    if "epi" not in MP:
                        continue
                    if rel_tail[0] is not None:
                        rel_tail[0]()
                        rel_tail[0] = None
                    def fn_g1(v, cbp=cbp):
                        return v.tensor_copy(
                            sc_raw[:],
                            cb_b[cbp].rearrange("p (h j c) -> p h j c",
                                                j=5, c=65)[:, :, :, 64])
                    k.op("dve", fn_g1, reads=[r_cb[cbp]], writes=[r_scraw])
                    k.op("dve", lambda v: v.tensor_scalar_max(sc_clip[:], sc_raw[:],
                                                              EPS_ATT),
                         reads=[r_scraw], writes=[r_scclip])
                    k.op("dve", lambda v: v.reciprocal(sc_raw[:], sc_clip[:]),
                         reads=[r_scclip], writes=[r_scraw])
                    k.op("dve", lambda v: v.tensor_tensor(sc_scale[:], sc_raw[:],
                                                          coefb[:], ALU.mult),
                         reads=[r_scraw, r_coefb], writes=[r_scscale])
                    engs = ["dve", "dve", "pool", "pool", "dve"]
                    for j in range(5):
                        def fn_mj(x, j=j, cbp=cbp):
                            dst = ep_t[j][:] if j < 4 else outpre_b[:]
                            return x.tensor_tensor(
                                dst,
                                cb_b[cbp].rearrange("p (h j c) -> p h j c",
                                                    j=5, c=65)[:, :, j, 0:64],
                                bass.AP(sc_scale, j,
                                        [[NH * 5, 128], [5, NH], [0, 64]]),
                                ALU.mult)
                        tgt = r_ept[j] if j < 4 else r_outpre
                        k.op(engs[j], fn_mj, reads=[r_cb[cbp], r_scscale], writes=[tgt])
                    k.op("dve", lambda x: x.tensor_add(ep_t[0][:], ep_t[0][:],
                                                       ep_t[1][:]),
                         reads=[r_ept[1]], writes=[r_ept[0]])
                    k.op("pool", lambda x: x.tensor_add(ep_t[2][:], ep_t[2][:],
                                                        ep_t[3][:]),
                         reads=[r_ept[3]], writes=[r_ept[2]])
                    k.op("pool", lambda x: x.tensor_add(outpre_b[:], outpre_b[:],
                                                        ep_t[0][:]),
                         reads=[r_ept[0]], writes=[r_outpre])
                    k.op("pool", lambda x: x.tensor_add(outpre_b[:], outpre_b[:],
                                                        ep_t[2][:]),
                         reads=[r_ept[2]], writes=[r_outpre])

                    def tail1(bi=bi, t=t, tb=tb, ci=ci, cbp=cbp, ch0=ch0, t0=t0):
                        if "wo" not in MP:
                            return
                        for fic in range(FC):
                            def fn_tp(t_, fic=fic):
                                return t_.transpose(
                                    PT[:, fic * 128:fic * 128 + 128],
                                    outpre_b[:, fic * 128:fic * 128 + 128],
                                    identb[:])
                            k.op("pe", fn_tp, reads=[r_outpre, r_consts],
                                 writes=[r_PT])
                        k.op("dve", lambda v: v.tensor_copy(opT_b[:, 0:H],
                                                            PT[:, 0:H]),
                             reads=[r_PT], writes=[r_opT])
                        for half in range(NFO):
                            pwh = half % 2
                            for fic in range(FC):
                                def fn_wo(t_, half=half, fic=fic, pwh=pwh):
                                    return t_.matmul(
                                        PW[:, pwh * 512:pwh * 512 + FOW],
                                        opT_b[:, fic * 128:fic * 128 + 128],
                                        W_AR[:, 3 * FC * H + fic * H + half * FOW:
                                             3 * FC * H + fic * H + half * FOW + FOW],
                                        start=(fic == 0), stop=(fic == FC - 1))
                                k.op("pe", fn_wo, reads=[r_opT, r_war],
                                     writes=[r_PW[pwh]])
                        if "ln" not in MP:
                            return
                        for half in range(NFO):
                            pwh = half % 2
                            def fn_res(v, half=half, cbp=cbp, pwh=pwh):
                                return v.scalar_tensor_tensor(
                                    x1r_b[:, half * FOW:half * FOW + FOW],
                                    PW[:, pwh * 512:pwh * 512 + FOW], 1.0,
                                    hs_b[cbp][:, half * FOW:half * FOW + FOW],
                                    ALU.mult, ALU.add,
                                    accum_out=lncol[:, half:half + 1])
                            k.op("dve", fn_res, reads=[r_PW[pwh], r_hs[cbp]],
                                 writes=[r_x1r, r_lncol])
                        if NFO == 2:
                            k.op("dve", lambda v: v.tensor_add(lncol[:, 2:3],
                                                               lncol[:, 0:1],
                                                               lncol[:, 1:2]),
                                 reads=[r_lncol], writes=[r_lncol])
                        else:
                            k.op("dve", lambda v: v.tensor_copy(lncol[:, 2:3],
                                                                lncol[:, 0:1]),
                                 reads=[r_lncol], writes=[r_lncol])
                        def fn_xsq(v):
                            return v.scalar_tensor_tensor(xsq_scr[:], x1r_b[:], 1.0,
                                                          x1r_b[:], ALU.mult,
                                                          ALU.mult,
                                                          accum_out=lncol[:, 3:4])
                        k.op("dve", fn_xsq, reads=[r_x1r], writes=[r_xsq, r_lncol])
                        k.op("dve", lambda v: v.tensor_scalar(lncol[:, 4:5],
                                                              lncol[:, 2:3],
                                                              -1.0 / H, None,
                                                              ALU.mult),
                             reads=[r_lncol], writes=[r_lncol])
                        k.op("dve", lambda v: v.tensor_scalar(lncol[:, 5:6],
                                                              lncol[:, 3:4],
                                                              1.0 / H, None,
                                                              ALU.mult),
                             reads=[r_lncol], writes=[r_lncol])
                        k.op("dve", lambda v: v.tensor_scalar(lncol[:, 6:7],
                                                              lncol[:, 4:5],
                                                              lncol[:, 4:5], None,
                                                              ALU.mult),
                             reads=[r_lncol], writes=[r_lncol])
                        k.op("dve", lambda v: v.tensor_sub(lncol[:, 7:8],
                                                           lncol[:, 5:6],
                                                           lncol[:, 6:7]),
                             reads=[r_lncol], writes=[r_lncol])
                        k.op("act", lambda s_: s_.activation(lncol[:, 6:7],
                                                             lncol[:, 7:8],
                                                             AF.Ln,
                                                             bias=epsb[:, 0:1]),
                             reads=[r_lncol], writes=[r_lncol])
                        k.op("act", lambda s_: s_.activation(lncol[:, 7:8],
                                                             lncol[:, 6:7],
                                                             AF.Exp, scale=-0.5),
                             reads=[r_lncol], writes=[r_lncol])
                        def fn_x1(v):
                            return v.tensor_scalar(x1c_b[:], x1r_b[:], lncol[:, 4:5],
                                                   lncol[:, 7:8], ALU.add, ALU.mult)
                        k.op("dve", fn_x1, reads=[r_x1r, r_lncol], writes=[r_x1c])
                        if dbg:
                            def fn_dx1f(v):
                                return v.tensor_scalar(dbg_xout[:], x1r_b[:],
                                                       lncol[:, 4:5],
                                                       lncol[:, 7:8], ALU.add,
                                                       ALU.mult)
                            k.op("dve", fn_dx1f, reads=[r_x1r, r_lncol],
                                 writes=[r_xout])
                            def fn_dx1(g, bi=bi, ch0=ch0):
                                return g.dma_start(
                                    out=bass.AP(dbg_outs["dbg_x1"].tensor,
                                                (bi * S + ch0) * H,
                                                [[H, 128], [1, H]]),
                                    in_=dbg_xout[:])
                            k.op("sp", fn_dx1, reads=[r_xout], inc=16)

                    def tail2(bi=bi, t=t, tb=tb, ci=ci, cbp=cbp, ch0=ch0, t0=t0):
                        if "wo" not in MP or "ln" not in MP or "x1t" not in MP:
                            return
                        for fic in range(FC):
                            def fn_tp(t_, fic=fic):
                                return t_.transpose(
                                    PT[:, fic * 128:fic * 128 + 128],
                                    x1c_b[:, fic * 128:fic * 128 + 128],
                                    identb[:])
                            k.op("pe", fn_tp, reads=[r_x1c, r_consts],
                                 writes=[r_PT])
                        def fn_x1te(s_, tb=tb, ci=ci):
                            return s_.copy(
                                x1t_tile[tb].rearrange("p (f t) -> p f t", t=TILE)[
                                    :, :, ci * 128:ci * 128 + 128],
                                PT[:, 0:FC * 128].rearrange("p (f t) -> p f t",
                                                            t=128))
                        k.op("act", fn_x1te, reads=[r_PT], writes=[r_x1tt[tb]])

                        if "rout" in MP:
                            def fn_lgb(t_):
                                return t_.matmul(PW[:, 0:E], ones_row[0:1, :],
                                                 lbb_bf[:],
                                                 start=True, stop=False)
                            k.op("pe", fn_lgb, reads=[r_consts, r_small],
                                 writes=[r_PW[0]])
                            for fic in range(FC):
                                def fn_lg(t_, fic=fic, tb=tb, ci=ci):
                                    return t_.matmul(
                                        PW[:, 0:E],
                                        x1t_tile[tb][:, fic * TILE + ci * 128:
                                                     fic * TILE + ci * 128 + 128],
                                        gwt_sb[:, fic * E:fic * E + E],
                                        start=False, stop=(fic == FC - 1))
                                k.op("pe", fn_lg, reads=[r_x1tt[tb], r_consts],
                                     writes=[r_PW[0]])
                            k.op("act", lambda s_: s_.copy(sco_p[:], PW[:, 0:E]),
                                 reads=[r_PW[0]], writes=[r_scop])
                            k.op("act", lambda s_: s_.activation(sco_e[:], sco_p[:],
                                                                 AF.Exp,
                                                                 scale=-1.0),
                                 reads=[r_scop], writes=[r_scoe])
                            k.op("dve", lambda v: v.tensor_scalar_add(sco_e[:],
                                                                      sco_e[:], 1.0),
                                 reads=[r_scoe], writes=[r_scoe])
                            k.op("dve", lambda v: v.reciprocal(scores[:], sco_e[:]),
                                 reads=[r_scoe], writes=[r_scores])
                            k.op("dve", lambda v: v.reduce_max(
                                rmx[:, 0:1], scores[:], axis=mybir.AxisListType.X),
                                 reads=[r_scores], writes=[r_rmx])
                            k.op("dve", lambda v: v.tensor_scalar(
                                ismax[:], scores[:], rmx[:, 0:1], None, ALU.is_ge),
                                 reads=[r_scores, r_rmx], writes=[r_ismax])
                            def fn_smk(v):
                                return v.scalar_tensor_tensor(smask[:], ismax[:],
                                                              -1e30, scores[:],
                                                              ALU.mult, ALU.add)
                            k.op("dve", fn_smk, reads=[r_ismax, r_scores],
                                 writes=[r_smask])
                            k.op("dve", lambda v: v.reduce_max(
                                rmx[:, 1:2], smask[:], axis=mybir.AxisListType.X),
                                 reads=[r_smask], writes=[r_rmx])
                            k.op("dve", lambda v: v.tensor_scalar(
                                self32[:], scores[:], rmx[:, 1:2], None, ALU.is_ge),
                                 reads=[r_scores, r_rmx], writes=[r_self32])
                            k.op("dve", lambda v: v.tensor_add(rmx[:, 2:3],
                                                               rmx[:, 0:1],
                                                               rmx[:, 1:2]),
                                 reads=[r_rmx], writes=[r_rmx])
                            k.op("dve", lambda v: v.tensor_scalar_add(rmx[:, 2:3],
                                                                      rmx[:, 2:3],
                                                                      EPS_TOP),
                                 reads=[r_rmx], writes=[r_rmx])
                            k.op("dve", lambda v: v.reciprocal(rmx[:, 3:4],
                                                               rmx[:, 2:3]),
                                 reads=[r_rmx], writes=[r_rmx])
                            k.op("dve", lambda v: v.tensor_tensor(smask[:],
                                                                  scores[:],
                                                                  self32[:],
                                                                  ALU.mult),
                                 reads=[r_scores, r_self32], writes=[r_smask])
                            k.op("dve", lambda v: v.tensor_scalar(
                                wsel[:, 0:E], smask[:], rmx[:, 3:4], None,
                                ALU.mult),
                                 reads=[r_smask, r_rmx], writes=[r_wsel])
                            k.op("dve", lambda v: v.tensor_copy(wsel[:, E:2 * E],
                                                                self32[:]),
                                 reads=[r_self32], writes=[r_wsel])
                            def fn_selc(v, tb=tb, ci=ci):
                                return v.tensor_copy(
                                    sel_b[tb][:, ci * E:ci * E + E], self32[:])
                            k.op("dve", fn_selc, reads=[r_self32],
                                 writes=[r_sel_b[tb]])
                            k.op("dve", lambda v: v.tensor_add(wselacc[:],
                                                               wselacc[:],
                                                               wsel[:]),
                                 reads=[r_wsel], writes=[r_wselacc])

                        if ci == CPT - 1:
                            def fn_x1s(g, bi=bi, t0=t0, tb=tb):
                                return g.dma_start(out=x1_scr_ap(bi, t0),
                                                   in_=x1t_tile[tb][:])
                            k.op("sp", fn_x1s, reads=[r_x1tt[tb]],
                                 writes=[r_x1scr_h[1 if t0 >= OH else 0]],
                                 inc=16)
                            def fn_sels(g, bi=bi, t0=t0, tb=tb):
                                return g.dma_start(
                                    out=bass.AP(sel_scr_t, (bi * S + t0) * E,
                                                [[E, 128], [128 * E, CPT], [1, E]]),
                                    in_=bass.AP(sel_b[tb], 0,
                                                [[CPT * E, 128], [E, CPT], [1, E]]))
                            k.op("sp", fn_sels, reads=[r_sel_b[tb]],
                                 writes=[r_selscr], inc=16)
                    emit_tail(1)
                    pending_tail[0] = tail1
                    pending_tail[1] = tail2
            emit_tail(0)
            emit_tail(1)
            if "rout" in MP:
                def fn_stm(t_):
                    return t_.matmul(PW[0:16, 1023:1024], wselacc[:],
                                     ones_col_f[:], start=True, stop=True)
                k.op("pe", fn_stm, reads=[r_wselacc, r_consts],
                     writes=[r_PW[1]])
                k.op("act", lambda s_: s_.copy(statacc[:],
                                               PW[0:16, 1023:1024]),
                     reads=[r_PW[1]], writes=[r_statacc])

        if rel_tail[0] is not None:
            rel_tail[0]()
            rel_tail[0] = None

        def mb_op(*a, **kw):
            if "mbox" in PH:
                return k.op(*a, **kw)
            return None
        def fn_statcc_a(g, sem, psem):
            g.dma_start(bass.AP(cstat_in_t, 0, [[16, 1], [1, 1], [1, 16]]),
                        bass.AP(statacc, 0, [[1, 16], [1, 1]])).then_inc(sem, 16)
            g.wait_ge(sem, 16)
            g.collective_compute(
                "AllReduce", ALU.add,
                replica_groups=[list(range(8))],
                ins=[cstat_in_t.ap().opt()], outs=[cstat_out_t.ap().opt()],
            ).then_inc(sem)
        def fn_statcc_b(g, sem, psem):
            g.dma_start(out=mrow[:, 0:16], in_=cstat_out_t[:, :]).then_inc(sem, 16)
        def emit_stats_cc():
            if "mbox" in PH:
                k.raw("pool", fn_statcc_a, reads=[r_statacc, r_cstat],
                      writes=[r_cstat], incs=17)
        def emit_stats_tail():
            if "mbox" not in PH:
                k.op("pool", lambda g: g.memset(wmeanb[:], 0.5),
                     writes=[r_wmeanb])
                return
            k.raw("pool", fn_statcc_b, reads=[r_cstat],
                  writes=[r_mrow], incs=16)
            mb_op("dve", lambda v: v.tensor_scalar_max(mrow[:, 24:24 + E],
                                                      mrow[:, E:2 * E], 1.0),
                 reads=[r_mrow], writes=[r_mrow])
            mb_op("dve", lambda v: v.reciprocal(mrow[:, 32:32 + E],
                                               mrow[:, 24:24 + E]),
                 reads=[r_mrow], writes=[r_mrow])
            mb_op("dve", lambda v: v.tensor_tensor(wm_row[:], mrow[:, 0:E],
                                                  mrow[:, 32:32 + E], ALU.mult),
                 reads=[r_mrow], writes=[r_wmrow])
            mb_op("dve", lambda v: v.tensor_copy(wm_bf[:], wm_row[:]),
                 reads=[r_wmrow], writes=[r_wmrow])
            mb_op("pe", lambda t_: t_.matmul(PA[:, 0:E], ones_row[:], wm_bf[:],
                                            start=True, stop=True),
                 reads=[r_consts, r_wmrow], writes=[r_PA])
            mb_op("act", lambda s_: s_.copy(wmeanb[:], PA[:, 0:E]),
                 reads=[r_PA], writes=[r_wmeanb])
        if "c" not in PH:
            emit_stats_cc()
            emit_stats_tail()

        att_alias = ([r_hsT[0], r_hsT[1], r_sqT[0], r_sqT[1], r_x1tt[0],
                      r_x1tt[1], r_cb[0], r_cb[1], r_rhs_S, r_rhs_mem,
                      r_skt[0], r_skt[1], r_skT[0], r_skT[1], r_vaug[0],
                      r_vaug[1], r_am[0], r_am[1], r_outpre, r_opT, r_x1c,
                      r_xsq, r_x1r, r_hs[0], r_hs[1], r_elu1, r_elu2, r_ssb]
                     + r_ept)
        if "c" in PH:
            k.op("pool", lambda g: g.engine_nop(),
                 reads=att_alias,
                 writes=[r_exp[0], r_exp[1], r_acc, r_hb[0], r_silu,
                         r_selown, r_xout, r_x2r, r_x1tok, r_war])
            emit_stats_cc()
            def fn_clh(h):
                def fn(g, sem, psem, h=h):
                    for fic in range(FC):
                        g.dma_start(
                            bass.AP(W_AR, fic * OWN + h * OH,
                                    [[4 * FC * H, 128], [1, OH]]),
                            bass.AP(x1t_scr_t, fic * 128 * NBS + h * OH,
                                    [[NBS, 128], [1, OH]])).then_inc(sem, 16)
                return fn
            k.raw("sp", fn_clh(0), reads=[r_x1scr_h[0], r_war],
                  writes=[r_war_h[0]], incs=FC * 16)
            k.raw("sp", fn_clh(1), reads=[r_x1scr_h[1], r_war],
                  writes=[r_war_h[1]], incs=FC * 16)
            def fn_cls(g):
                return g.dma_start(
                    sel_own.rearrange("p (a b) -> p a b", b=E),
                    bass.AP(sel_scr_t, 0, [[E, 128], [128 * E, NOC], [1, E]]))
            k.op("pool", fn_cls, reads=[r_selscr], writes=[r_selown], inc=16)

        accv = accv_full
        acc_cols = (OH // 128) * H

        def emit_ln2(half, tc):
            oc_own = half * (OH // 128) + tc
            col0 = tc * H
            for fic in range(FC):
                def fn_tp(t_, fic=fic, oc_own=oc_own):
                    return t_.transpose(
                        PT[:, fic * 128:fic * 128 + 128],
                        W_AR[:, fic * OWN + oc_own * 128:
                             fic * OWN + oc_own * 128 + 128],
                        identb[:])
                k.op("pe", fn_tp, reads=[r_war_h[half], r_consts],
                     writes=[r_PT])
            k.op("act", lambda s_: s_.copy(x1tok_b[:, 0:H], PT[:, 0:H]),
                 reads=[r_PT], writes=[r_x1tok])
            for hh in range(NFO):
                def fn_r(v, hh=hh, col0=col0):
                    return v.scalar_tensor_tensor(
                        x2r_b[:, hh * FOW:hh * FOW + FOW],
                        x1tok_b[:, hh * FOW:hh * FOW + FOW], 1.0,
                        accv[:, col0 + hh * FOW:col0 + hh * FOW + FOW],
                        ALU.mult, ALU.add,
                        accum_out=lncol[:, hh:hh + 1])
                k.op("dve", fn_r, reads=[r_x1tok, r_acc],
                     writes=[r_x2r, r_lncol])
            if NFO == 2:
                k.op("dve", lambda v: v.tensor_add(lncol[:, 2:3], lncol[:, 0:1],
                                                   lncol[:, 1:2]),
                     reads=[r_lncol], writes=[r_lncol])
            else:
                k.op("dve", lambda v: v.tensor_copy(lncol[:, 2:3], lncol[:, 0:1]),
                     reads=[r_lncol], writes=[r_lncol])
            def fn_xsq(v, col0=col0):
                return v.scalar_tensor_tensor(accv[:, col0:col0 + H],
                                              x2r_b[:], 1.0,
                                              x2r_b[:], ALU.mult, ALU.mult,
                                              accum_out=lncol[:, 3:4])
            k.op("dve", fn_xsq, reads=[r_x2r], writes=[r_acc, r_lncol])
            k.op("dve", lambda v: v.tensor_scalar(lncol[:, 4:5], lncol[:, 2:3],
                                                  -1.0 / H, None, ALU.mult),
                 reads=[r_lncol], writes=[r_lncol])
            k.op("dve", lambda v: v.tensor_scalar(lncol[:, 5:6], lncol[:, 3:4],
                                                  1.0 / H, None, ALU.mult),
                 reads=[r_lncol], writes=[r_lncol])
            k.op("dve", lambda v: v.tensor_scalar(lncol[:, 6:7], lncol[:, 4:5],
                                                  lncol[:, 4:5], None, ALU.mult),
                 reads=[r_lncol], writes=[r_lncol])
            k.op("dve", lambda v: v.tensor_sub(lncol[:, 7:8], lncol[:, 5:6],
                                               lncol[:, 6:7]),
                 reads=[r_lncol], writes=[r_lncol])
            k.op("act", lambda s_: s_.activation(lncol[:, 6:7], lncol[:, 7:8],
                                                 AF.Ln, bias=epsb[:, 0:1]),
                 reads=[r_lncol], writes=[r_lncol])
            k.op("act", lambda s_: s_.activation(lncol[:, 7:8], lncol[:, 6:7],
                                                 AF.Exp, scale=-0.5),
                 reads=[r_lncol], writes=[r_lncol])
            def fn_outop(v):
                return v.tensor_scalar(xout[:], x2r_b[:], lncol[:, 4:5],
                                       lncol[:, 7:8], ALU.add, ALU.mult)
            k.op("dve", fn_outop, reads=[r_x2r, r_lncol], writes=[r_xout])
            def fn_os(g, half=half, tc=tc):
                row0 = half * OH + tc * 128
                return g.dma_start(
                    out=bass.AP(out.tensor, row0 * H, [[H, 128], [1, H]]),
                    in_=xout[:])
            k.op("sp", fn_os, reads=[r_xout], inc=16)
        if "c" not in PH:
            for tc in range(OWN // 128):
                def fn_os0(g, tc=tc):
                    return g.dma_start(
                        out=bass.AP(out.tensor, tc * 128 * H, [[H, 128], [1, H]]),
                        in_=xout[:])
                k.op("sp", fn_os0, reads=[r_xout], inc=16)
        for half in range(2):
            if "c" not in PH:
                break
            k.op("pool", lambda g: g.memset(accv[:, 0:acc_cols], 0.0),
                 writes=[r_acc])

            experts = (list(range(E)) + ["sh"]) if half == 0 \
                else (["sh"] + list(range(E)))
            for eidx, e in enumerate(experts):
                if e == "sh":
                    n_ic = SIC
                    II = SI
                    sofs = 0
                    u_ofs, d_ofs = FC * SI, 2 * FC * SI
                    wrs = [r_exp[0], r_exp[1]]
                    wrs_g = [r_exp[0]]
                    wrs_d = [r_exp[1]]
                    def fn_sh_g(g):
                        return g.dma_start(
                            out=EXP_AR[:, 0:FC * SI].rearrange(
                                "p (f i) -> p f i", i=SI),
                            in_=bass.AP(sgT.tensor, 0,
                                        [[SI, 128], [128 * SI, FC], [1, SI]]))
                    def fn_sh_u1(g):
                        return g.dma_start(
                            out=EXP_AR[:, FC * SI:FC * SI + 4 * SI].rearrange(
                                "p (f i) -> p f i", i=SI),
                            in_=bass.AP(suT.tensor, 0,
                                        [[SI, 128], [128 * SI, 4], [1, SI]]))
                    def fn_sh_u2(g):
                        return g.dma_start(
                            out=EXP_AR[:, FC * SI + 4 * SI:2 * FC * SI].rearrange(
                                "p (f i) -> p f i", i=SI),
                            in_=bass.AP(suT.tensor, 4 * 128 * SI,
                                        [[SI, 128], [128 * SI, 4], [1, SI]]))
                    def fn_sh_d(g):
                        return g.dma_start(
                            out=EXP_AR[:, 2 * FC * SI:2 * FC * SI + SIC * H].rearrange(
                                "p (f i) -> p f i", i=H),
                            in_=bass.AP(sdT.tensor, 0,
                                        [[H, 128], [128 * H, SIC], [1, H]]))
                else:
                    n_ic = EIC
                    II = EI
                    sofs = (eidx % 2) * EXPCOLS
                    u_ofs, d_ofs = FC * EI, 2 * FC * EI
                    wrs = [r_exp[eidx % 2]]
                    wrs_g = wrs
                    wrs_d = wrs
                    def fn_wl1(g, e=e, sofs=sofs):
                        return g.dma_start(
                            out=EXP_AR[:, sofs:sofs + FC * EI].rearrange(
                                "p (f i) -> p f i", i=EI),
                            in_=bass.AP(egT.tensor, e * H * EI,
                                        [[EI, 128], [128 * EI, FC], [1, EI]]))
                    def fn_wl2(g, e=e, sofs=sofs):
                        return g.dma_start(
                            out=EXP_AR[:, sofs + FC * EI:sofs + 2 * FC * EI].rearrange(
                                "p (f i) -> p f i", i=EI),
                            in_=bass.AP(euT.tensor, e * H * EI,
                                        [[EI, 128], [128 * EI, FC], [1, EI]]))
                    def fn_wl3(g, e=e, sofs=sofs):
                        return g.dma_start(
                            out=EXP_AR[:, sofs + 2 * FC * EI:
                                       sofs + 2 * FC * EI + EIC * H].rearrange(
                                "p (f i) -> p f i", i=H),
                            in_=bass.AP(edT.tensor, e * EI * H,
                                        [[H, 128], [128 * H, EIC], [1, H]]))
                if e == "sh":
                    if half == 0:
                        def fn_sh_a(g, sem, psem, f1=fn_sh_g, f2=fn_sh_u1):
                            f1(g).then_inc(sem, 16)
                            f2(g).then_inc(sem, 16)
                        def fn_sh_b(g, sem, psem, f1=fn_sh_u2, f2=fn_sh_d):
                            f1(g).then_inc(sem, 16)
                            f2(g).then_inc(sem, 16)
                        k.raw("pool", fn_sh_a, writes=[r_exp[0]], incs=32)
                        k.raw("pool", fn_sh_b, writes=[r_exp[1]], incs=32)
                else:
                    def fn_wl123(g, sem, psem, f1=fn_wl1, f2=fn_wl2, f3=fn_wl3):
                        f1(g).then_inc(sem, 16)
                        f2(g).then_inc(sem, 16)
                        f3(g).then_inc(sem, 16)
                    k.raw("pool", fn_wl123, writes=wrs, incs=48)

                for t in range(TPH):
                    th0 = half * OH + t * CT
                    hbp = t % 2
                    for ioc in range(n_ic):
                        pn = ioc % 2
                        for fic in range(FC):
                            def fn_gm(t_, fic=fic, ioc=ioc, pn=pn, sofs=sofs,
                                      th0=th0, II=II):
                                return t_.matmul(
                                    PN[:, pn * 512:pn * 512 + CT],
                                    EXP_AR[:, sofs + fic * II + ioc * 128:
                                           sofs + fic * II + ioc * 128 + 128],
                                    W_AR[:, fic * OWN + th0:fic * OWN + th0 + CT],
                                    start=(fic == 0), stop=(fic == FC - 1))
                            k.op("pe", fn_gm, reads=[r_war_h[half]] + wrs_g,
                                 writes=[r_PN[pn]])
                        def fn_silu(s_, pn=pn):
                            return s_.activation(silu_b[:, 0:CT],
                                                 PN[:, pn * 512:pn * 512 + CT],
                                                 AF.Silu)
                        k.op("act", fn_silu, reads=[r_PN[pn]], writes=[r_silu])
                        for fic in range(FC):
                            def fn_um(t_, fic=fic, ioc=ioc, pn=pn, sofs=sofs,
                                      th0=th0, II=II, u_ofs=u_ofs):
                                return t_.matmul(
                                    PW[:, pn * 512:pn * 512 + CT],
                                    EXP_AR[:, sofs + u_ofs + fic * II + ioc * 128:
                                           sofs + u_ofs + fic * II + ioc * 128 + 128],
                                    W_AR[:, fic * OWN + th0:fic * OWN + th0 + CT],
                                    start=(fic == 0), stop=(fic == FC - 1))
                            wu = (wrs if e != "sh"
                                  else ([r_exp[0]] if fic < 4 else [r_exp[1]]))
                            k.op("pe", fn_um, reads=[r_war_h[half]] + wu,
                                 writes=[r_PW[pn]])
                        def fn_h(v, pn=pn, ioc=ioc, hbp=hbp):
                            return v.tensor_tensor(
                                h_b[hbp][:, ioc * CT:ioc * CT + CT],
                                silu_b[:, 0:CT], PW[:, pn * 512:pn * 512 + CT],
                                ALU.mult)
                        k.op("dve", fn_h, reads=[r_silu, r_PW[pn]], writes=[r_hb[hbp]])
                    if half == 0 and eidx == 0 and t == 0:
                        emit_stats_tail()
                    for tc in range(CTC):
                        oc = (half * OH + t * CT + tc * 128) // 128
                        if e != "sh":
                            def fn_wt(v, oc=oc, e=e):
                                return v.tensor_tensor(
                                    wtil[:],
                                    sel_own[:, oc * E + e:oc * E + e + 1],
                                    wmeanb[:, e:e + 1], ALU.mult)
                            k.op("dve", fn_wt, reads=[r_selown, r_wmeanb],
                                 writes=[r_wtil])
                        for foh in range(NFO):
                            pw = foh % 2
                            for ioc in range(n_ic):
                                def fn_dm(t_, ioc=ioc, tc=tc, foh=foh, pw=pw,
                                          sofs=sofs, d_ofs=d_ofs, hbp=hbp, n_ic=n_ic):
                                    return t_.matmul(
                                        PW[:, pw * 512:pw * 512 + FOW],
                                        h_b[hbp][:, ioc * CT + tc * 128:
                                                 ioc * CT + tc * 128 + 128],
                                        EXP_AR[:, sofs + d_ofs + ioc * H + foh * FOW:
                                               sofs + d_ofs + ioc * H + foh * FOW + FOW],
                                        start=(ioc == 0), stop=(ioc == n_ic - 1))
                                k.op("pe", fn_dm, reads=[r_hb[hbp]] + wrs_d,
                                     writes=[r_PW[pw]])
                            col0 = ((t * CT + tc * 128) // 128) * H + foh * FOW
                            if e == "sh":
                                def fn_acc(v, pw=pw, col0=col0):
                                    return v.tensor_add(
                                        accv[:, col0:col0 + FOW],
                                        accv[:, col0:col0 + FOW],
                                        PW[:, pw * 512:pw * 512 + FOW])
                                k.op("dve", fn_acc, reads=[r_PW[pw]], writes=[r_acc])
                            else:
                                def fn_acc(v, pw=pw, col0=col0):
                                    return v.scalar_tensor_tensor(
                                        accv[:, col0:col0 + FOW],
                                        PW[:, pw * 512:pw * 512 + FOW], wtil[:],
                                        accv[:, col0:col0 + FOW],
                                        ALU.mult, ALU.add)
                                k.op("dve", fn_acc, reads=[r_PW[pw], r_wtil],
                                     writes=[r_acc])
                        if eidx == len(experts) - 1:
                            emit_ln2(half, t * CTC + tc)

        emit_all(nc, plan, es)
    return nc, c, dbg_outs



def prep_inputs(cin, hs_full, w_q, w_k, w_v, w_o, gate, memories, memory_norms,
                gate_w, lb_bias, egw, euw, edw, sgw, suw, sdw):
    c = derive(cin)
    S, NH, M, E, H = c["S"], c["NH"], c["M"], c["E"], c["H"]

    wqT = np.ascontiguousarray(w_q.T)
    wkT = np.ascontiguousarray(w_k.T)
    wvT = np.ascontiguousarray(w_v.T)
    woT = np.ascontiguousarray(w_o.T)
    memcat = np.zeros((NH, 64, M * 65), np.float32)
    for m in range(M):
        memcat[:, :, 65 * m:65 * m + 64] = memories[m]
        memcat[:, :, 65 * m + 64] = memory_norms[m]
    wmn = np.zeros((H, NH * M), np.float32)
    for h in range(NH):
        for m in range(M):
            wmn[64 * h:64 * h + 64, h * M + m] = memory_norms[m, h]
    gate_in = gate.reshape(NH, 1).astype(np.float32)
    gwT = np.ascontiguousarray(gate_w.T)
    lbbv = lb_bias.reshape(1, E).astype(np.float32)
    egTv = np.ascontiguousarray(egw.transpose(0, 2, 1))
    euTv = np.ascontiguousarray(euw.transpose(0, 2, 1))
    edTv = np.ascontiguousarray(edw.transpose(0, 2, 1))
    sgTv = np.ascontiguousarray(sgw.T)
    suTv = np.ascontiguousarray(suw.T)
    sdTv = np.ascontiguousarray(sdw.T)

    SO = c["SO"]
    n_batch = hs_full.shape[0]
    hsT_first = []
    hsT_second = []
    for b in range(n_batch):
        hsTb = hs_full[b].T
        hsT_first.append(np.ascontiguousarray(hsTb[:, 0:SO]))
        hsT_second.append(np.ascontiguousarray(hsTb[:, SO:2 * SO]))

    in_maps = []
    placements = []
    for cid in range(8):
        b = cid // 2
        half = cid % 2
        m = {
            "hsT_own": hsT_second[b] if half else hsT_first[b],
            "hsT_state": hsT_first[b],
            "hs_own": np.ascontiguousarray(hs_full[b, half * SO:(half + 1) * SO]),
            "wqT": wqT, "wkT": wkT, "wvT": wvT, "woT": woT,
            "memcat": memcat, "wmn": wmn, "gate_in": gate_in,
            "gwT": gwT, "lbb": lbbv,
            "egT": egTv, "euT": euTv, "edT": edTv,
            "sgT": sgTv, "suT": suTv, "sdT": sdTv,
            "cfgv": np.array([[half, 0, 0, 0, 0, 0, 0, 0]], np.int32),
        }
        in_maps.append(m)
        placements.append((b, half * SO))
    return in_maps, placements


_BUILD_CACHE = {}


def kernel(L0_attn_hidden_states, L0_attn_w_q, L0_attn_w_k, L0_attn_w_v,
           L0_attn_w_o, L0_attn_gate, L0_attn_memories, L0_attn_memory_norms,
           L0_ffn_gate_w, L0_ffn_lb_bias, L0_ffn_expert_gate_w,
           L0_ffn_expert_up_w, L0_ffn_expert_down_w, L0_ffn_shared_gate_w,
           L0_ffn_shared_up_w, L0_ffn_shared_down_w):
    cfg = real_cfg()
    c = derive(cfg)
    hs = np.asarray(L0_attn_hidden_states, np.float32)
    in_maps, placements = prep_inputs(
        cfg, hs,
        np.asarray(L0_attn_w_q, np.float32), np.asarray(L0_attn_w_k, np.float32),
        np.asarray(L0_attn_w_v, np.float32), np.asarray(L0_attn_w_o, np.float32),
        np.asarray(L0_attn_gate, np.float32),
        np.asarray(L0_attn_memories, np.float32),
        np.asarray(L0_attn_memory_norms, np.float32),
        np.asarray(L0_ffn_gate_w, np.float32),
        np.asarray(L0_ffn_lb_bias, np.float32),
        np.asarray(L0_ffn_expert_gate_w, np.float32),
        np.asarray(L0_ffn_expert_up_w, np.float32),
        np.asarray(L0_ffn_expert_down_w, np.float32),
        np.asarray(L0_ffn_shared_gate_w, np.float32),
        np.asarray(L0_ffn_shared_up_w, np.float32),
        np.asarray(L0_ffn_shared_down_w, np.float32))

    if "real" not in _BUILD_CACHE:
        _BUILD_CACHE["real"] = build(cfg, dbg=False)
    nc, _, _ = _BUILD_CACHE["real"]

    res = run_bass_kernel_spmd(nc, in_maps, core_ids=list(range(8)))

    outv = np.zeros((4, c["S"], c["H"]), np.float32)
    for cid in range(8):
        own_b, seg_in = placements[cid]
        outv[own_b, seg_in:seg_in + c["OWN"]] = res.results[cid]["out"]
    return outv


# version 18 (final)
# speedup vs baseline: 1.1138x; 1.1138x over previous
import numpy as np
import ml_dtypes
from contextlib import ExitStack

import concourse.bass as bass
import concourse.mybir as mybir
from concourse.bass_utils import run_bass_kernel_spmd

F32 = mybir.dt.float32
BF16 = mybir.dt.bfloat16
I32 = mybir.dt.int32
ALU = mybir.AluOpType
AF = mybir.ActivationFunctionType

EPS_ATT = 1e-6
EPS_LN = 1e-5
EPS_TOP = 1e-8

ENGINES = ("pe", "dve", "act", "pool", "sp")


def real_cfg():
    return dict(S=8192, NH=16, M=4, E=8, EI=512)


def small_cfg():
    return dict(S=512, NH=4, M=4, E=8, EI=128)


def derive(cin):
    c = dict(cin)
    c["D"] = 64
    c["H"] = c["NH"] * 64
    c["FC"] = c["H"] // 128
    c["CH"] = 128
    c["SO"] = c["S"] // 2
    c["TILE"] = min(512, c["SO"])
    c["CPT"] = c["TILE"] // c["CH"]
    c["NT"] = c["SO"] // c["TILE"]
    c["NB"] = 1
    c["NBS"] = c["SO"]
    c["OWN"] = c["SO"]
    c["OH"] = c["OWN"] // 2
    c["CT"] = min(512, c["OH"])
    c["TPH"] = c["OH"] // c["CT"]
    c["FOW"] = min(512, c["H"])
    c["NFO"] = c["H"] // c["FOW"]
    c["SI"] = 2 * c["EI"]
    c["EIC"] = c["EI"] // 128
    c["SIC"] = c["SI"] // 128
    return c



class Plan:
    def __init__(self):
        self.items = {e: [] for e in ENGINES}
        self.needed = set()

    def add(self, eng, kind, fn, deps, inc):
        toks = [d for d in deps if d is not None]
        self.items[eng].append([kind, fn, toks, inc])
        return (eng, len(self.items[eng]) - 1)


class Rs:
    def __init__(self, name):
        self.name = name
        self.w = None
        self.readers = []

    def rd(self):
        return [self.w] if self.w is not None else []

    def wd(self):
        d = list(self.readers)
        if self.w is not None:
            d.append(self.w)
        return d

    def did_read(self, tok):
        self.readers.append(tok)

    def did_write(self, tok):
        self.w = tok
        self.readers = []


class K:
    def __init__(self, plan):
        self.p = plan

    def _deps(self, reads, writes, deps):
        d = list(deps)
        for r in reads:
            d += r.rd()
        for w in writes:
            d += w.wd()
        return d

    def op(self, eng, fn, reads=(), writes=(), deps=(), inc=1):
        tok = self.p.add(eng, "op", fn, self._deps(reads, writes, deps), inc)
        for r in reads:
            r.did_read(tok)
        for w in writes:
            w.did_write(tok)
        return tok

    def raw(self, eng, fn, reads=(), writes=(), deps=(), incs=0):
        tok = self.p.add(eng, "raw", fn, self._deps(reads, writes, deps), incs)
        if incs:
            self.p.needed.add(tok)
        for r in reads:
            r.did_read(tok)
        for w in writes:
            w.did_write(tok)
        return tok


def emit_all(nc, plan, es):
    NQ = 8
    for e in ENGINES:
        for i, it in enumerate(plan.items[e]):
            for (de, di) in it[2]:
                if de == e == "pe":
                    continue
                plan.needed.add((de, di))
            if it[0] == "op" and it[3] == 16:
                plan.needed.add((e, i))
    csem = {e: es.enter_context(nc.semaphore(f"sem_{e}_c")) for e in ENGINES}
    dsem = {}
    for e in ("pool", "sp"):
        for q in range(NQ):
            dsem[(e, q)] = es.enter_context(nc.semaphore(f"sem_{e}_d{q}"))
    psem = es.enter_context(nc.semaphore("sem_poll"))
    rawsems = {}

    msval = {}
    pre_wait = {}
    ccnt = {e: 0 for e in ENGINES}
    dcnt = {}
    dn = {e: 0 for e in ENGINES}
    for e in ENGINES:
        for i, it in enumerate(plan.items[e]):
            kind, fn, deps, inc = it
            is_dma = (kind == "op" and inc == 16)
            if kind == "raw" and inc > 0:
                rawsems[(e, i)] = es.enter_context(
                    nc.semaphore(f"sem_raw_{e}_{i}"))
                msval[(e, i)] = ("r", (e, i), inc)
                continue
            if (e, i) not in plan.needed and not is_dma:
                continue
            if is_dma:
                q = dn[e] % NQ
                dn[e] += 1
                prev = dcnt.get((e, q), 0)
                if prev > 0:
                    pre_wait[(e, i)] = (q, prev)
                dcnt[(e, q)] = prev + 16
                msval[(e, i)] = ("d", e, q, dcnt[(e, q)])
            else:
                ccnt[e] += inc
                msval[(e, i)] = ("c", e, ccnt[e])

    block = es.enter_context(nc.Block())
    eng_attr = {"pe": "tensor", "dve": "vector", "act": "scalar",
                "pool": "gpsimd", "sp": "sync"}

    def sem_of(mv):
        if mv[0] == "c":
            return csem[mv[1]], mv[2]
        if mv[0] == "d":
            return dsem[(mv[1], mv[2])], mv[3]
        return rawsems[mv[1]], mv[2]

    def make_body(e):
        def body(eng):
            hw = {}
            def wait_to(sem, val):
                if hw.get(sem.name if hasattr(sem, "name") else id(sem), 0) < val:
                    eng.wait_ge(sem, val)
                    hw[sem.name if hasattr(sem, "name") else id(sem)] = val
            for i, (kind, fn, deps, inc) in enumerate(plan.items[e]):
                want = {}
                for (de, di) in deps:
                    if de == e == "pe":
                        continue
                    mv = msval.get((de, di))
                    if mv is None:
                        continue
                    sem, v = sem_of(mv)
                    kkey = sem.name if hasattr(sem, "name") else id(sem)
                    if v > want.get(kkey, (None, 0))[1]:
                        want[kkey] = (sem, v)
                for kkey in sorted(want):
                    wait_to(*want[kkey])
                if (e, i) in pre_wait:
                    q, v = pre_wait[(e, i)]
                    wait_to(dsem[(e, q)], v)
                if kind == "op":
                    inst = fn(eng)
                    mv = msval.get((e, i))
                    if mv is not None:
                        sem, _ = sem_of(mv)
                        inst.then_inc(sem, inc)
                elif kind == "raw" and inc > 0:
                    fn(eng, rawsems[(e, i)], psem)
                else:
                    fn(eng, None, psem)
            if e == "sp":
                for q in range(NQ):
                    if dcnt.get((e, q), 0) > 0:
                        eng.wait_ge(dsem[(e, q)], dcnt[(e, q)])
        return body

    for e in ENGINES:
        getattr(block, eng_attr[e])(make_body(e))



def build(cin, dbg=False, races=True, phases=None, mparts=None):
    c = derive(cin)
    S, NH, M, E, EI = c["S"], c["NH"], c["M"], c["E"], c["EI"]
    D, H, FC = c["D"], c["H"], c["FC"]
    CH, TILE, CPT, NT = c["CH"], c["TILE"], c["CPT"], c["NT"]
    NB, NBS, OWN, OH = c["NB"], c["NBS"], c["OWN"], c["OH"]
    SO = c["SO"]
    CT, TPH = c["CT"], c["TPH"]
    SI, EIC, SIC = c["SI"], c["EIC"], c["SIC"]
    NFO, FOW = c["NFO"], c["FOW"]
    M65 = M * 65
    NH325 = NH * 325
    NOC = OWN // 128
    CTC = CT // 128

    PH = set(phases) if phases is not None else {"pq0", "state", "main0",
                                                  "mbox", "c"}
    MP = set(mparts) if mparts is not None else {"kv", "skt", "heads", "epi",
                                                 "wo", "ln", "x1t", "rout",
                                                 "hat", "hatm", "hcm", "hni", "hst", "x"}
    nc = bass.Bass("TRN2", target_bir_lowering=False,
                   detect_race_conditions=races)

    hsT = [nc.dram_tensor("hsT_own", [H, SO], F32, kind="ExternalInput").ap()]
    hsT_state = nc.dram_tensor("hsT_state", [H, SO], F32,
                               kind="ExternalInput").ap()
    hstok = [nc.dram_tensor("hs_own", [SO, H], F32, kind="ExternalInput").ap()]
    wT_in = [nc.dram_tensor(n, [H, H], F32, kind="ExternalInput").ap()
             for n in ("wqT", "wkT", "wvT", "woT")]
    memcat = nc.dram_tensor("memcat", [NH, D, M65], F32, kind="ExternalInput").ap()
    wmn = nc.dram_tensor("wmn", [H, NH * M], F32, kind="ExternalInput").ap()
    gate_in = nc.dram_tensor("gate_in", [NH, 1], F32, kind="ExternalInput").ap()
    gwT = nc.dram_tensor("gwT", [H, E], F32, kind="ExternalInput").ap()
    lbb = nc.dram_tensor("lbb", [1, E], F32, kind="ExternalInput").ap()
    egT = nc.dram_tensor("egT", [E, H, EI], F32, kind="ExternalInput").ap()
    euT = nc.dram_tensor("euT", [E, H, EI], F32, kind="ExternalInput").ap()
    edT = nc.dram_tensor("edT", [E, EI, H], F32, kind="ExternalInput").ap()
    sgT = nc.dram_tensor("sgT", [H, SI], F32, kind="ExternalInput").ap()
    suT = nc.dram_tensor("suT", [H, SI], F32, kind="ExternalInput").ap()
    sdT = nc.dram_tensor("sdT", [SI, H], F32, kind="ExternalInput").ap()
    cfgv = nc.dram_tensor("cfgv", [1, 8], I32, kind="ExternalInput").ap()

    out = nc.dram_tensor("out", [OWN, H], F32, kind="ExternalOutput").ap()

    sq_scr_t = nc.dram_tensor("sq_scr", [FC, 128, NBS], BF16, kind="Internal")
    x1t_scr_t = nc.dram_tensor("x1t_scr", [FC, 128, NBS], BF16, kind="Internal")
    sel_scr_t = nc.dram_tensor("sel_scr", [NBS, E], BF16, kind="Internal")
    SSW = (NH // 2) * 65
    sdump_t = nc.dram_tensor("sdump", [2, 128, SSW], F32, kind="Internal")
    crel_in_t = nc.dram_tensor("crel_in", [1, NH * M], F32, kind="Internal")
    crel_out_t = nc.dram_tensor("crel_out", [1, NH * M], F32, kind="Internal")
    cstat_in_t = nc.dram_tensor("cstat_in", [1, 16], F32, kind="Internal")
    cstat_out_t = nc.dram_tensor("cstat_out", [1, 16], F32, kind="Internal")

    dbg_outs = {}
    if dbg:
        dbg_outs["dbg_x1"] = nc.dram_tensor("dbg_x1", [NBS, H], F32,
                                            kind="ExternalOutput").ap()
        dbg_outs["dbg_stats"] = nc.dram_tensor("dbg_stats", [1, 16], F32,
                                               kind="ExternalOutput").ap()
        dbg_outs["dbg_w"] = nc.dram_tensor("dbg_w", [NB, NH, 5], F32,
                                           kind="ExternalOutput").ap()
        dbg_outs["dbg_wmean"] = nc.dram_tensor("dbg_wmean", [1, E], F32,
                                               kind="ExternalOutput").ap()

    with ExitStack() as es:
        ctx = es.enter_context

        W_AR = ctx(nc.sbuf_tensor("W_AR", [128, 4 * FC * H], BF16))
        EXPCOLS = 2 * FC * EI + EIC * H
        W2 = (NH // 2) * 325

        _off = [0]
        _c_marks = {}
        def _carve(n):
            c0 = _off[0]
            _off[0] += (n + 1) & ~1
            return c0
        O_hsT = [_carve(FC * TILE) for _ in range(2)]
        O_sqT = [_carve(FC * TILE) for _ in range(2)]
        O_x1tt = [_carve(FC * TILE) for _ in range(2)]
        O_cb = [_carve(NH325) for _ in range(2)]
        O_rhs = _carve(W2)
        O_skt = [_carve(H) for _ in range(2)]
        O_skT = [_carve(H) for _ in range(2)]
        O_va = [_carve(NH * 65) for _ in range(2)]
        O_am = [_carve(512) for _ in range(2)]
        O_ep = [_carve(H) for _ in range(4)]
        O_outpre = _carve(H)
        O_opT = _carve(H)
        O_x1c = _carve(H)
        O_xsq = _carve(H)
        O_x1r = _carve(2 * H)
        O_hs = [_carve(2 * H) for _ in range(2)]
        O_elu = [_carve(1024) for _ in range(2)]
        O_ssb = _carve((NH // 2) * 65 * 2)
        att_end = _off[0]
        _off[0] = 0
        O_exp = _carve(2 * EXPCOLS)
        O_acc = _carve(2 * (OH // 128) * H)
        O_hb = _carve(SIC * CT)
        O_silu = _carve(CT)
        O_selown = _carve(NOC * E)
        O_xout = _carve(2 * H)
        O_x2r = _carve(2 * H)
        O_x1tok = _carve(H)
        UNW = max(att_end, _off[0])
        UN = ctx(nc.sbuf_tensor("UN", [128, UNW], BF16))
        def _v(o, n):
            return UN[:, o:o + n]
        def _vf(o, n):
            return UN[:, o:o + 2 * n].bitcast(F32)
        hsT_b = [_v(O_hsT[i], FC * TILE) for i in range(2)]
        sqT_b = [_v(O_sqT[i], FC * TILE) for i in range(2)]
        x1t_tile = [_v(O_x1tt[i], FC * TILE) for i in range(2)]
        cb_b = [_v(O_cb[i], NH325) for i in range(2)]
        rhscat = _v(O_rhs, W2)
        skt_b = [_v(O_skt[i], H) for i in range(2)]
        skT_b = [_v(O_skT[i], H) for i in range(2)]
        vaug_b = [_v(O_va[i], NH * 65) for i in range(2)]
        am_b = [_v(O_am[i], 512) for i in range(2)]
        ep_t = [_v(O_ep[i], H) for i in range(4)]
        outpre_b = _v(O_outpre, H)
        opT_b = _v(O_opT, H)
        x1c_b = _v(O_x1c, H)
        xsq_scr = _v(O_xsq, H)
        x1r_b = _vf(O_x1r, H)
        hs_b = [_vf(O_hs[i], H) for i in range(2)]
        elu_t1 = _vf(O_elu[0], 512)
        elu_t2 = _vf(O_elu[1], 512)
        ssb = _vf(O_ssb, (NH // 2) * 65)
        EXP_AR = _v(O_exp, 2 * EXPCOLS)
        accv_full = _vf(O_acc, (OH // 128) * H)
        _hbv = _v(O_hb, SIC * CT)
        h_b = [_hbv, _hbv]
        silu_b = _v(O_silu, CT)
        sel_own = _v(O_selown, NOC * E)
        xout = _vf(O_xout, H)
        x2r_b = _vf(O_x2r, H)
        x1tok_b = _v(O_x1tok, H)
        sqs_tile = ctx(nc.sbuf_tensor("sqs_tile", [128, FC], F32))
        sqsum = ctx(nc.sbuf_tensor("sqsum", [128, FC], F32))
        sqsum_bf = ctx(nc.sbuf_tensor("sqsum_bf", [128, FC], BF16))
        wmn_sb = ctx(nc.sbuf_tensor("wmn_sb", [128, FC * NH * M], BF16))
        gwt_sb = ctx(nc.sbuf_tensor("gwt_sb", [128, FC * E], BF16))
        biasb = ctx(nc.sbuf_tensor("biasb", [128, E], F32))
        coefb = ctx(nc.sbuf_tensor("coefb", [128, NH * 5], F32))
        wmeanb = ctx(nc.sbuf_tensor("wmeanb", [128, E], F32))
        sc_raw = ctx(nc.sbuf_tensor("sc_raw", [128, NH * 5], F32))
        sc_clip = ctx(nc.sbuf_tensor("sc_clip", [128, NH * 5], F32))
        sc_scale = ctx(nc.sbuf_tensor("sc_scale", [128, NH * 5], F32))
        lncol = ctx(nc.sbuf_tensor("lncol", [128, 8], F32))
        sco_p = ctx(nc.sbuf_tensor("sco_p", [128, E], F32))
        sco_e = ctx(nc.sbuf_tensor("sco_e", [128, E], F32))
        scores = ctx(nc.sbuf_tensor("scores", [128, E], F32))
        rmx = ctx(nc.sbuf_tensor("rmx", [128, 4], F32))
        ismax = ctx(nc.sbuf_tensor("ismax", [128, E], F32))
        smask = ctx(nc.sbuf_tensor("smask", [128, E], F32))
        self32 = ctx(nc.sbuf_tensor("self32", [128, E], F32))
        wsel = ctx(nc.sbuf_tensor("wsel", [128, 16], F32))
        wselacc = ctx(nc.sbuf_tensor("wselacc", [128, 16], F32))
        ones_col_f = ctx(nc.sbuf_tensor("ones_col_f", [128, 1], F32))
        sel_b = [ctx(nc.sbuf_tensor(f"sel_b{i}", [128, CPT * E], BF16)) for i in range(2)]
        statacc = ctx(nc.sbuf_tensor("statacc", [16, 1], F32))
        g16 = ctx(nc.sbuf_tensor("g16", [NH, 1], F32))
        omg16 = ctx(nc.sbuf_tensor("omg16", [NH, 1], F32))
        gtmp = ctx(nc.sbuf_tensor("gtmp", [NH, 4], F32))
        relrow = ctx(nc.sbuf_tensor("relrow", [1, NH * M], F32))
        relt16 = ctx(nc.sbuf_tensor("relt16", [NH, M], F32))
        coef16 = ctx(nc.sbuf_tensor("coef16", [NH, 5], F32))
        rowbuf = ctx(nc.sbuf_tensor("rowbuf", [1, NH * 5], F32))
        rowbuf_bf = ctx(nc.sbuf_tensor("rowbuf_bf", [1, NH * 5], BF16))
        ones_row = ctx(nc.sbuf_tensor("ones_row", [1, 128], BF16))
        ones_col = ctx(nc.sbuf_tensor("ones_col", [128, 1], BF16))
        identb = ctx(nc.sbuf_tensor("identb", [128, 128], BF16))
        trib = ctx(nc.sbuf_tensor("trib", [128, 512], BF16))
        lbb_sb = ctx(nc.sbuf_tensor("lbb_sb", [1, E], F32))
        lbb_bf = ctx(nc.sbuf_tensor("lbb_bf", [1, E], BF16))
        epsb = ctx(nc.sbuf_tensor("epsb", [128, 1], F32))
        cfg_sb = ctx(nc.sbuf_tensor("cfg_sb", [1, 8], I32))
        mrow = ctx(nc.sbuf_tensor("mrow", [1, 64], F32))
        wm_row = ctx(nc.sbuf_tensor("wm_row", [1, E], F32))
        wm_bf = ctx(nc.sbuf_tensor("wm_bf", [1, E], BF16))
        wtil = ctx(nc.sbuf_tensor("wtil", [128, 1], F32))
        dbg_xout = (ctx(nc.sbuf_tensor("dbg_xout", [128, H], F32))
                    if dbg else None)

        PW = ctx(nc.psum_tensor("PW", [128, 1024], F32))
        PN = ctx(nc.psum_tensor("PN", [128, 1024], F32))
        PA = ctx(nc.psum_tensor("PA", [128, 512], F32))
        PT = ctx(nc.psum_tensor("PT", [128, 1024], BF16))
        PS = ctx(nc.psum_tensor("PS", [128, 512], F32))
        PS2 = ctx(nc.psum_tensor("PS2", [128, 512], F32))

        plan = Plan()
        k = K(plan)

        r_war = Rs("W_AR")
        r_exp = [Rs(f"EXP{i}") for i in range(2)]
        r_acc = Rs("ACC")
        r_hsT = [Rs(f"hsT{i}") for i in range(2)]
        r_sqT = [Rs(f"sqT{i}") for i in range(2)]
        r_hs = [Rs(f"hs{i}") for i in range(2)]
        r_skt = [Rs(f"skt{i}") for i in range(2)]
        r_skT = [Rs(f"skT{i}") for i in range(2)]
        r_vaug = [Rs(f"vaug{i}") for i in range(2)]
        r_am = [Rs(f"am{i}") for i in range(2)]
        r_cb = [Rs(f"cb{i}") for i in range(2)]
        r_rhs_mem = Rs("rhs_mem")
        r_rhs_S = Rs("rhs_S")
        r_ssb = Rs("ssb")
        r_outpre = Rs("outpre")
        r_opT = Rs("opT")
        r_x1r = Rs("x1r")
        r_x1c = Rs("x1c")
        r_x1tt = [Rs(f"x1tt{i}") for i in range(2)]
        r_elu1 = Rs("elu1")
        r_elu2 = Rs("elu2")
        r_sqs = Rs("sqs")
        r_sqsum = Rs("sqsum")
        r_sqsum_bf = Rs("sqsum_bf")
        r_consts = Rs("consts")
        r_coefb = Rs("coefb")
        r_scraw = Rs("scraw")
        r_scclip = Rs("scclip")
        r_scscale = Rs("scscale")
        r_ept = [Rs(f"ept{i}") for i in range(4)]
        r_xsq = Rs("xsq")
        r_lncol = Rs("lncol")
        r_scop = Rs("scop")
        r_scoe = Rs("scoe")
        r_scores = Rs("scores")
        r_rmx = Rs("rmx")
        r_ismax = Rs("ismax")
        r_smask = Rs("smask")
        r_self32 = Rs("self32")
        r_wsel = Rs("wsel")
        r_wselacc = Rs("wselacc")
        r_sel_b = [Rs(f"selb{i}") for i in range(2)]
        r_statacc = Rs("statacc")
        r_small = Rs("small16")
        r_relrow = Rs("relrow")
        r_selown = Rs("selown")
        _rhb = Rs("hb")
        r_hb = [_rhb, _rhb]
        r_silu = Rs("silu")
        r_wtil = Rs("wtil")
        r_xout = Rs("xout")
        r_x2r = Rs("x2r")
        r_x1tok = Rs("x1tok")
        r_mrow = Rs("mrow")
        r_both = Rs("both")
        r_wmrow = Rs("wmrow")
        r_wmeanb = Rs("wmeanb")
        r_PW = [Rs("PW0"), Rs("PW1")]
        r_PN = [Rs("PN0"), Rs("PN1")]
        r_PA = Rs("PA")
        r_PT = Rs("PT")
        r_PS = Rs("PS")
        r_PS2 = Rs("PS2")
        r_sqscr = Rs("sq_scr")
        r_x1scr_h = [Rs("x1_scr0"), Rs("x1_scr1")]
        r_x1scr_t0 = Rs("x1_scr_t0")
        r_war_h = [Rs("W_AR_a"), Rs("W_AR_b")]
        r_war_t0 = Rs("W_AR_t0")
        r_selscr = Rs("sel_scr")
        r_sdump = Rs("sdump")
        r_crel = Rs("crel")
        r_cstat = Rs("cstat")

        def fn_wl4(g, sem, psem):
            for wi in range(4):
                g.dma_start(
                    out=bass.AP(W_AR, wi * FC * H, [[4 * FC * H, 128], [H, FC], [1, H]]),
                    in_=bass.AP(wT_in[wi].tensor, 0, [[H, 128], [128 * H, FC], [1, H]])
                ).then_inc(sem, 16)
        k.raw("pool", fn_wl4, writes=[r_war], incs=64)

        for h2 in range(2):
            def fn_memcat(g, h2=h2):
                return g.dma_start(
                    out=rhscat[h2 * 64:h2 * 64 + 64, :].rearrange(
                        "p (a b) -> p a b", b=325)[:, :, 65:325],
                    in_=bass.AP(memcat.tensor, h2 * D * M65,
                                [[M65, 64], [2 * D * M65, NH // 2], [1, M65]]))
            k.op("pool", fn_memcat, writes=[r_rhs_mem], inc=16)

        def fn_wmn(g):
            return g.dma_start(
                out=bass.AP(wmn_sb, 0, [[FC * NH * M, 128], [NH * M, FC], [1, NH * M]]),
                in_=bass.AP(wmn.tensor, 0, [[NH * M, 128], [128 * NH * M, FC], [1, NH * M]]))
        k.op("pool", fn_wmn, writes=[r_consts], inc=16)

        def fn_gwt(g):
            return g.dma_start(
                out=bass.AP(gwt_sb, 0, [[FC * E, 128], [E, FC], [1, E]]),
                in_=bass.AP(gwT.tensor, 0, [[E, 128], [128 * E, FC], [1, E]]))
        k.op("pool", fn_gwt, writes=[r_consts], inc=16)

        t_gate = k.op("pool", lambda g: g.dma_start(out=g16[:], in_=gate_in),
                      writes=[r_small], inc=16)
        t_lbb = k.op("pool", lambda g: g.dma_start(out=lbb_sb[:], in_=lbb),
                     writes=[r_small], inc=16)
        k.op("pool", lambda g: g.dma_start(out=cfg_sb[:], in_=cfgv),
             writes=[r_small], inc=16)

        k.op("pool", lambda g: g.memset(ones_row[:], 1.0), writes=[r_consts])
        k.op("pool", lambda g: g.memset(ones_col[:], 1.0), writes=[r_consts])
        k.op("pool", lambda g: g.memset(ones_col_f[:], 1.0), writes=[r_consts])
        k.op("pool", lambda g: g.memset(identb[:], 1.0), writes=[r_consts])
        k.op("pool", lambda g: g.affine_select(identb[:], identb[:], [[1, 128]],
                                               ALU.is_equal, 0.0, base=0,
                                               channel_multiplier=-1),
             writes=[r_consts])
        k.op("pool", lambda g: g.memset(trib[:], 1.0), writes=[r_consts])
        for kk in range(4):
            def fn_tri(g, kk=kk):
                v = trib[:, kk * 128:(kk + 1) * 128]
                return g.affine_select(v, v, [[1, 128]], ALU.is_ge, 0.0,
                                       base=0, channel_multiplier=-1)
            k.op("pool", fn_tri, writes=[r_consts])
        for i in range(2):
            def fn_vone(g, i=i):
                return g.memset(
                    vaug_b[i].rearrange("p (h c) -> p h c", c=65)[:, :, 64:65], 1.0)
            k.op("pool", fn_vone, writes=[r_vaug[i]])

        k.op("pool", lambda g: g.memset(statacc[:], 0.0), writes=[r_statacc])
        k.op("pool", lambda g: g.memset(wselacc[:], 0.0), writes=[r_wselacc])
        k.op("pool", lambda g: g.memset(epsb[:], EPS_LN), writes=[r_consts])

        k.op("act", lambda s_: s_.activation(gtmp[:, 0:1], g16[:], AF.Exp, scale=-1.0),
             reads=[r_small], writes=[r_small], deps=[t_gate])
        k.op("dve", lambda v: v.tensor_scalar_add(gtmp[:, 1:2], gtmp[:, 0:1], 1.0),
             reads=[r_small], writes=[r_small])
        k.op("dve", lambda v: v.reciprocal(g16[:], gtmp[:, 1:2]),
             reads=[r_small], writes=[r_small])
        k.op("dve", lambda v: v.tensor_scalar(omg16[:], g16[:], -1.0, 1.0,
                                              ALU.mult, ALU.add),
             reads=[r_small], writes=[r_small])

        k.op("dve", lambda v: v.tensor_copy(lbb_bf[:], lbb_sb[:]),
             reads=[r_small], writes=[r_small], deps=[t_lbb])
        k.op("pe", lambda t_: t_.matmul(PA[:, 0:E], ones_row[:], lbb_bf[:],
                                        start=True, stop=True),
             reads=[r_consts, r_small], writes=[r_PA])
        k.op("act", lambda s_: s_.copy(biasb[:], PA[:, 0:E]),
             reads=[r_PA], writes=[r_consts])

        def sq_scr_ap(bi, t0):
            return bass.AP(sq_scr_t, bi * S + t0,
                           [[NBS, 128], [128 * NBS, FC], [1, TILE]])

        def x1_scr_ap(bi, t0):
            return bass.AP(x1t_scr_t, bi * S + t0,
                           [[NBS, 128], [128 * NBS, FC], [1, TILE]])

        pending_tail = [None, None]

        def emit_tail(idx):
            fn, pending_tail[idx] = pending_tail[idx], None
            if fn is not None:
                fn()

        rel_tail = [None]
        for bi in range(NB):
            if f"pq{bi}" not in PH:
                continue
            k.op("pool", lambda g: g.memset(sqsum[:], 0.0), writes=[r_sqsum])
            def emit_pq_load(t, bi=bi):
                tb = t % 2
                t0 = t * TILE
                def fn_hsT(g, bi=bi, t0=t0, tb=tb):
                    return g.dma_start(
                        out=hsT_b[tb][:],
                        in_=bass.AP(hsT[bi].tensor, t0,
                                    [[SO, 128], [128 * SO, FC], [1, TILE]]))
                k.op("pool", fn_hsT, writes=[r_hsT[tb]], inc=16)
            emit_pq_load(0)
            for t in range(NT):
                tb = t % 2
                t0 = t * TILE
                if t + 1 < NT:
                    emit_pq_load(t + 1)

                for foc in range(FC):
                    pw = foc % 2
                    for fic in range(FC):
                        def fn_mm(t_, foc=foc, fic=fic, tb=tb, pw=pw):
                            return t_.matmul(
                                PW[:, pw * 512:pw * 512 + 512],
                                W_AR[:, fic * H + foc * 128:fic * H + foc * 128 + 128],
                                hsT_b[tb][:, fic * TILE:fic * TILE + TILE],
                                start=(fic == 0), stop=(fic == FC - 1))
                        k.op("pe", fn_mm, reads=[r_war, r_hsT[tb]], writes=[r_PW[pw]])
                    def fn_nrelu(s_, pw=pw):
                        return s_.activation(elu_t1[:],
                                             PW[:, pw * 512:pw * 512 + 512],
                                             AF.Relu, scale=-1.0)
                    k.op("act", fn_nrelu, reads=[r_PW[pw]], writes=[r_elu1])
                    k.op("act", lambda s_: s_.activation(elu_t2[:], elu_t1[:],
                                                         AF.Exp, scale=-1.0),
                         reads=[r_elu1], writes=[r_elu2])
                    def fn_comb(v, pw=pw, foc=foc, tb=tb):
                        return v.scalar_tensor_tensor(
                            sqT_b[tb][:, foc * TILE:foc * TILE + TILE],
                            PW[:, pw * 512:pw * 512 + 512], 0.0, elu_t2[:],
                            ALU.max, ALU.add,
                            accum_out=sqs_tile[:, foc:foc + 1])
                    k.op("dve", fn_comb, reads=[r_PW[pw], r_elu2],
                         writes=[r_sqT[tb], r_sqs])
                k.op("dve", lambda v: v.tensor_add(sqsum[:], sqsum[:], sqs_tile[:]),
                     reads=[r_sqs], writes=[r_sqsum])
                def fn_store(g, bi=bi, t0=t0, tb=tb):
                    return g.dma_start(out=sq_scr_ap(bi, t0), in_=sqT_b[tb][:])
                k.op("sp", fn_store, reads=[r_sqT[tb]], writes=[r_sqscr], inc=16)

            if "state" in PH:
                k.op("pool", lambda g: g.memset(ssb[:], 0.0), writes=[r_ssb])
                def fn_zdump(g):
                    return g.dma_start(
                        out=bass.AP(sdump_t, 0, [[SSW, 128], [1, SSW]]),
                        in_=ssb[:])
                k.op("sp", fn_zdump, reads=[r_ssb], writes=[r_sdump], inc=16)
                def emit_state_load(t):
                    tb = t % 2
                    t0 = t * TILE
                    def fn_hsTs(g, t0=t0, tb=tb):
                        return g.dma_start(
                            out=hsT_b[tb][:],
                            in_=bass.AP(hsT_state.tensor, t0,
                                        [[SO, 128], [128 * SO, FC], [1, TILE]]))
                    k.op("pool", fn_hsTs, writes=[r_hsT[tb]], inc=16)
                emit_state_load(0)
                for t in range(NT):
                    tb = t % 2
                    t0 = t * TILE
                    if t + 1 < NT:
                        emit_state_load(t + 1)
                    for ci in range(CPT):
                        cbp = ci % 2
                        for wofs in (1, 2):
                            for half in range(NFO):
                                pw = half % 2
                                for fic in range(FC):
                                    def fn_mm(t_, wofs=wofs, half=half, fic=fic,
                                              tb=tb, ci=ci, pw=pw):
                                        return t_.matmul(
                                            PW[:, pw * 512:pw * 512 + FOW],
                                            hsT_b[tb][:, fic * TILE + ci * 128:
                                                      fic * TILE + ci * 128 + 128],
                                            W_AR[:, wofs * FC * H + fic * H + half * FOW:
                                                 wofs * FC * H + fic * H + half * FOW + FOW],
                                            start=(fic == 0), stop=(fic == FC - 1))
                                    k.op("pe", fn_mm, reads=[r_war, r_hsT[tb]],
                                         writes=[r_PW[pw]])
                                if wofs == 1:
                                    def fn_nrelu(s_, pw=pw):
                                        return s_.activation(
                                            elu_t1[:, 0:FOW],
                                            PW[:, pw * 512:pw * 512 + FOW],
                                            AF.Relu, scale=-1.0)
                                    k.op("act", fn_nrelu, reads=[r_PW[pw]],
                                         writes=[r_elu1])
                                    k.op("act",
                                         lambda s_: s_.activation(elu_t2[:, 0:FOW],
                                                                  elu_t1[:, 0:FOW],
                                                                  AF.Exp,
                                                                  scale=-1.0),
                                         reads=[r_elu1], writes=[r_elu2])
                                    def fn_comb(v, pw=pw, half=half, cbp=cbp):
                                        return v.scalar_tensor_tensor(
                                            skt_b[cbp][:, half * FOW:half * FOW + FOW],
                                            PW[:, pw * 512:pw * 512 + FOW], 0.0,
                                            elu_t2[:, 0:FOW],
                                            ALU.max, ALU.add)
                                    k.op("dve", fn_comb, reads=[r_PW[pw], r_elu2],
                                         writes=[r_skt[cbp]])
                                else:
                                    def fn_ve(v, pw=pw, half=half, cbp=cbp):
                                        nh_g = FOW // 64
                                        return v.tensor_copy(
                                            vaug_b[cbp].rearrange(
                                                "p (h c) -> p h c", c=65)[
                                                :, half * nh_g:(half + 1) * nh_g, 0:64],
                                            PW[:, pw * 512:pw * 512 + FOW])
                                    k.op("dve", fn_ve, reads=[r_PW[pw]],
                                         writes=[r_vaug[cbp]])
                        evens = [h for h in range(NH) if h % 2 == 0]
                        odds = [h for h in range(NH) if h % 2 == 1]
                        groups = []
                        for lst in (evens, odds):
                            for gi in range(0, len(lst), 4):
                                groups.append(lst[gi:gi + 4])
                        for hg, heads in enumerate(groups):
                            p2 = heads[0] % 2
                            gl = len(heads)
                            sup = heads[0] // 8
                            for j, h in enumerate(heads):
                                def fn_st(t_, h=h, j=j, p2=p2, cbp=cbp, gl=gl):
                                    kw = {}
                                    bank = PS if p2 == 0 else PS2
                                    if p2 == 1:
                                        kw["tile_position"] = (0, 64)
                                    return t_.matmul(
                                        bank[p2 * 64:p2 * 64 + 64,
                                             j * 65:j * 65 + 65],
                                        skt_b[cbp][:, h * 64:h * 64 + 64],
                                        vaug_b[cbp][:, h * 65:h * 65 + 65],
                                        start=(j == 0), stop=(j == gl - 1), **kw)
                                k.op("pe", fn_st, reads=[r_skt[cbp], r_vaug[cbp]],
                                     writes=[r_PS if p2 == 0 else r_PS2])
                            def fn_sa(v, p2=p2, sup=sup, gl=gl):
                                bank = PS if p2 == 0 else PS2
                                return v.tensor_add(
                                    ssb[p2 * 64:p2 * 64 + 64,
                                        sup * 260:sup * 260 + gl * 65],
                                    ssb[p2 * 64:p2 * 64 + 64,
                                        sup * 260:sup * 260 + gl * 65],
                                    bank[p2 * 64:p2 * 64 + 64, 0:gl * 65])
                            k.op("dve", fn_sa, reads=[r_PS if p2 == 0 else r_PS2],
                                 writes=[r_ssb])
                def fn_sdump(g):
                    return g.dma_start(
                        out=bass.AP(sdump_t, 128 * SSW, [[SSW, 128], [1, SSW]]),
                        in_=ssb[:])
                k.op("sp", fn_sdump, reads=[r_ssb], writes=[r_sdump], inc=16)
                def fn_sreload(g, sem, psem):
                    with g.register("sr") as sr, g.register("soff") as soff:
                        g.reg_load(sr, cfg_sb[:1, 0:1])
                        g.reg_mul(soff, sr, 128 * SSW)
                        g.dma_start(
                            out=ssb[:],
                            in_=bass.AP(sdump_t, soff, [[SSW, 128], [1, SSW]])
                        ).then_inc(sem, 16)
                k.raw("pool", fn_sreload, reads=[r_small, r_sdump],
                      writes=[r_ssb], incs=16)
                def fn_sc0(v):
                    return v.tensor_copy(
                        rhscat.rearrange("p (a b) -> p a b", b=325)[:, :, 0:65],
                        ssb.rearrange("p (a b) -> p a b", b=65))
                k.op("dve", fn_sc0, reads=[r_ssb], writes=[r_rhs_S])

            k.op("dve", lambda v: v.tensor_copy(sqsum_bf[:], sqsum[:]),
                 reads=[r_sqsum], writes=[r_sqsum_bf])
            for fic in range(FC):
                def fn_rel(t_, fic=fic):
                    return t_.matmul(PA[0:1, 0:NH * M],
                                     sqsum_bf[:, fic:fic + 1],
                                     wmn_sb[:, fic * NH * M:(fic + 1) * NH * M],
                                     start=(fic == 0), stop=(fic == FC - 1))
                k.op("pe", fn_rel, reads=[r_sqsum_bf, r_consts], writes=[r_PA])
            k.op("act", lambda s_: s_.copy(relrow[:], PA[0:1, 0:NH * M]),
                 reads=[r_PA], writes=[r_relrow])
            def fn_relcc_a(g, sem, psem):
                g.dma_start(out=crel_in_t[:, :], in_=relrow[:]).then_inc(sem, 16)
                g.wait_ge(sem, 16)
                g.collective_compute(
                    "AllReduce", ALU.add,
                    replica_groups=[[0, 1], [2, 3], [4, 5], [6, 7]],
                    ins=[crel_in_t.ap().opt()], outs=[crel_out_t.ap().opt()],
                ).then_inc(sem)
            k.raw("pool", fn_relcc_a, reads=[r_relrow, r_crel],
                  writes=[r_crel], incs=17)
            def emit_rel_tail():
                def fn_relcc_b(g, sem, psem):
                    g.dma_start(out=relrow[:], in_=crel_out_t[:, :]).then_inc(sem, 16)
                k.raw("pool", fn_relcc_b, reads=[r_crel],
                      writes=[r_relrow], incs=16)
                def fn_r16(g):
                    return g.dma_start(
                        out=bass.AP(relt16, 0, [[M, NH], [1, M]]),
                        in_=bass.AP(relrow, 0, [[NH * M, 1], [M, NH], [1, M]]))
                k.op("pool", fn_r16, reads=[r_relrow], writes=[r_small], inc=16)
                k.op("dve", lambda v: v.reduce_max(gtmp[:, 0:1], relt16[:],
                                                   axis=mybir.AxisListType.X),
                     reads=[r_small], writes=[r_small])
                k.op("dve", lambda v: v.tensor_scalar(gtmp[:, 1:2], gtmp[:, 0:1],
                                                      -1.0 / S, None, ALU.mult),
                     reads=[r_small], writes=[r_small])
                k.op("act", lambda s_: s_.activation(relt16[:], relt16[:], AF.Exp,
                                                     bias=gtmp[:, 1:2], scale=1.0 / S),
                     reads=[r_small], writes=[r_small])
                k.op("dve", lambda v: v.reduce_sum(gtmp[:, 2:3], relt16[:],
                                                   axis=mybir.AxisListType.X),
                     reads=[r_small], writes=[r_small])
                k.op("dve", lambda v: v.reciprocal(gtmp[:, 3:4], gtmp[:, 2:3]),
                     reads=[r_small], writes=[r_small])
                k.op("dve", lambda v: v.tensor_scalar(relt16[:], relt16[:], gtmp[:, 3:4],
                                                      None, ALU.mult),
                     reads=[r_small], writes=[r_small])
                k.op("dve", lambda v: v.tensor_copy(coef16[:, 0:1], omg16[:]),
                     reads=[r_small], writes=[r_small])
                k.op("dve", lambda v: v.tensor_scalar(coef16[:, 1:1 + M], relt16[:],
                                                      g16[:], None, ALU.mult),
                     reads=[r_small], writes=[r_small])
                if dbg:
                    def fn_dw(g, bi=bi):
                        return g.dma_start(
                            out=bass.AP(dbg_outs["dbg_w"].tensor, bi * NH * 5,
                                        [[5, NH], [1, 5]]),
                            in_=bass.AP(coef16, 0, [[5, NH], [1, 5]]))
                    k.op("sp", fn_dw, reads=[r_small], inc=16)
                def fn_row(g):
                    return g.dma_start(
                        out=bass.AP(rowbuf, 0, [[NH * 5, 1], [5, NH], [1, 5]]),
                        in_=bass.AP(coef16, 0, [[5, NH], [1, 5]]))
                k.op("pool", fn_row, reads=[r_small], writes=[r_small], inc=16)
                k.op("dve", lambda v: v.tensor_copy(rowbuf_bf[:], rowbuf[:]),
                     reads=[r_small], writes=[r_small])
                k.op("pe", lambda t_: t_.matmul(PA[:, 0:NH * 5], ones_row[:], rowbuf_bf[:],
                                                start=True, stop=True),
                     reads=[r_consts, r_small], writes=[r_PA])
                k.op("act", lambda s_: s_.copy(coefb[:], PA[:, 0:NH * 5]),
                     reads=[r_PA], writes=[r_coefb])
            rel_tail[0] = emit_rel_tail

            if f"main{bi}" not in PH:
                continue
            if "state" not in PH:
                k.op("pool", lambda g: g.memset(ssb[:], 0.0), writes=[r_ssb])
                def fn_srst(g):
                    return g.memset(
                        rhscat.rearrange("p (a b) -> p a b", b=325)[:, :, 0:65], 0.0)
                k.op("pool", fn_srst, writes=[r_rhs_S])

            def emit_main_loads(t, bi=bi):
                tb = t % 2
                t0 = t * TILE
                def fn_hsT(g, bi=bi, t0=t0, tb=tb):
                    return g.dma_start(
                        out=hsT_b[tb][:],
                        in_=bass.AP(hsT[bi].tensor, t0,
                                    [[SO, 128], [128 * SO, FC], [1, TILE]]))
                k.op("pool", fn_hsT, writes=[r_hsT[tb]], inc=16)
                def fn_sqTl(g, bi=bi, t0=t0, tb=tb):
                    return g.dma_start(out=sqT_b[tb][:], in_=sq_scr_ap(bi, t0))
                k.op("sp", fn_sqTl, reads=[r_sqscr], writes=[r_sqT[tb]], inc=16)

            emit_main_loads(0)
            for t in range(NT):
                tb = t % 2
                t0 = t * TILE
                if t + 1 < NT:
                    emit_main_loads(t + 1)

                for ci in range(CPT):
                    cbp = ci % 2
                    ch0 = t0 + ci * CH
                    def fn_hs(g, bi=bi, ch0=ch0, cbp=cbp):
                        return g.dma_start(
                            out=hs_b[cbp][:],
                            in_=bass.AP(hstok[bi].tensor, ch0 * H, [[H, 128], [1, H]]))
                    k.op("sp", fn_hs, writes=[r_hs[cbp]], inc=16)

                    for wofs in ((1, 2) if "kv" in MP else ()):
                        for half in range(NFO):
                            pw = half % 2
                            bank = PW if wofs == 1 else PN
                            rbank = r_PW[pw] if wofs == 1 else r_PN[pw]
                            for fic in range(FC):
                                def fn_mm(t_, wofs=wofs, half=half, fic=fic,
                                          tb=tb, ci=ci, pw=pw, bank=bank):
                                    return t_.matmul(
                                        bank[:, pw * 512:pw * 512 + FOW],
                                        hsT_b[tb][:, fic * TILE + ci * 128:
                                                  fic * TILE + ci * 128 + 128],
                                        W_AR[:, wofs * FC * H + fic * H + half * FOW:
                                             wofs * FC * H + fic * H + half * FOW + FOW],
                                        start=(fic == 0), stop=(fic == FC - 1))
                                k.op("pe", fn_mm, reads=[r_war, r_hsT[tb]],
                                     writes=[rbank])
                            if wofs == 1:
                                def fn_nrelu(s_, pw=pw):
                                    return s_.activation(
                                        elu_t1[:, 0:FOW],
                                        PW[:, pw * 512:pw * 512 + FOW],
                                        AF.Relu, scale=-1.0)
                                k.op("act", fn_nrelu, reads=[r_PW[pw]],
                                     writes=[r_elu1])
                                k.op("act",
                                     lambda s_: s_.activation(elu_t2[:, 0:FOW],
                                                              elu_t1[:, 0:FOW],
                                                              AF.Exp, scale=-1.0),
                                     reads=[r_elu1], writes=[r_elu2])
                                def fn_comb(v, pw=pw, half=half, cbp=cbp):
                                    return v.scalar_tensor_tensor(
                                        skt_b[cbp][:, half * FOW:half * FOW + FOW],
                                        PW[:, pw * 512:pw * 512 + FOW], 0.0,
                                        elu_t2[:, 0:FOW],
                                        ALU.max, ALU.add)
                                k.op("dve", fn_comb, reads=[r_PW[pw], r_elu2],
                                     writes=[r_skt[cbp]])
                            else:
                                def fn_ve(s_, pw=pw, half=half, cbp=cbp):
                                    nh_g = FOW // 64
                                    return s_.copy(
                                        vaug_b[cbp].rearrange(
                                            "p (h c) -> p h c", c=65)[
                                            :, half * nh_g:(half + 1) * nh_g, 0:64],
                                        PN[:, pw * 512:pw * 512 + FOW])
                                k.op("act", fn_ve, reads=[r_PN[pw]], writes=[r_vaug[cbp]])

                    for fic in (range(FC) if "skt" in MP else ()):
                        def fn_tp(t_, fic=fic, cbp=cbp):
                            return t_.transpose(PT[:, fic * 128:fic * 128 + 128],
                                                skt_b[cbp][:, fic * 128:fic * 128 + 128],
                                                identb[:])
                        k.op("pe", fn_tp, reads=[r_skt[cbp], r_consts], writes=[r_PT])
                    if "skt" in MP:
                        def fn_tpe(s_, cbp=cbp):
                            return s_.copy(skT_b[cbp][:, 0:H], PT[:, 0:H])
                        k.op("act", fn_tpe, reads=[r_PT], writes=[r_skT[cbp]])

                    emit_tail(0)

                    evens = [h for h in range(NH) if h % 2 == 0]
                    odds = [h for h in range(NH) if h % 2 == 1]
                    groups = []
                    for lst in (evens, odds):
                        for gi in range(0, len(lst), 4):
                            groups.append(lst[gi:gi + 4])
                    if "heads" not in MP:
                        groups = []
                    for hg, heads in enumerate(groups):
                        p2 = heads[0] % 2
                        pr = p2 * 64
                        gl = len(heads)
                        sup = heads[0] // 8
                        hat_list = list(enumerate(heads)) if "hat" in MP else []
                        if "hat1" in MP:
                            hat_list = hat_list[:1]
                        for j, h in hat_list:
                            fic_h = h // 2
                            def fn_at(t_, j=j, fic_h=fic_h, pr=pr, cbp=cbp,
                                      tb=tb, ci=ci):
                                return t_.matmul(
                                    PA[:, j * 128:j * 128 + 128],
                                    skT_b[cbp][pr:pr + 64,
                                               fic_h * 128:fic_h * 128 + 128],
                                    sqT_b[tb][pr:pr + 64,
                                              fic_h * TILE + ci * 128:
                                              fic_h * TILE + ci * 128 + 128],
                                    start=True, stop=True)
                            k.op("pe", fn_at, reads=[r_skT[cbp], r_sqT[tb]],
                                 writes=[r_PA])
                        if "hatm" in MP:
                            def fn_mask(v, hg=hg, gl=gl):
                                return v.tensor_tensor(am_b[hg % 2][:, 0:gl * 128],
                                                       PA[:, 0:gl * 128],
                                                       trib[:, 0:gl * 128], ALU.mult)
                            k.op("dve", fn_mask, reads=[r_PA, r_consts],
                                 writes=[r_am[hg % 2]])

                        for j, h in enumerate(heads):
                            hb = j % 2
                            fic_h = h // 2
                            if "hcm" in MP:
                                def fn_cm(t_, h=h, hb=hb, fic_h=fic_h, pr=pr,
                                          tb=tb, ci=ci):
                                    return t_.matmul(
                                        PN[:, hb * 512:hb * 512 + 325],
                                        sqT_b[tb][pr:pr + 64,
                                                  fic_h * TILE + ci * 128:
                                                  fic_h * TILE + ci * 128 + 128],
                                        rhscat[pr:pr + 64,
                                               (h // 2) * 325:(h // 2) * 325 + 325],
                                        start=True, stop=("hni" not in MP))
                                k.op("pe", fn_cm, reads=[r_sqT[tb], r_rhs_S, r_rhs_mem],
                                     writes=[r_PN[hb]])
                            if "hni" in MP:
                                def fn_ni(t_, h=h, j=j, hb=hb, hg=hg, cbp=cbp):
                                    return t_.matmul(
                                        PN[:, hb * 512:hb * 512 + 65],
                                        am_b[hg % 2][:, j * 128:j * 128 + 128],
                                        vaug_b[cbp][:, h * 65:h * 65 + 65],
                                        start=("hcm" not in MP), stop=True)
                                k.op("pe", fn_ni, reads=[r_am[hg % 2], r_vaug[cbp]],
                                     writes=[r_PN[hb]])
                            if "hcm" in MP or "hni" in MP:
                                def fn_ne(s_, h=h, hb=hb, cbp=cbp):
                                    return s_.copy(
                                        cb_b[cbp][:, h * 325:h * 325 + 325],
                                        PN[:, hb * 512:hb * 512 + 325])
                                k.op("act", fn_ne, reads=[r_PN[hb]],
                                     writes=[r_cb[cbp]])
                            if "hst" not in MP:
                                continue
                            def fn_st(t_, h=h, j=j, p2=p2, cbp=cbp, gl=gl):
                                kw = {}
                                bank = PS if p2 == 0 else PS2
                                if p2 == 1:
                                    kw["tile_position"] = (0, 64)
                                return t_.matmul(
                                    bank[p2 * 64:p2 * 64 + 64,
                                         j * 65:j * 65 + 65],
                                    skt_b[cbp][:, h * 64:h * 64 + 64],
                                    vaug_b[cbp][:, h * 65:h * 65 + 65],
                                    start=(j == 0), stop=(j == gl - 1), **kw)
                            k.op("pe", fn_st, reads=[r_skt[cbp], r_vaug[cbp]],
                                 writes=[r_PS if p2 == 0 else r_PS2])
                        if "hst" not in MP:
                            continue
                        def fn_sa(v, p2=p2, sup=sup, gl=gl):
                            bank = PS if p2 == 0 else PS2
                            return v.tensor_add(
                                ssb[p2 * 64:p2 * 64 + 64,
                                    sup * 260:sup * 260 + gl * 65],
                                ssb[p2 * 64:p2 * 64 + 64,
                                    sup * 260:sup * 260 + gl * 65],
                                bank[p2 * 64:p2 * 64 + 64, 0:gl * 65])
                        k.op("dve", fn_sa, reads=[r_PS if p2 == 0 else r_PS2],
                             writes=[r_ssb])
                    if "hst" not in MP:
                        continue
                    def fn_sc(s_):
                        return s_.copy(
                            rhscat.rearrange("p (a b) -> p a b", b=325)[:, :, 0:65],
                            ssb.rearrange("p (a b) -> p a b", b=65))
                    k.op("act", fn_sc, reads=[r_ssb], writes=[r_rhs_S])

                    if "epi" not in MP:
                        continue
                    if rel_tail[0] is not None:
                        rel_tail[0]()
                        rel_tail[0] = None
                    def fn_g1(v, cbp=cbp):
                        return v.tensor_copy(
                            sc_raw[:],
                            cb_b[cbp].rearrange("p (h j c) -> p h j c",
                                                j=5, c=65)[:, :, :, 64])
                    k.op("dve", fn_g1, reads=[r_cb[cbp]], writes=[r_scraw])
                    k.op("dve", lambda v: v.tensor_scalar_max(sc_clip[:], sc_raw[:],
                                                              EPS_ATT),
                         reads=[r_scraw], writes=[r_scclip])
                    k.op("dve", lambda v: v.reciprocal(sc_raw[:], sc_clip[:]),
                         reads=[r_scclip], writes=[r_scraw])
                    k.op("dve", lambda v: v.tensor_tensor(sc_scale[:], sc_raw[:],
                                                          coefb[:], ALU.mult),
                         reads=[r_scraw, r_coefb], writes=[r_scscale])
                    engs = ["dve", "dve", "pool", "pool", "dve"]
                    for j in range(5):
                        def fn_mj(x, j=j, cbp=cbp):
                            dst = ep_t[j][:] if j < 4 else outpre_b[:]
                            return x.tensor_tensor(
                                dst,
                                cb_b[cbp].rearrange("p (h j c) -> p h j c",
                                                    j=5, c=65)[:, :, j, 0:64],
                                bass.AP(sc_scale, j,
                                        [[NH * 5, 128], [5, NH], [0, 64]]),
                                ALU.mult)
                        tgt = r_ept[j] if j < 4 else r_outpre
                        k.op(engs[j], fn_mj, reads=[r_cb[cbp], r_scscale], writes=[tgt])
                    k.op("dve", lambda x: x.tensor_add(ep_t[0][:], ep_t[0][:],
                                                       ep_t[1][:]),
                         reads=[r_ept[1]], writes=[r_ept[0]])
                    k.op("pool", lambda x: x.tensor_add(ep_t[2][:], ep_t[2][:],
                                                        ep_t[3][:]),
                         reads=[r_ept[3]], writes=[r_ept[2]])
                    k.op("pool", lambda x: x.tensor_add(outpre_b[:], outpre_b[:],
                                                        ep_t[0][:]),
                         reads=[r_ept[0]], writes=[r_outpre])
                    k.op("pool", lambda x: x.tensor_add(outpre_b[:], outpre_b[:],
                                                        ep_t[2][:]),
                         reads=[r_ept[2]], writes=[r_outpre])

                    def tail1(bi=bi, t=t, tb=tb, ci=ci, cbp=cbp, ch0=ch0, t0=t0):
                        if "wo" not in MP:
                            return
                        for fic in range(FC):
                            def fn_tp(t_, fic=fic):
                                return t_.transpose(
                                    PT[:, fic * 128:fic * 128 + 128],
                                    outpre_b[:, fic * 128:fic * 128 + 128],
                                    identb[:])
                            k.op("pe", fn_tp, reads=[r_outpre, r_consts],
                                 writes=[r_PT])
                        k.op("dve", lambda v: v.tensor_copy(opT_b[:, 0:H],
                                                            PT[:, 0:H]),
                             reads=[r_PT], writes=[r_opT])
                        for half in range(NFO):
                            pwh = half % 2
                            for fic in range(FC):
                                def fn_wo(t_, half=half, fic=fic, pwh=pwh):
                                    return t_.matmul(
                                        PW[:, pwh * 512:pwh * 512 + FOW],
                                        opT_b[:, fic * 128:fic * 128 + 128],
                                        W_AR[:, 3 * FC * H + fic * H + half * FOW:
                                             3 * FC * H + fic * H + half * FOW + FOW],
                                        start=(fic == 0), stop=(fic == FC - 1))
                                k.op("pe", fn_wo, reads=[r_opT, r_war],
                                     writes=[r_PW[pwh]])
                        if "ln" not in MP:
                            return
                        for half in range(NFO):
                            pwh = half % 2
                            def fn_res(v, half=half, cbp=cbp, pwh=pwh):
                                return v.scalar_tensor_tensor(
                                    x1r_b[:, half * FOW:half * FOW + FOW],
                                    PW[:, pwh * 512:pwh * 512 + FOW], 1.0,
                                    hs_b[cbp][:, half * FOW:half * FOW + FOW],
                                    ALU.mult, ALU.add,
                                    accum_out=lncol[:, half:half + 1])
                            k.op("dve", fn_res, reads=[r_PW[pwh], r_hs[cbp]],
                                 writes=[r_x1r, r_lncol])
                        if NFO == 2:
                            k.op("dve", lambda v: v.tensor_add(lncol[:, 2:3],
                                                               lncol[:, 0:1],
                                                               lncol[:, 1:2]),
                                 reads=[r_lncol], writes=[r_lncol])
                        else:
                            k.op("dve", lambda v: v.tensor_copy(lncol[:, 2:3],
                                                                lncol[:, 0:1]),
                                 reads=[r_lncol], writes=[r_lncol])
                        def fn_xsq(v):
                            return v.scalar_tensor_tensor(xsq_scr[:], x1r_b[:], 1.0,
                                                          x1r_b[:], ALU.mult,
                                                          ALU.mult,
                                                          accum_out=lncol[:, 3:4])
                        k.op("dve", fn_xsq, reads=[r_x1r], writes=[r_xsq, r_lncol])
                        k.op("dve", lambda v: v.tensor_scalar(lncol[:, 4:5],
                                                              lncol[:, 2:3],
                                                              -1.0 / H, None,
                                                              ALU.mult),
                             reads=[r_lncol], writes=[r_lncol])
                        k.op("dve", lambda v: v.tensor_scalar(lncol[:, 5:6],
                                                              lncol[:, 3:4],
                                                              1.0 / H, None,
                                                              ALU.mult),
                             reads=[r_lncol], writes=[r_lncol])
                        k.op("dve", lambda v: v.tensor_scalar(lncol[:, 6:7],
                                                              lncol[:, 4:5],
                                                              lncol[:, 4:5], None,
                                                              ALU.mult),
                             reads=[r_lncol], writes=[r_lncol])
                        k.op("dve", lambda v: v.tensor_sub(lncol[:, 7:8],
                                                           lncol[:, 5:6],
                                                           lncol[:, 6:7]),
                             reads=[r_lncol], writes=[r_lncol])
                        k.op("act", lambda s_: s_.activation(lncol[:, 6:7],
                                                             lncol[:, 7:8],
                                                             AF.Ln,
                                                             bias=epsb[:, 0:1]),
                             reads=[r_lncol], writes=[r_lncol])
                        k.op("act", lambda s_: s_.activation(lncol[:, 7:8],
                                                             lncol[:, 6:7],
                                                             AF.Exp, scale=-0.5),
                             reads=[r_lncol], writes=[r_lncol])
                        def fn_x1(v):
                            return v.tensor_scalar(x1c_b[:], x1r_b[:], lncol[:, 4:5],
                                                   lncol[:, 7:8], ALU.add, ALU.mult)
                        k.op("dve", fn_x1, reads=[r_x1r, r_lncol], writes=[r_x1c])
                        if dbg:
                            def fn_dx1f(v):
                                return v.tensor_scalar(dbg_xout[:], x1r_b[:],
                                                       lncol[:, 4:5],
                                                       lncol[:, 7:8], ALU.add,
                                                       ALU.mult)
                            k.op("dve", fn_dx1f, reads=[r_x1r, r_lncol],
                                 writes=[r_xout])
                            def fn_dx1(g, bi=bi, ch0=ch0):
                                return g.dma_start(
                                    out=bass.AP(dbg_outs["dbg_x1"].tensor,
                                                (bi * S + ch0) * H,
                                                [[H, 128], [1, H]]),
                                    in_=dbg_xout[:])
                            k.op("sp", fn_dx1, reads=[r_xout], inc=16)

                    def tail2(bi=bi, t=t, tb=tb, ci=ci, cbp=cbp, ch0=ch0, t0=t0):
                        if "wo" not in MP or "ln" not in MP or "x1t" not in MP:
                            return
                        for fic in range(FC):
                            def fn_tp(t_, fic=fic):
                                return t_.transpose(
                                    PT[:, fic * 128:fic * 128 + 128],
                                    x1c_b[:, fic * 128:fic * 128 + 128],
                                    identb[:])
                            k.op("pe", fn_tp, reads=[r_x1c, r_consts],
                                 writes=[r_PT])
                        def fn_x1te(s_, tb=tb, ci=ci):
                            return s_.copy(
                                x1t_tile[tb].rearrange("p (f t) -> p f t", t=TILE)[
                                    :, :, ci * 128:ci * 128 + 128],
                                PT[:, 0:FC * 128].rearrange("p (f t) -> p f t",
                                                            t=128))
                        k.op("act", fn_x1te, reads=[r_PT], writes=[r_x1tt[tb]])

                        if "rout" in MP:
                            def fn_lgb(t_):
                                return t_.matmul(PW[:, 0:E], ones_row[0:1, :],
                                                 lbb_bf[:],
                                                 start=True, stop=False)
                            k.op("pe", fn_lgb, reads=[r_consts, r_small],
                                 writes=[r_PW[0]])
                            for fic in range(FC):
                                def fn_lg(t_, fic=fic, tb=tb, ci=ci):
                                    return t_.matmul(
                                        PW[:, 0:E],
                                        x1t_tile[tb][:, fic * TILE + ci * 128:
                                                     fic * TILE + ci * 128 + 128],
                                        gwt_sb[:, fic * E:fic * E + E],
                                        start=False, stop=(fic == FC - 1))
                                k.op("pe", fn_lg, reads=[r_x1tt[tb], r_consts],
                                     writes=[r_PW[0]])
                            k.op("act", lambda s_: s_.copy(sco_p[:], PW[:, 0:E]),
                                 reads=[r_PW[0]], writes=[r_scop])
                            k.op("act", lambda s_: s_.activation(sco_e[:], sco_p[:],
                                                                 AF.Exp,
                                                                 scale=-1.0),
                                 reads=[r_scop], writes=[r_scoe])
                            k.op("dve", lambda v: v.tensor_scalar_add(sco_e[:],
                                                                      sco_e[:], 1.0),
                                 reads=[r_scoe], writes=[r_scoe])
                            k.op("dve", lambda v: v.reciprocal(scores[:], sco_e[:]),
                                 reads=[r_scoe], writes=[r_scores])
                            k.op("dve", lambda v: v.reduce_max(
                                rmx[:, 0:1], scores[:], axis=mybir.AxisListType.X),
                                 reads=[r_scores], writes=[r_rmx])
                            k.op("dve", lambda v: v.tensor_scalar(
                                ismax[:], scores[:], rmx[:, 0:1], None, ALU.is_ge),
                                 reads=[r_scores, r_rmx], writes=[r_ismax])
                            def fn_smk(v):
                                return v.scalar_tensor_tensor(smask[:], ismax[:],
                                                              -1e30, scores[:],
                                                              ALU.mult, ALU.add)
                            k.op("dve", fn_smk, reads=[r_ismax, r_scores],
                                 writes=[r_smask])
                            k.op("dve", lambda v: v.reduce_max(
                                rmx[:, 1:2], smask[:], axis=mybir.AxisListType.X),
                                 reads=[r_smask], writes=[r_rmx])
                            k.op("dve", lambda v: v.tensor_scalar(
                                self32[:], scores[:], rmx[:, 1:2], None, ALU.is_ge),
                                 reads=[r_scores, r_rmx], writes=[r_self32])
                            k.op("dve", lambda v: v.tensor_add(rmx[:, 2:3],
                                                               rmx[:, 0:1],
                                                               rmx[:, 1:2]),
                                 reads=[r_rmx], writes=[r_rmx])
                            k.op("dve", lambda v: v.tensor_scalar_add(rmx[:, 2:3],
                                                                      rmx[:, 2:3],
                                                                      EPS_TOP),
                                 reads=[r_rmx], writes=[r_rmx])
                            k.op("dve", lambda v: v.reciprocal(rmx[:, 3:4],
                                                               rmx[:, 2:3]),
                                 reads=[r_rmx], writes=[r_rmx])
                            k.op("dve", lambda v: v.tensor_tensor(smask[:],
                                                                  scores[:],
                                                                  self32[:],
                                                                  ALU.mult),
                                 reads=[r_scores, r_self32], writes=[r_smask])
                            k.op("dve", lambda v: v.tensor_scalar(
                                wsel[:, 0:E], smask[:], rmx[:, 3:4], None,
                                ALU.mult),
                                 reads=[r_smask, r_rmx], writes=[r_wsel])
                            k.op("dve", lambda v: v.tensor_copy(wsel[:, E:2 * E],
                                                                self32[:]),
                                 reads=[r_self32], writes=[r_wsel])
                            def fn_selc(v, tb=tb, ci=ci):
                                return v.tensor_copy(
                                    sel_b[tb][:, ci * E:ci * E + E], self32[:])
                            k.op("dve", fn_selc, reads=[r_self32],
                                 writes=[r_sel_b[tb]])
                            k.op("dve", lambda v: v.tensor_add(wselacc[:],
                                                               wselacc[:],
                                                               wsel[:]),
                                 reads=[r_wsel], writes=[r_wselacc])

                        if ci == CPT - 1:
                            def fn_x1s(g, bi=bi, t0=t0, tb=tb):
                                return g.dma_start(out=x1_scr_ap(bi, t0),
                                                   in_=x1t_tile[tb][:])
                            k.op("sp", fn_x1s, reads=[r_x1tt[tb]],
                                 writes=[r_x1scr_t0 if t0 == 0 else
                                         r_x1scr_h[1 if t0 >= OH else 0]],
                                 inc=16)
                            def fn_sels(g, bi=bi, t0=t0, tb=tb):
                                return g.dma_start(
                                    out=bass.AP(sel_scr_t, (bi * S + t0) * E,
                                                [[E, 128], [128 * E, CPT], [1, E]]),
                                    in_=bass.AP(sel_b[tb], 0,
                                                [[CPT * E, 128], [E, CPT], [1, E]]))
                            k.op("sp", fn_sels, reads=[r_sel_b[tb]],
                                 writes=[r_selscr], inc=16)
                    emit_tail(1)
                    pending_tail[0] = tail1
                    pending_tail[1] = tail2
            emit_tail(0)
            emit_tail(1)
            if "rout" in MP:
                def fn_stm(t_):
                    return t_.matmul(PW[0:16, 1023:1024], wselacc[:],
                                     ones_col_f[:], start=True, stop=True)
                k.op("pe", fn_stm, reads=[r_wselacc, r_consts],
                     writes=[r_PW[1]])
                k.op("act", lambda s_: s_.copy(statacc[:],
                                               PW[0:16, 1023:1024]),
                     reads=[r_PW[1]], writes=[r_statacc])

        if rel_tail[0] is not None:
            rel_tail[0]()
            rel_tail[0] = None

        def mb_op(*a, **kw):
            if "mbox" in PH:
                return k.op(*a, **kw)
            return None
        def fn_statcc_a(g, sem, psem):
            g.dma_start(bass.AP(cstat_in_t, 0, [[16, 1], [1, 1], [1, 16]]),
                        bass.AP(statacc, 0, [[1, 16], [1, 1]])).then_inc(sem, 16)
            g.wait_ge(sem, 16)
            g.collective_compute(
                "AllReduce", ALU.add,
                replica_groups=[list(range(8))],
                ins=[cstat_in_t.ap().opt()], outs=[cstat_out_t.ap().opt()],
            ).then_inc(sem)
        def fn_statcc_b(g, sem, psem):
            g.dma_start(out=mrow[:, 0:16], in_=cstat_out_t[:, :]).then_inc(sem, 16)
        def emit_stats_cc():
            if "mbox" in PH:
                k.raw("pool", fn_statcc_a, reads=[r_statacc, r_cstat],
                      writes=[r_cstat], incs=17)
        def emit_stats_tail():
            if "mbox" not in PH:
                k.op("pool", lambda g: g.memset(wmeanb[:], 0.5),
                     writes=[r_wmeanb])
                return
            k.raw("pool", fn_statcc_b, reads=[r_cstat],
                  writes=[r_mrow], incs=16)
            mb_op("dve", lambda v: v.tensor_scalar_max(mrow[:, 24:24 + E],
                                                      mrow[:, E:2 * E], 1.0),
                 reads=[r_mrow], writes=[r_mrow])
            mb_op("dve", lambda v: v.reciprocal(mrow[:, 32:32 + E],
                                               mrow[:, 24:24 + E]),
                 reads=[r_mrow], writes=[r_mrow])
            mb_op("dve", lambda v: v.tensor_tensor(wm_row[:], mrow[:, 0:E],
                                                  mrow[:, 32:32 + E], ALU.mult),
                 reads=[r_mrow], writes=[r_wmrow])
            mb_op("dve", lambda v: v.tensor_copy(wm_bf[:], wm_row[:]),
                 reads=[r_wmrow], writes=[r_wmrow])
            mb_op("pe", lambda t_: t_.matmul(PA[:, 0:E], ones_row[:], wm_bf[:],
                                            start=True, stop=True),
                 reads=[r_consts, r_wmrow], writes=[r_PA])
            mb_op("act", lambda s_: s_.copy(wmeanb[:], PA[:, 0:E]),
                 reads=[r_PA], writes=[r_wmeanb])
        if "c" not in PH:
            emit_stats_cc()
            emit_stats_tail()

        att_alias = ([r_hsT[0], r_hsT[1], r_sqT[0], r_sqT[1], r_x1tt[0],
                      r_x1tt[1], r_cb[0], r_cb[1], r_rhs_S, r_rhs_mem,
                      r_skt[0], r_skt[1], r_skT[0], r_skT[1], r_vaug[0],
                      r_vaug[1], r_am[0], r_am[1], r_outpre, r_opT, r_x1c,
                      r_xsq, r_x1r, r_hs[0], r_hs[1], r_elu1, r_elu2, r_ssb]
                     + r_ept)
        if "c" in PH:
            k.op("pool", lambda g: g.engine_nop(),
                 reads=att_alias,
                 writes=[r_exp[0], r_exp[1], r_acc, r_hb[0], r_silu,
                         r_selown, r_xout, r_x2r, r_x1tok, r_war])
            emit_stats_cc()
            def fn_clr(base, ln):
                def fn(g, sem, psem, base=base, ln=ln):
                    for fic in range(FC):
                        g.dma_start(
                            bass.AP(W_AR, fic * OWN + base,
                                    [[4 * FC * H, 128], [1, ln]]),
                            bass.AP(x1t_scr_t, fic * 128 * NBS + base,
                                    [[NBS, 128], [1, ln]])).then_inc(sem, 16)
                return fn
            k.raw("pool", fn_clr(0, CT), reads=[r_x1scr_t0, r_war],
                  writes=[r_war_t0], incs=FC * 16)
            k.raw("sp", fn_clr(CT, OH - CT), reads=[r_x1scr_h[0], r_war],
                  writes=[r_war_h[0]], incs=FC * 16)
            k.raw("sp", fn_clr(OH, OH), reads=[r_x1scr_h[1], r_war],
                  writes=[r_war_h[1]], incs=FC * 16)
            def fn_cls(g):
                return g.dma_start(
                    sel_own.rearrange("p (a b) -> p a b", b=E),
                    bass.AP(sel_scr_t, 0, [[E, 128], [128 * E, NOC], [1, E]]))
            k.op("pool", fn_cls, reads=[r_selscr], writes=[r_selown], inc=16)

        accv = accv_full
        acc_cols = (OH // 128) * H

        def emit_ln2(half, tc):
            oc_own = half * (OH // 128) + tc
            col0 = tc * H
            rwl = (r_war_t0 if (half == 0 and tc < CT // 128)
                   else r_war_h[half])
            for fic in range(FC):
                def fn_tp(t_, fic=fic, oc_own=oc_own):
                    return t_.transpose(
                        PT[:, fic * 128:fic * 128 + 128],
                        W_AR[:, fic * OWN + oc_own * 128:
                             fic * OWN + oc_own * 128 + 128],
                        identb[:])
                k.op("pe", fn_tp, reads=[rwl, r_consts],
                     writes=[r_PT])
            k.op("act", lambda s_: s_.copy(x1tok_b[:, 0:H], PT[:, 0:H]),
                 reads=[r_PT], writes=[r_x1tok])
            for hh in range(NFO):
                def fn_r(v, hh=hh, col0=col0):
                    return v.scalar_tensor_tensor(
                        x2r_b[:, hh * FOW:hh * FOW + FOW],
                        x1tok_b[:, hh * FOW:hh * FOW + FOW], 1.0,
                        accv[:, col0 + hh * FOW:col0 + hh * FOW + FOW],
                        ALU.mult, ALU.add,
                        accum_out=lncol[:, hh:hh + 1])
                k.op("dve", fn_r, reads=[r_x1tok, r_acc],
                     writes=[r_x2r, r_lncol])
            if NFO == 2:
                k.op("dve", lambda v: v.tensor_add(lncol[:, 2:3], lncol[:, 0:1],
                                                   lncol[:, 1:2]),
                     reads=[r_lncol], writes=[r_lncol])
            else:
                k.op("dve", lambda v: v.tensor_copy(lncol[:, 2:3], lncol[:, 0:1]),
                     reads=[r_lncol], writes=[r_lncol])
            def fn_xsq(v, col0=col0):
                return v.scalar_tensor_tensor(accv[:, col0:col0 + H],
                                              x2r_b[:], 1.0,
                                              x2r_b[:], ALU.mult, ALU.mult,
                                              accum_out=lncol[:, 3:4])
            k.op("dve", fn_xsq, reads=[r_x2r], writes=[r_acc, r_lncol])
            k.op("dve", lambda v: v.tensor_scalar(lncol[:, 4:5], lncol[:, 2:3],
                                                  -1.0 / H, None, ALU.mult),
                 reads=[r_lncol], writes=[r_lncol])
            k.op("dve", lambda v: v.tensor_scalar(lncol[:, 5:6], lncol[:, 3:4],
                                                  1.0 / H, None, ALU.mult),
                 reads=[r_lncol], writes=[r_lncol])
            k.op("dve", lambda v: v.tensor_scalar(lncol[:, 6:7], lncol[:, 4:5],
                                                  lncol[:, 4:5], None, ALU.mult),
                 reads=[r_lncol], writes=[r_lncol])
            k.op("dve", lambda v: v.tensor_sub(lncol[:, 7:8], lncol[:, 5:6],
                                               lncol[:, 6:7]),
                 reads=[r_lncol], writes=[r_lncol])
            k.op("act", lambda s_: s_.activation(lncol[:, 6:7], lncol[:, 7:8],
                                                 AF.Ln, bias=epsb[:, 0:1]),
                 reads=[r_lncol], writes=[r_lncol])
            k.op("act", lambda s_: s_.activation(lncol[:, 7:8], lncol[:, 6:7],
                                                 AF.Exp, scale=-0.5),
                 reads=[r_lncol], writes=[r_lncol])
            def fn_outop(v):
                return v.tensor_scalar(xout[:], x2r_b[:], lncol[:, 4:5],
                                       lncol[:, 7:8], ALU.add, ALU.mult)
            k.op("dve", fn_outop, reads=[r_x2r, r_lncol], writes=[r_xout])
            def fn_os(g, half=half, tc=tc):
                row0 = half * OH + tc * 128
                return g.dma_start(
                    out=bass.AP(out.tensor, row0 * H, [[H, 128], [1, H]]),
                    in_=xout[:])
            k.op("sp", fn_os, reads=[r_xout], inc=16)
        if "c" not in PH:
            for tc in range(OWN // 128):
                def fn_os0(g, tc=tc):
                    return g.dma_start(
                        out=bass.AP(out.tensor, tc * 128 * H, [[H, 128], [1, H]]),
                        in_=xout[:])
                k.op("sp", fn_os0, reads=[r_xout], inc=16)
        for half in range(2):
            if "c" not in PH:
                break
            k.op("pool", lambda g: g.memset(accv[:, 0:acc_cols], 0.0),
                 writes=[r_acc])

            experts = (list(range(E)) + ["sh"]) if half == 0 \
                else (["sh"] + list(range(E)))
            for eidx, e in enumerate(experts):
                if e == "sh":
                    n_ic = SIC
                    II = SI
                    sofs = 0
                    u_ofs, d_ofs = FC * SI, 2 * FC * SI
                    wrs = [r_exp[0], r_exp[1]]
                    wrs_g = [r_exp[0]]
                    wrs_d = [r_exp[1]]
                    def fn_sh_g(g):
                        return g.dma_start(
                            out=EXP_AR[:, 0:FC * SI].rearrange(
                                "p (f i) -> p f i", i=SI),
                            in_=bass.AP(sgT.tensor, 0,
                                        [[SI, 128], [128 * SI, FC], [1, SI]]))
                    def fn_sh_u1(g):
                        return g.dma_start(
                            out=EXP_AR[:, FC * SI:FC * SI + 4 * SI].rearrange(
                                "p (f i) -> p f i", i=SI),
                            in_=bass.AP(suT.tensor, 0,
                                        [[SI, 128], [128 * SI, 4], [1, SI]]))
                    def fn_sh_u2(g):
                        return g.dma_start(
                            out=EXP_AR[:, FC * SI + 4 * SI:2 * FC * SI].rearrange(
                                "p (f i) -> p f i", i=SI),
                            in_=bass.AP(suT.tensor, 4 * 128 * SI,
                                        [[SI, 128], [128 * SI, 4], [1, SI]]))
                    def fn_sh_d(g):
                        return g.dma_start(
                            out=EXP_AR[:, 2 * FC * SI:2 * FC * SI + SIC * H].rearrange(
                                "p (f i) -> p f i", i=H),
                            in_=bass.AP(sdT.tensor, 0,
                                        [[H, 128], [128 * H, SIC], [1, H]]))
                else:
                    n_ic = EIC
                    II = EI
                    sofs = (eidx % 2) * EXPCOLS
                    u_ofs, d_ofs = FC * EI, 2 * FC * EI
                    wrs = [r_exp[eidx % 2]]
                    wrs_g = wrs
                    wrs_d = wrs
                    def fn_wl1(g, e=e, sofs=sofs):
                        return g.dma_start(
                            out=EXP_AR[:, sofs:sofs + FC * EI].rearrange(
                                "p (f i) -> p f i", i=EI),
                            in_=bass.AP(egT.tensor, e * H * EI,
                                        [[EI, 128], [128 * EI, FC], [1, EI]]))
                    def fn_wl2(g, e=e, sofs=sofs):
                        return g.dma_start(
                            out=EXP_AR[:, sofs + FC * EI:sofs + 2 * FC * EI].rearrange(
                                "p (f i) -> p f i", i=EI),
                            in_=bass.AP(euT.tensor, e * H * EI,
                                        [[EI, 128], [128 * EI, FC], [1, EI]]))
                    def fn_wl3(g, e=e, sofs=sofs):
                        return g.dma_start(
                            out=EXP_AR[:, sofs + 2 * FC * EI:
                                       sofs + 2 * FC * EI + EIC * H].rearrange(
                                "p (f i) -> p f i", i=H),
                            in_=bass.AP(edT.tensor, e * EI * H,
                                        [[H, 128], [128 * H, EIC], [1, H]]))
                if e == "sh":
                    if half == 0:
                        def fn_sh_a(g, sem, psem, f1=fn_sh_g, f2=fn_sh_u1):
                            f1(g).then_inc(sem, 16)
                            f2(g).then_inc(sem, 16)
                        def fn_sh_b(g, sem, psem, f1=fn_sh_u2, f2=fn_sh_d):
                            f1(g).then_inc(sem, 16)
                            f2(g).then_inc(sem, 16)
                        k.raw("pool", fn_sh_a, writes=[r_exp[0]], incs=32)
                        k.raw("pool", fn_sh_b, writes=[r_exp[1]], incs=32)
                else:
                    def fn_wl123(g, sem, psem, f1=fn_wl1, f2=fn_wl2, f3=fn_wl3):
                        f1(g).then_inc(sem, 16)
                        f2(g).then_inc(sem, 16)
                        f3(g).then_inc(sem, 16)
                    k.raw("pool", fn_wl123, writes=wrs, incs=48)

                for t in range(TPH):
                    th0 = half * OH + t * CT
                    hbp = t % 2
                    r_warx = (r_war_t0 if (half == 0 and t == 0)
                              else r_war_h[half])
                    for ioc in range(n_ic):
                        pn = ioc % 2
                        for fic in range(FC):
                            def fn_gm(t_, fic=fic, ioc=ioc, pn=pn, sofs=sofs,
                                      th0=th0, II=II):
                                return t_.matmul(
                                    PN[:, pn * 512:pn * 512 + CT],
                                    EXP_AR[:, sofs + fic * II + ioc * 128:
                                           sofs + fic * II + ioc * 128 + 128],
                                    W_AR[:, fic * OWN + th0:fic * OWN + th0 + CT],
                                    start=(fic == 0), stop=(fic == FC - 1))
                            k.op("pe", fn_gm, reads=[r_warx] + wrs_g,
                                 writes=[r_PN[pn]])
                        def fn_silu(s_, pn=pn):
                            return s_.activation(silu_b[:, 0:CT],
                                                 PN[:, pn * 512:pn * 512 + CT],
                                                 AF.Silu)
                        k.op("act", fn_silu, reads=[r_PN[pn]], writes=[r_silu])
                        for fic in range(FC):
                            def fn_um(t_, fic=fic, ioc=ioc, pn=pn, sofs=sofs,
                                      th0=th0, II=II, u_ofs=u_ofs):
                                return t_.matmul(
                                    PW[:, pn * 512:pn * 512 + CT],
                                    EXP_AR[:, sofs + u_ofs + fic * II + ioc * 128:
                                           sofs + u_ofs + fic * II + ioc * 128 + 128],
                                    W_AR[:, fic * OWN + th0:fic * OWN + th0 + CT],
                                    start=(fic == 0), stop=(fic == FC - 1))
                            wu = (wrs if e != "sh"
                                  else ([r_exp[0]] if fic < 4 else [r_exp[1]]))
                            k.op("pe", fn_um, reads=[r_warx] + wu,
                                 writes=[r_PW[pn]])
                        def fn_h(v, pn=pn, ioc=ioc, hbp=hbp):
                            return v.tensor_tensor(
                                h_b[hbp][:, ioc * CT:ioc * CT + CT],
                                silu_b[:, 0:CT], PW[:, pn * 512:pn * 512 + CT],
                                ALU.mult)
                        k.op("dve", fn_h, reads=[r_silu, r_PW[pn]], writes=[r_hb[hbp]])
                    if half == 0 and eidx == 0 and t == 0:
                        emit_stats_tail()
                    for tc in range(CTC):
                        oc = (half * OH + t * CT + tc * 128) // 128
                        if e != "sh":
                            def fn_wt(v, oc=oc, e=e):
                                return v.tensor_tensor(
                                    wtil[:],
                                    sel_own[:, oc * E + e:oc * E + e + 1],
                                    wmeanb[:, e:e + 1], ALU.mult)
                            k.op("dve", fn_wt, reads=[r_selown, r_wmeanb],
                                 writes=[r_wtil])
                        for foh in range(NFO):
                            pw = foh % 2
                            for ioc in range(n_ic):
                                def fn_dm(t_, ioc=ioc, tc=tc, foh=foh, pw=pw,
                                          sofs=sofs, d_ofs=d_ofs, hbp=hbp, n_ic=n_ic):
                                    return t_.matmul(
                                        PW[:, pw * 512:pw * 512 + FOW],
                                        h_b[hbp][:, ioc * CT + tc * 128:
                                                 ioc * CT + tc * 128 + 128],
                                        EXP_AR[:, sofs + d_ofs + ioc * H + foh * FOW:
                                               sofs + d_ofs + ioc * H + foh * FOW + FOW],
                                        start=(ioc == 0), stop=(ioc == n_ic - 1))
                                k.op("pe", fn_dm, reads=[r_hb[hbp]] + wrs_d,
                                     writes=[r_PW[pw]])
                            col0 = ((t * CT + tc * 128) // 128) * H + foh * FOW
                            if e == "sh":
                                def fn_acc(v, pw=pw, col0=col0):
                                    return v.tensor_add(
                                        accv[:, col0:col0 + FOW],
                                        accv[:, col0:col0 + FOW],
                                        PW[:, pw * 512:pw * 512 + FOW])
                                k.op("dve", fn_acc, reads=[r_PW[pw]], writes=[r_acc])
                            else:
                                def fn_acc(v, pw=pw, col0=col0):
                                    return v.scalar_tensor_tensor(
                                        accv[:, col0:col0 + FOW],
                                        PW[:, pw * 512:pw * 512 + FOW], wtil[:],
                                        accv[:, col0:col0 + FOW],
                                        ALU.mult, ALU.add)
                                k.op("dve", fn_acc, reads=[r_PW[pw], r_wtil],
                                     writes=[r_acc])
                        if eidx == len(experts) - 1:
                            emit_ln2(half, t * CTC + tc)

        emit_all(nc, plan, es)
    return nc, c, dbg_outs



def prep_inputs(cin, hs_full, w_q, w_k, w_v, w_o, gate, memories, memory_norms,
                gate_w, lb_bias, egw, euw, edw, sgw, suw, sdw):
    c = derive(cin)
    S, NH, M, E, H = c["S"], c["NH"], c["M"], c["E"], c["H"]

    wqT = np.ascontiguousarray(w_q.T)
    wkT = np.ascontiguousarray(w_k.T)
    wvT = np.ascontiguousarray(w_v.T)
    woT = np.ascontiguousarray(w_o.T)
    memcat = np.zeros((NH, 64, M * 65), np.float32)
    for m in range(M):
        memcat[:, :, 65 * m:65 * m + 64] = memories[m]
        memcat[:, :, 65 * m + 64] = memory_norms[m]
    wmn = np.zeros((H, NH * M), np.float32)
    for h in range(NH):
        for m in range(M):
            wmn[64 * h:64 * h + 64, h * M + m] = memory_norms[m, h]
    gate_in = gate.reshape(NH, 1).astype(np.float32)
    gwT = np.ascontiguousarray(gate_w.T)
    lbbv = lb_bias.reshape(1, E).astype(np.float32)
    egTv = np.ascontiguousarray(egw.transpose(0, 2, 1))
    euTv = np.ascontiguousarray(euw.transpose(0, 2, 1))
    edTv = np.ascontiguousarray(edw.transpose(0, 2, 1))
    sgTv = np.ascontiguousarray(sgw.T)
    suTv = np.ascontiguousarray(suw.T)
    sdTv = np.ascontiguousarray(sdw.T)

    SO = c["SO"]
    n_batch = hs_full.shape[0]
    hsT_first = []
    hsT_second = []
    for b in range(n_batch):
        hsTb = hs_full[b].T
        hsT_first.append(np.ascontiguousarray(hsTb[:, 0:SO]))
        hsT_second.append(np.ascontiguousarray(hsTb[:, SO:2 * SO]))

    in_maps = []
    placements = []
    for cid in range(8):
        b = cid // 2
        half = cid % 2
        m = {
            "hsT_own": hsT_second[b] if half else hsT_first[b],
            "hsT_state": hsT_first[b],
            "hs_own": np.ascontiguousarray(hs_full[b, half * SO:(half + 1) * SO]),
            "wqT": wqT, "wkT": wkT, "wvT": wvT, "woT": woT,
            "memcat": memcat, "wmn": wmn, "gate_in": gate_in,
            "gwT": gwT, "lbb": lbbv,
            "egT": egTv, "euT": euTv, "edT": edTv,
            "sgT": sgTv, "suT": suTv, "sdT": sdTv,
            "cfgv": np.array([[half, 0, 0, 0, 0, 0, 0, 0]], np.int32),
        }
        in_maps.append(m)
        placements.append((b, half * SO))
    return in_maps, placements


_BUILD_CACHE = {}


def kernel(L0_attn_hidden_states, L0_attn_w_q, L0_attn_w_k, L0_attn_w_v,
           L0_attn_w_o, L0_attn_gate, L0_attn_memories, L0_attn_memory_norms,
           L0_ffn_gate_w, L0_ffn_lb_bias, L0_ffn_expert_gate_w,
           L0_ffn_expert_up_w, L0_ffn_expert_down_w, L0_ffn_shared_gate_w,
           L0_ffn_shared_up_w, L0_ffn_shared_down_w):
    cfg = real_cfg()
    c = derive(cfg)
    hs = np.asarray(L0_attn_hidden_states, np.float32)
    in_maps, placements = prep_inputs(
        cfg, hs,
        np.asarray(L0_attn_w_q, np.float32), np.asarray(L0_attn_w_k, np.float32),
        np.asarray(L0_attn_w_v, np.float32), np.asarray(L0_attn_w_o, np.float32),
        np.asarray(L0_attn_gate, np.float32),
        np.asarray(L0_attn_memories, np.float32),
        np.asarray(L0_attn_memory_norms, np.float32),
        np.asarray(L0_ffn_gate_w, np.float32),
        np.asarray(L0_ffn_lb_bias, np.float32),
        np.asarray(L0_ffn_expert_gate_w, np.float32),
        np.asarray(L0_ffn_expert_up_w, np.float32),
        np.asarray(L0_ffn_expert_down_w, np.float32),
        np.asarray(L0_ffn_shared_gate_w, np.float32),
        np.asarray(L0_ffn_shared_up_w, np.float32),
        np.asarray(L0_ffn_shared_down_w, np.float32))

    if "real" not in _BUILD_CACHE:
        _BUILD_CACHE["real"] = build(cfg, dbg=False)
    nc, _, _ = _BUILD_CACHE["real"]

    res = run_bass_kernel_spmd(nc, in_maps, core_ids=list(range(8)))

    outv = np.zeros((4, c["S"], c["H"]), np.float32)
    for cid in range(8):
        own_b, seg_in = placements[cid]
        outv[own_b, seg_in:seg_in + c["OWN"]] = res.results[cid]["out"]
    return outv


# version 19
# speedup vs baseline: 1.1170x; 1.0029x over previous
import numpy as np
import ml_dtypes
from contextlib import ExitStack

import concourse.bass as bass
import concourse.mybir as mybir
from concourse.bass_utils import run_bass_kernel_spmd

F32 = mybir.dt.float32
BF16 = mybir.dt.bfloat16
I32 = mybir.dt.int32
ALU = mybir.AluOpType
AF = mybir.ActivationFunctionType

EPS_ATT = 1e-6
EPS_LN = 1e-5
EPS_TOP = 1e-8

ENGINES = ("pe", "dve", "act", "pool", "sp")


def real_cfg():
    return dict(S=8192, NH=16, M=4, E=8, EI=512)


def small_cfg():
    return dict(S=512, NH=4, M=4, E=8, EI=128)


def derive(cin):
    c = dict(cin)
    c["D"] = 64
    c["H"] = c["NH"] * 64
    c["FC"] = c["H"] // 128
    c["CH"] = 128
    c["SO"] = c["S"] // 2
    c["TILE"] = min(512, c["SO"])
    c["CPT"] = c["TILE"] // c["CH"]
    c["NT"] = c["SO"] // c["TILE"]
    c["NB"] = 1
    c["NBS"] = c["SO"]
    c["OWN"] = c["SO"]
    c["OH"] = c["OWN"] // 2
    c["CT"] = min(512, c["OH"])
    c["TPH"] = c["OH"] // c["CT"]
    c["FOW"] = min(512, c["H"])
    c["NFO"] = c["H"] // c["FOW"]
    c["SI"] = 2 * c["EI"]
    c["EIC"] = c["EI"] // 128
    c["SIC"] = c["SI"] // 128
    return c



class Plan:
    def __init__(self):
        self.items = {e: [] for e in ENGINES}
        self.needed = set()

    def add(self, eng, kind, fn, deps, inc):
        toks = [d for d in deps if d is not None]
        self.items[eng].append([kind, fn, toks, inc])
        return (eng, len(self.items[eng]) - 1)


class Rs:
    def __init__(self, name):
        self.name = name
        self.w = None
        self.readers = []

    def rd(self):
        return [self.w] if self.w is not None else []

    def wd(self):
        d = list(self.readers)
        if self.w is not None:
            d.append(self.w)
        return d

    def did_read(self, tok):
        self.readers.append(tok)

    def did_write(self, tok):
        self.w = tok
        self.readers = []


class K:
    def __init__(self, plan):
        self.p = plan

    def _deps(self, reads, writes, deps):
        d = list(deps)
        for r in reads:
            d += r.rd()
        for w in writes:
            d += w.wd()
        return d

    def op(self, eng, fn, reads=(), writes=(), deps=(), inc=1):
        tok = self.p.add(eng, "op", fn, self._deps(reads, writes, deps), inc)
        for r in reads:
            r.did_read(tok)
        for w in writes:
            w.did_write(tok)
        return tok

    def raw(self, eng, fn, reads=(), writes=(), deps=(), incs=0):
        tok = self.p.add(eng, "raw", fn, self._deps(reads, writes, deps), incs)
        if incs:
            self.p.needed.add(tok)
        for r in reads:
            r.did_read(tok)
        for w in writes:
            w.did_write(tok)
        return tok


def emit_all(nc, plan, es):
    NQ = 8
    for e in ENGINES:
        for i, it in enumerate(plan.items[e]):
            for (de, di) in it[2]:
                if de == e == "pe":
                    continue
                plan.needed.add((de, di))
            if it[0] == "op" and it[3] == 16:
                plan.needed.add((e, i))
    csem = {e: es.enter_context(nc.semaphore(f"sem_{e}_c")) for e in ENGINES}
    dsem = {}
    for e in ("pool", "sp"):
        for q in range(NQ):
            dsem[(e, q)] = es.enter_context(nc.semaphore(f"sem_{e}_d{q}"))
    psem = es.enter_context(nc.semaphore("sem_poll"))
    rawsems = {}

    msval = {}
    pre_wait = {}
    ccnt = {e: 0 for e in ENGINES}
    dcnt = {}
    dn = {e: 0 for e in ENGINES}
    for e in ENGINES:
        for i, it in enumerate(plan.items[e]):
            kind, fn, deps, inc = it
            is_dma = (kind == "op" and inc == 16)
            if kind == "raw" and inc > 0:
                rawsems[(e, i)] = es.enter_context(
                    nc.semaphore(f"sem_raw_{e}_{i}"))
                msval[(e, i)] = ("r", (e, i), inc)
                continue
            if (e, i) not in plan.needed and not is_dma:
                continue
            if is_dma:
                q = dn[e] % NQ
                dn[e] += 1
                prev = dcnt.get((e, q), 0)
                if prev > 0:
                    pre_wait[(e, i)] = (q, prev)
                dcnt[(e, q)] = prev + 16
                msval[(e, i)] = ("d", e, q, dcnt[(e, q)])
            else:
                ccnt[e] += inc
                msval[(e, i)] = ("c", e, ccnt[e])

    block = es.enter_context(nc.Block())
    eng_attr = {"pe": "tensor", "dve": "vector", "act": "scalar",
                "pool": "gpsimd", "sp": "sync"}

    def sem_of(mv):
        if mv[0] == "c":
            return csem[mv[1]], mv[2]
        if mv[0] == "d":
            return dsem[(mv[1], mv[2])], mv[3]
        return rawsems[mv[1]], mv[2]

    def make_body(e):
        def body(eng):
            hw = {}
            def wait_to(sem, val):
                if hw.get(sem.name if hasattr(sem, "name") else id(sem), 0) < val:
                    eng.wait_ge(sem, val)
                    hw[sem.name if hasattr(sem, "name") else id(sem)] = val
            for i, (kind, fn, deps, inc) in enumerate(plan.items[e]):
                want = {}
                for (de, di) in deps:
                    if de == e == "pe":
                        continue
                    mv = msval.get((de, di))
                    if mv is None:
                        continue
                    sem, v = sem_of(mv)
                    kkey = sem.name if hasattr(sem, "name") else id(sem)
                    if v > want.get(kkey, (None, 0))[1]:
                        want[kkey] = (sem, v)
                for kkey in sorted(want):
                    wait_to(*want[kkey])
                if (e, i) in pre_wait:
                    q, v = pre_wait[(e, i)]
                    wait_to(dsem[(e, q)], v)
                if kind == "op":
                    inst = fn(eng)
                    mv = msval.get((e, i))
                    if mv is not None:
                        sem, _ = sem_of(mv)
                        inst.then_inc(sem, inc)
                elif kind == "raw" and inc > 0:
                    fn(eng, rawsems[(e, i)], psem)
                else:
                    fn(eng, None, psem)
            if e == "sp":
                for q in range(NQ):
                    if dcnt.get((e, q), 0) > 0:
                        eng.wait_ge(dsem[(e, q)], dcnt[(e, q)])
        return body

    for e in ENGINES:
        getattr(block, eng_attr[e])(make_body(e))



def build(cin, dbg=False, races=True, phases=None, mparts=None):
    c = derive(cin)
    S, NH, M, E, EI = c["S"], c["NH"], c["M"], c["E"], c["EI"]
    D, H, FC = c["D"], c["H"], c["FC"]
    CH, TILE, CPT, NT = c["CH"], c["TILE"], c["CPT"], c["NT"]
    NB, NBS, OWN, OH = c["NB"], c["NBS"], c["OWN"], c["OH"]
    SO = c["SO"]
    CT, TPH = c["CT"], c["TPH"]
    SI, EIC, SIC = c["SI"], c["EIC"], c["SIC"]
    NFO, FOW = c["NFO"], c["FOW"]
    M65 = M * 65
    NH325 = NH * 325
    NOC = OWN // 128
    CTC = CT // 128

    PH = set(phases) if phases is not None else {"pq0", "state", "main0",
                                                  "mbox", "c"}
    MP = set(mparts) if mparts is not None else {"kv", "skt", "heads", "epi",
                                                 "wo", "ln", "x1t", "rout",
                                                 "hat", "hatm", "hcm", "hni", "hst", "x"}
    nc = bass.Bass("TRN2", target_bir_lowering=False,
                   detect_race_conditions=races)

    hsT = [nc.dram_tensor("hsT_own", [H, SO], F32, kind="ExternalInput").ap()]
    hsT_state = nc.dram_tensor("hsT_state", [H, SO], F32,
                               kind="ExternalInput").ap()
    hstok = [nc.dram_tensor("hs_own", [SO, H], F32, kind="ExternalInput").ap()]
    wT_in = [nc.dram_tensor(n, [H, H], F32, kind="ExternalInput").ap()
             for n in ("wqT", "wkT", "wvT", "woT")]
    memcat = nc.dram_tensor("memcat", [NH, D, M65], F32, kind="ExternalInput").ap()
    wmn = nc.dram_tensor("wmn", [H, NH * M], F32, kind="ExternalInput").ap()
    gate_in = nc.dram_tensor("gate_in", [NH, 1], F32, kind="ExternalInput").ap()
    gwT = nc.dram_tensor("gwT", [H, E], F32, kind="ExternalInput").ap()
    lbb = nc.dram_tensor("lbb", [1, E], F32, kind="ExternalInput").ap()
    egT = nc.dram_tensor("egT", [E, H, EI], F32, kind="ExternalInput").ap()
    euT = nc.dram_tensor("euT", [E, H, EI], F32, kind="ExternalInput").ap()
    edT = nc.dram_tensor("edT", [E, EI, H], F32, kind="ExternalInput").ap()
    sgT = nc.dram_tensor("sgT", [H, SI], F32, kind="ExternalInput").ap()
    suT = nc.dram_tensor("suT", [H, SI], F32, kind="ExternalInput").ap()
    sdT = nc.dram_tensor("sdT", [SI, H], F32, kind="ExternalInput").ap()
    cfgv = nc.dram_tensor("cfgv", [1, 8], I32, kind="ExternalInput").ap()

    out = nc.dram_tensor("out", [OWN, H], F32, kind="ExternalOutput").ap()

    sq_scr_t = nc.dram_tensor("sq_scr", [FC, 128, NBS], BF16, kind="Internal")
    x1t_scr_t = nc.dram_tensor("x1t_scr", [FC, 128, NBS], BF16, kind="Internal")
    sel_scr_t = nc.dram_tensor("sel_scr", [NBS, E], BF16, kind="Internal")
    SSW = (NH // 2) * 65
    sdump_t = nc.dram_tensor("sdump", [2, 128, SSW], F32, kind="Internal")
    crel_in_t = nc.dram_tensor("crel_in", [1, NH * M], F32, kind="Internal")
    crel_out_t = nc.dram_tensor("crel_out", [1, NH * M], F32, kind="Internal")
    cstat_in_t = nc.dram_tensor("cstat_in", [1, 16], F32, kind="Internal")
    cstat_out_t = nc.dram_tensor("cstat_out", [1, 16], F32, kind="Internal")

    dbg_outs = {}
    if dbg:
        dbg_outs["dbg_x1"] = nc.dram_tensor("dbg_x1", [NBS, H], F32,
                                            kind="ExternalOutput").ap()
        dbg_outs["dbg_stats"] = nc.dram_tensor("dbg_stats", [1, 16], F32,
                                               kind="ExternalOutput").ap()
        dbg_outs["dbg_w"] = nc.dram_tensor("dbg_w", [NB, NH, 5], F32,
                                           kind="ExternalOutput").ap()
        dbg_outs["dbg_wmean"] = nc.dram_tensor("dbg_wmean", [1, E], F32,
                                               kind="ExternalOutput").ap()

    with ExitStack() as es:
        ctx = es.enter_context

        W_AR = ctx(nc.sbuf_tensor("W_AR", [128, 4 * FC * H], BF16))
        EXPCOLS = 2 * FC * EI + EIC * H
        W2 = (NH // 2) * 325

        _off = [0]
        _c_marks = {}
        def _carve(n):
            c0 = _off[0]
            _off[0] += (n + 1) & ~1
            return c0
        O_hsT = [_carve(FC * TILE) for _ in range(2)]
        O_sqT = [_carve(FC * TILE) for _ in range(2)]
        O_x1tt = [_carve(FC * TILE) for _ in range(2)]
        O_cb = [_carve(NH325) for _ in range(2)]
        O_rhs = _carve(W2)
        O_skt = [_carve(H) for _ in range(2)]
        O_skT = [_carve(H) for _ in range(2)]
        O_va = [_carve(NH * 65) for _ in range(2)]
        O_am = [_carve(512) for _ in range(2)]
        O_ep = [_carve(H) for _ in range(4)]
        O_outpre = _carve(H)
        O_opT = _carve(H)
        O_x1c = _carve(H)
        O_xsq = _carve(H)
        O_x1r = _carve(2 * H)
        O_hs = [_carve(2 * H) for _ in range(2)]
        O_elu = [_carve(1024) for _ in range(2)]
        O_ssb = _carve((NH // 2) * 65 * 2)
        att_end = _off[0]
        _off[0] = 0
        O_exp = _carve(2 * EXPCOLS)
        O_acc = _carve(2 * (OH // 128) * H)
        O_hb = _carve(SIC * CT)
        O_silu = _carve(CT)
        O_selown = _carve(NOC * E)
        O_xout = _carve(2 * H)
        O_x2r = _carve(2 * H)
        O_x1tok = _carve(H)
        UNW = max(att_end, _off[0])
        UN = ctx(nc.sbuf_tensor("UN", [128, UNW], BF16))
        def _v(o, n):
            return UN[:, o:o + n]
        def _vf(o, n):
            return UN[:, o:o + 2 * n].bitcast(F32)
        hsT_b = [_v(O_hsT[i], FC * TILE) for i in range(2)]
        sqT_b = [_v(O_sqT[i], FC * TILE) for i in range(2)]
        x1t_tile = [_v(O_x1tt[i], FC * TILE) for i in range(2)]
        cb_b = [_v(O_cb[i], NH325) for i in range(2)]
        rhscat = _v(O_rhs, W2)
        skt_b = [_v(O_skt[i], H) for i in range(2)]
        skT_b = [_v(O_skT[i], H) for i in range(2)]
        vaug_b = [_v(O_va[i], NH * 65) for i in range(2)]
        am_b = [_v(O_am[i], 512) for i in range(2)]
        ep_t = [_v(O_ep[i], H) for i in range(4)]
        outpre_b = _v(O_outpre, H)
        opT_b = _v(O_opT, H)
        x1c_b = _v(O_x1c, H)
        xsq_scr = _v(O_xsq, H)
        x1r_b = _vf(O_x1r, H)
        hs_b = [_vf(O_hs[i], H) for i in range(2)]
        elu_t1 = _vf(O_elu[0], 512)
        elu_t2 = _vf(O_elu[1], 512)
        ssb = _vf(O_ssb, (NH // 2) * 65)
        EXP_AR = _v(O_exp, 2 * EXPCOLS)
        accv_full = _vf(O_acc, (OH // 128) * H)
        _hbv = _v(O_hb, SIC * CT)
        h_b = [_hbv, _hbv]
        silu_b = _v(O_silu, CT)
        sel_own = _v(O_selown, NOC * E)
        xout = _vf(O_xout, H)
        x2r_b = _vf(O_x2r, H)
        x1tok_b = _v(O_x1tok, H)
        sqs_tile = ctx(nc.sbuf_tensor("sqs_tile", [128, FC], F32))
        sqsum = ctx(nc.sbuf_tensor("sqsum", [128, FC], F32))
        sqsum_bf = ctx(nc.sbuf_tensor("sqsum_bf", [128, FC], BF16))
        wmn_sb = ctx(nc.sbuf_tensor("wmn_sb", [128, FC * NH * M], BF16))
        gwt_sb = ctx(nc.sbuf_tensor("gwt_sb", [128, FC * E], BF16))
        biasb = ctx(nc.sbuf_tensor("biasb", [128, E], F32))
        coefb = ctx(nc.sbuf_tensor("coefb", [128, NH * 5], F32))
        wmeanb = ctx(nc.sbuf_tensor("wmeanb", [128, E], F32))
        sc_raw = ctx(nc.sbuf_tensor("sc_raw", [128, NH * 5], F32))
        sc_clip = ctx(nc.sbuf_tensor("sc_clip", [128, NH * 5], F32))
        sc_scale = ctx(nc.sbuf_tensor("sc_scale", [128, NH * 5], F32))
        lncol = ctx(nc.sbuf_tensor("lncol", [128, 8], F32))
        sco_p = ctx(nc.sbuf_tensor("sco_p", [128, E], F32))
        sco_e = ctx(nc.sbuf_tensor("sco_e", [128, E], F32))
        scores = ctx(nc.sbuf_tensor("scores", [128, E], F32))
        rmx = ctx(nc.sbuf_tensor("rmx", [128, 4], F32))
        ismax = ctx(nc.sbuf_tensor("ismax", [128, E], F32))
        smask = ctx(nc.sbuf_tensor("smask", [128, E], F32))
        self32 = ctx(nc.sbuf_tensor("self32", [128, E], F32))
        wsel = ctx(nc.sbuf_tensor("wsel", [128, 16], F32))
        wselacc = ctx(nc.sbuf_tensor("wselacc", [128, 16], F32))
        ones_col_f = ctx(nc.sbuf_tensor("ones_col_f", [128, 1], F32))
        sel_b = [ctx(nc.sbuf_tensor(f"sel_b{i}", [128, CPT * E], BF16)) for i in range(2)]
        statacc = ctx(nc.sbuf_tensor("statacc", [16, 1], F32))
        g16 = ctx(nc.sbuf_tensor("g16", [NH, 1], F32))
        omg16 = ctx(nc.sbuf_tensor("omg16", [NH, 1], F32))
        gtmp = ctx(nc.sbuf_tensor("gtmp", [NH, 4], F32))
        relrow = ctx(nc.sbuf_tensor("relrow", [1, NH * M], F32))
        relt16 = ctx(nc.sbuf_tensor("relt16", [NH, M], F32))
        coef16 = ctx(nc.sbuf_tensor("coef16", [NH, 5], F32))
        rowbuf = ctx(nc.sbuf_tensor("rowbuf", [1, NH * 5], F32))
        rowbuf_bf = ctx(nc.sbuf_tensor("rowbuf_bf", [1, NH * 5], BF16))
        ones_row = ctx(nc.sbuf_tensor("ones_row", [1, 128], BF16))
        ones_col = ctx(nc.sbuf_tensor("ones_col", [128, 1], BF16))
        identb = ctx(nc.sbuf_tensor("identb", [128, 128], BF16))
        trib = ctx(nc.sbuf_tensor("trib", [128, 512], BF16))
        lbb_sb = ctx(nc.sbuf_tensor("lbb_sb", [1, E], F32))
        lbb_bf = ctx(nc.sbuf_tensor("lbb_bf", [1, E], BF16))
        epsb = ctx(nc.sbuf_tensor("epsb", [128, 1], F32))
        cfg_sb = ctx(nc.sbuf_tensor("cfg_sb", [1, 8], I32))
        mrow = ctx(nc.sbuf_tensor("mrow", [1, 64], F32))
        wm_row = ctx(nc.sbuf_tensor("wm_row", [1, E], F32))
        wm_bf = ctx(nc.sbuf_tensor("wm_bf", [1, E], BF16))
        wtil = ctx(nc.sbuf_tensor("wtil", [128, 1], F32))
        dbg_xout = (ctx(nc.sbuf_tensor("dbg_xout", [128, H], F32))
                    if dbg else None)

        PW = ctx(nc.psum_tensor("PW", [128, 1024], F32))
        PN = ctx(nc.psum_tensor("PN", [128, 1024], F32))
        PA = ctx(nc.psum_tensor("PA", [128, 512], F32))
        PT = ctx(nc.psum_tensor("PT", [128, 1024], BF16))
        PS = ctx(nc.psum_tensor("PS", [128, 512], F32))
        PS2 = ctx(nc.psum_tensor("PS2", [128, 512], F32))

        plan = Plan()
        k = K(plan)

        r_war = Rs("W_AR")
        r_exp = [Rs(f"EXP{i}") for i in range(2)]
        r_acc = Rs("ACC")
        r_hsT = [Rs(f"hsT{i}") for i in range(2)]
        r_sqT = [Rs(f"sqT{i}") for i in range(2)]
        r_hs = [Rs(f"hs{i}") for i in range(2)]
        r_skt = [Rs(f"skt{i}") for i in range(2)]
        r_skT = [Rs(f"skT{i}") for i in range(2)]
        r_vaug = [Rs(f"vaug{i}") for i in range(2)]
        r_am = [Rs(f"am{i}") for i in range(2)]
        r_cb = [Rs(f"cb{i}") for i in range(2)]
        r_rhs_mem = Rs("rhs_mem")
        r_rhs_S = Rs("rhs_S")
        r_ssb = Rs("ssb")
        r_outpre = Rs("outpre")
        r_opT = Rs("opT")
        r_x1r = Rs("x1r")
        r_x1c = Rs("x1c")
        r_x1tt = [Rs(f"x1tt{i}") for i in range(2)]
        r_elu1 = Rs("elu1")
        r_elu2 = Rs("elu2")
        r_sqs = Rs("sqs")
        r_sqsum = Rs("sqsum")
        r_sqsum_bf = Rs("sqsum_bf")
        r_consts = Rs("consts")
        r_coefb = Rs("coefb")
        r_scraw = Rs("scraw")
        r_scclip = Rs("scclip")
        r_scscale = Rs("scscale")
        r_ept = [Rs(f"ept{i}") for i in range(4)]
        r_xsq = Rs("xsq")
        r_lncol = Rs("lncol")
        r_scop = Rs("scop")
        r_scoe = Rs("scoe")
        r_scores = Rs("scores")
        r_rmx = Rs("rmx")
        r_ismax = Rs("ismax")
        r_smask = Rs("smask")
        r_self32 = Rs("self32")
        r_wsel = Rs("wsel")
        r_wselacc = Rs("wselacc")
        r_sel_b = [Rs(f"selb{i}") for i in range(2)]
        r_statacc = Rs("statacc")
        r_small = Rs("small16")
        r_relrow = Rs("relrow")
        r_selown = Rs("selown")
        _rhb = Rs("hb")
        r_hb = [_rhb, _rhb]
        r_silu = Rs("silu")
        r_wtil = Rs("wtil")
        r_xout = Rs("xout")
        r_x2r = Rs("x2r")
        r_x1tok = Rs("x1tok")
        r_mrow = Rs("mrow")
        r_both = Rs("both")
        r_wmrow = Rs("wmrow")
        r_wmeanb = Rs("wmeanb")
        r_PW = [Rs("PW0"), Rs("PW1")]
        r_PN = [Rs("PN0"), Rs("PN1")]
        r_PA = Rs("PA")
        r_PT = Rs("PT")
        r_PS = Rs("PS")
        r_PS2 = Rs("PS2")
        r_sqscr = Rs("sq_scr")
        r_x1scr_h = [Rs("x1_scr0"), Rs("x1_scr1")]
        r_x1scr_t0 = Rs("x1_scr_t0")
        r_war_h = [Rs("W_AR_a"), Rs("W_AR_b")]
        r_war_t0 = Rs("W_AR_t0")
        r_selscr = Rs("sel_scr")
        r_sdump = Rs("sdump")
        r_crel = Rs("crel")
        r_cstat = Rs("cstat")

        def fn_wl4(g, sem, psem):
            for wi in range(4):
                g.dma_start(
                    out=bass.AP(W_AR, wi * FC * H, [[4 * FC * H, 128], [H, FC], [1, H]]),
                    in_=bass.AP(wT_in[wi].tensor, 0, [[H, 128], [128 * H, FC], [1, H]])
                ).then_inc(sem, 16)
        k.raw("pool", fn_wl4, writes=[r_war], incs=64)

        for h2 in range(2):
            def fn_memcat(g, h2=h2):
                return g.dma_start(
                    out=rhscat[h2 * 64:h2 * 64 + 64, :].rearrange(
                        "p (a b) -> p a b", b=325)[:, :, 65:325],
                    in_=bass.AP(memcat.tensor, h2 * D * M65,
                                [[M65, 64], [2 * D * M65, NH // 2], [1, M65]]))
            k.op("pool", fn_memcat, writes=[r_rhs_mem], inc=16)

        def fn_wmn(g):
            return g.dma_start(
                out=bass.AP(wmn_sb, 0, [[FC * NH * M, 128], [NH * M, FC], [1, NH * M]]),
                in_=bass.AP(wmn.tensor, 0, [[NH * M, 128], [128 * NH * M, FC], [1, NH * M]]))
        k.op("pool", fn_wmn, writes=[r_consts], inc=16)

        def fn_gwt(g):
            return g.dma_start(
                out=bass.AP(gwt_sb, 0, [[FC * E, 128], [E, FC], [1, E]]),
                in_=bass.AP(gwT.tensor, 0, [[E, 128], [128 * E, FC], [1, E]]))
        k.op("pool", fn_gwt, writes=[r_consts], inc=16)

        t_gate = k.op("pool", lambda g: g.dma_start(out=g16[:], in_=gate_in),
                      writes=[r_small], inc=16)
        t_lbb = k.op("pool", lambda g: g.dma_start(out=lbb_sb[:], in_=lbb),
                     writes=[r_small], inc=16)
        k.op("pool", lambda g: g.dma_start(out=cfg_sb[:], in_=cfgv),
             writes=[r_small], inc=16)

        k.op("pool", lambda g: g.memset(ones_row[:], 1.0), writes=[r_consts])
        k.op("pool", lambda g: g.memset(ones_col[:], 1.0), writes=[r_consts])
        k.op("pool", lambda g: g.memset(ones_col_f[:], 1.0), writes=[r_consts])
        k.op("pool", lambda g: g.memset(identb[:], 1.0), writes=[r_consts])
        k.op("pool", lambda g: g.affine_select(identb[:], identb[:], [[1, 128]],
                                               ALU.is_equal, 0.0, base=0,
                                               channel_multiplier=-1),
             writes=[r_consts])
        k.op("pool", lambda g: g.memset(trib[:], 1.0), writes=[r_consts])
        for kk in range(4):
            def fn_tri(g, kk=kk):
                v = trib[:, kk * 128:(kk + 1) * 128]
                return g.affine_select(v, v, [[1, 128]], ALU.is_ge, 0.0,
                                       base=0, channel_multiplier=-1)
            k.op("pool", fn_tri, writes=[r_consts])
        for i in range(2):
            def fn_vone(g, i=i):
                return g.memset(
                    vaug_b[i].rearrange("p (h c) -> p h c", c=65)[:, :, 64:65], 1.0)
            k.op("pool", fn_vone, writes=[r_vaug[i]])

        k.op("pool", lambda g: g.memset(statacc[:], 0.0), writes=[r_statacc])
        k.op("pool", lambda g: g.memset(wselacc[:], 0.0), writes=[r_wselacc])
        k.op("pool", lambda g: g.memset(epsb[:], EPS_LN), writes=[r_consts])

        k.op("act", lambda s_: s_.activation(gtmp[:, 0:1], g16[:], AF.Exp, scale=-1.0),
             reads=[r_small], writes=[r_small], deps=[t_gate])
        k.op("dve", lambda v: v.tensor_scalar_add(gtmp[:, 1:2], gtmp[:, 0:1], 1.0),
             reads=[r_small], writes=[r_small])
        k.op("dve", lambda v: v.reciprocal(g16[:], gtmp[:, 1:2]),
             reads=[r_small], writes=[r_small])
        k.op("dve", lambda v: v.tensor_scalar(omg16[:], g16[:], -1.0, 1.0,
                                              ALU.mult, ALU.add),
             reads=[r_small], writes=[r_small])

        k.op("dve", lambda v: v.tensor_copy(lbb_bf[:], lbb_sb[:]),
             reads=[r_small], writes=[r_small], deps=[t_lbb])
        k.op("pe", lambda t_: t_.matmul(PA[:, 0:E], ones_row[:], lbb_bf[:],
                                        start=True, stop=True),
             reads=[r_consts, r_small], writes=[r_PA])
        k.op("act", lambda s_: s_.copy(biasb[:], PA[:, 0:E]),
             reads=[r_PA], writes=[r_consts])

        def sq_scr_ap(bi, t0):
            return bass.AP(sq_scr_t, bi * S + t0,
                           [[NBS, 128], [128 * NBS, FC], [1, TILE]])

        def x1_scr_ap(bi, t0):
            return bass.AP(x1t_scr_t, bi * S + t0,
                           [[NBS, 128], [128 * NBS, FC], [1, TILE]])

        pending_tail = [None, None]

        def emit_tail(idx):
            fn, pending_tail[idx] = pending_tail[idx], None
            if fn is not None:
                fn()

        rel_tail = [None]
        for bi in range(NB):
            if f"pq{bi}" not in PH:
                continue
            k.op("pool", lambda g: g.memset(sqsum[:], 0.0), writes=[r_sqsum])
            def emit_pq_load(t, bi=bi):
                tb = t % 2
                t0 = t * TILE
                def fn_hsT(g, bi=bi, t0=t0, tb=tb):
                    return g.dma_start(
                        out=hsT_b[tb][:],
                        in_=bass.AP(hsT[bi].tensor, t0,
                                    [[SO, 128], [128 * SO, FC], [1, TILE]]))
                k.op("pool", fn_hsT, writes=[r_hsT[tb]], inc=16)
            emit_pq_load(0)
            for t in range(NT):
                tb = t % 2
                t0 = t * TILE
                if t + 1 < NT:
                    emit_pq_load(t + 1)

                for foc in range(FC):
                    pw = foc % 2
                    for fic in range(FC):
                        def fn_mm(t_, foc=foc, fic=fic, tb=tb, pw=pw):
                            return t_.matmul(
                                PW[:, pw * 512:pw * 512 + 512],
                                W_AR[:, fic * H + foc * 128:fic * H + foc * 128 + 128],
                                hsT_b[tb][:, fic * TILE:fic * TILE + TILE],
                                start=(fic == 0), stop=(fic == FC - 1))
                        k.op("pe", fn_mm, reads=[r_war, r_hsT[tb]], writes=[r_PW[pw]])
                    def fn_nrelu(s_, pw=pw):
                        return s_.activation(elu_t1[:],
                                             PW[:, pw * 512:pw * 512 + 512],
                                             AF.Relu, scale=-1.0)
                    k.op("act", fn_nrelu, reads=[r_PW[pw]], writes=[r_elu1])
                    k.op("act", lambda s_: s_.activation(elu_t2[:], elu_t1[:],
                                                         AF.Exp, scale=-1.0),
                         reads=[r_elu1], writes=[r_elu2])
                    def fn_comb(v, pw=pw, foc=foc, tb=tb):
                        return v.scalar_tensor_tensor(
                            sqT_b[tb][:, foc * TILE:foc * TILE + TILE],
                            PW[:, pw * 512:pw * 512 + 512], 0.0, elu_t2[:],
                            ALU.max, ALU.add,
                            accum_out=sqs_tile[:, foc:foc + 1])
                    k.op("dve", fn_comb, reads=[r_PW[pw], r_elu2],
                         writes=[r_sqT[tb], r_sqs])
                k.op("dve", lambda v: v.tensor_add(sqsum[:], sqsum[:], sqs_tile[:]),
                     reads=[r_sqs], writes=[r_sqsum])
                def fn_store(g, bi=bi, t0=t0, tb=tb):
                    return g.dma_start(out=sq_scr_ap(bi, t0), in_=sqT_b[tb][:])
                k.op("sp", fn_store, reads=[r_sqT[tb]], writes=[r_sqscr], inc=16)

            if "state" in PH:
                k.op("pool", lambda g: g.memset(ssb[:], 0.0), writes=[r_ssb])
                def fn_zdump(g):
                    return g.dma_start(
                        out=bass.AP(sdump_t, 0, [[SSW, 128], [1, SSW]]),
                        in_=ssb[:])
                k.op("sp", fn_zdump, reads=[r_ssb], writes=[r_sdump], inc=16)
                def emit_state_load(t):
                    tb = t % 2
                    t0 = t * TILE
                    def fn_hsTs(g, t0=t0, tb=tb):
                        return g.dma_start(
                            out=hsT_b[tb][:],
                            in_=bass.AP(hsT_state.tensor, t0,
                                        [[SO, 128], [128 * SO, FC], [1, TILE]]))
                    k.op("pool", fn_hsTs, writes=[r_hsT[tb]], inc=16)
                emit_state_load(0)
                for t in range(NT):
                    tb = t % 2
                    t0 = t * TILE
                    if t + 1 < NT:
                        emit_state_load(t + 1)
                    for ci in range(CPT):
                        cbp = ci % 2
                        for wofs in (1, 2):
                            for half in range(NFO):
                                pw = half % 2
                                for fic in range(FC):
                                    def fn_mm(t_, wofs=wofs, half=half, fic=fic,
                                              tb=tb, ci=ci, pw=pw):
                                        return t_.matmul(
                                            PW[:, pw * 512:pw * 512 + FOW],
                                            hsT_b[tb][:, fic * TILE + ci * 128:
                                                      fic * TILE + ci * 128 + 128],
                                            W_AR[:, wofs * FC * H + fic * H + half * FOW:
                                                 wofs * FC * H + fic * H + half * FOW + FOW],
                                            start=(fic == 0), stop=(fic == FC - 1))
                                    k.op("pe", fn_mm, reads=[r_war, r_hsT[tb]],
                                         writes=[r_PW[pw]])
                                if wofs == 1:
                                    def fn_nrelu(s_, pw=pw):
                                        return s_.activation(
                                            elu_t1[:, 0:FOW],
                                            PW[:, pw * 512:pw * 512 + FOW],
                                            AF.Relu, scale=-1.0)
                                    k.op("act", fn_nrelu, reads=[r_PW[pw]],
                                         writes=[r_elu1])
                                    k.op("act",
                                         lambda s_: s_.activation(elu_t2[:, 0:FOW],
                                                                  elu_t1[:, 0:FOW],
                                                                  AF.Exp,
                                                                  scale=-1.0),
                                         reads=[r_elu1], writes=[r_elu2])
                                    def fn_comb(v, pw=pw, half=half, cbp=cbp):
                                        return v.scalar_tensor_tensor(
                                            skt_b[cbp][:, half * FOW:half * FOW + FOW],
                                            PW[:, pw * 512:pw * 512 + FOW], 0.0,
                                            elu_t2[:, 0:FOW],
                                            ALU.max, ALU.add)
                                    k.op("dve", fn_comb, reads=[r_PW[pw], r_elu2],
                                         writes=[r_skt[cbp]])
                                else:
                                    def fn_ve(v, pw=pw, half=half, cbp=cbp):
                                        nh_g = FOW // 64
                                        return v.tensor_copy(
                                            vaug_b[cbp].rearrange(
                                                "p (h c) -> p h c", c=65)[
                                                :, half * nh_g:(half + 1) * nh_g, 0:64],
                                            PW[:, pw * 512:pw * 512 + FOW])
                                    k.op("dve", fn_ve, reads=[r_PW[pw]],
                                         writes=[r_vaug[cbp]])
                        evens = [h for h in range(NH) if h % 2 == 0]
                        odds = [h for h in range(NH) if h % 2 == 1]
                        groups = []
                        for lst in (evens, odds):
                            for gi in range(0, len(lst), 4):
                                groups.append(lst[gi:gi + 4])
                        for hg, heads in enumerate(groups):
                            p2 = heads[0] % 2
                            gl = len(heads)
                            sup = heads[0] // 8
                            for j, h in enumerate(heads):
                                def fn_st(t_, h=h, j=j, p2=p2, cbp=cbp, gl=gl):
                                    kw = {}
                                    bank = PS if p2 == 0 else PS2
                                    if p2 == 1:
                                        kw["tile_position"] = (0, 64)
                                    return t_.matmul(
                                        bank[p2 * 64:p2 * 64 + 64,
                                             j * 65:j * 65 + 65],
                                        skt_b[cbp][:, h * 64:h * 64 + 64],
                                        vaug_b[cbp][:, h * 65:h * 65 + 65],
                                        start=(j == 0), stop=(j == gl - 1), **kw)
                                k.op("pe", fn_st, reads=[r_skt[cbp], r_vaug[cbp]],
                                     writes=[r_PS if p2 == 0 else r_PS2])
                            def fn_sa(v, p2=p2, sup=sup, gl=gl):
                                bank = PS if p2 == 0 else PS2
                                return v.tensor_add(
                                    ssb[p2 * 64:p2 * 64 + 64,
                                        sup * 260:sup * 260 + gl * 65],
                                    ssb[p2 * 64:p2 * 64 + 64,
                                        sup * 260:sup * 260 + gl * 65],
                                    bank[p2 * 64:p2 * 64 + 64, 0:gl * 65])
                            k.op("dve", fn_sa, reads=[r_PS if p2 == 0 else r_PS2],
                                 writes=[r_ssb])
                def fn_sdump(g):
                    return g.dma_start(
                        out=bass.AP(sdump_t, 128 * SSW, [[SSW, 128], [1, SSW]]),
                        in_=ssb[:])
                k.op("sp", fn_sdump, reads=[r_ssb], writes=[r_sdump], inc=16)
                def fn_sreload(g, sem, psem):
                    with g.register("sr") as sr, g.register("soff") as soff:
                        g.reg_load(sr, cfg_sb[:1, 0:1])
                        g.reg_mul(soff, sr, 128 * SSW)
                        g.dma_start(
                            out=ssb[:],
                            in_=bass.AP(sdump_t, soff, [[SSW, 128], [1, SSW]])
                        ).then_inc(sem, 16)
                k.raw("pool", fn_sreload, reads=[r_small, r_sdump],
                      writes=[r_ssb], incs=16)
                def fn_sc0(v):
                    return v.tensor_copy(
                        rhscat.rearrange("p (a b) -> p a b", b=325)[:, :, 0:65],
                        ssb.rearrange("p (a b) -> p a b", b=65))
                k.op("dve", fn_sc0, reads=[r_ssb], writes=[r_rhs_S])

            k.op("dve", lambda v: v.tensor_copy(sqsum_bf[:], sqsum[:]),
                 reads=[r_sqsum], writes=[r_sqsum_bf])
            for fic in range(FC):
                def fn_rel(t_, fic=fic):
                    return t_.matmul(PA[0:1, 0:NH * M],
                                     sqsum_bf[:, fic:fic + 1],
                                     wmn_sb[:, fic * NH * M:(fic + 1) * NH * M],
                                     start=(fic == 0), stop=(fic == FC - 1))
                k.op("pe", fn_rel, reads=[r_sqsum_bf, r_consts], writes=[r_PA])
            k.op("act", lambda s_: s_.copy(relrow[:], PA[0:1, 0:NH * M]),
                 reads=[r_PA], writes=[r_relrow])
            def fn_relcc_a(g, sem, psem):
                g.dma_start(out=crel_in_t[:, :], in_=relrow[:]).then_inc(sem, 16)
                g.wait_ge(sem, 16)
                g.collective_compute(
                    "AllReduce", ALU.add,
                    replica_groups=[[0, 1], [2, 3], [4, 5], [6, 7]],
                    ins=[crel_in_t.ap().opt()], outs=[crel_out_t.ap().opt()],
                ).then_inc(sem)
            k.raw("pool", fn_relcc_a, reads=[r_relrow, r_crel],
                  writes=[r_crel], incs=17)
            def emit_rel_tail():
                def fn_relcc_b(g, sem, psem):
                    g.dma_start(out=relrow[:], in_=crel_out_t[:, :]).then_inc(sem, 16)
                k.raw("pool", fn_relcc_b, reads=[r_crel],
                      writes=[r_relrow], incs=16)
                def fn_r16(g):
                    return g.dma_start(
                        out=bass.AP(relt16, 0, [[M, NH], [1, M]]),
                        in_=bass.AP(relrow, 0, [[NH * M, 1], [M, NH], [1, M]]))
                k.op("pool", fn_r16, reads=[r_relrow], writes=[r_small], inc=16)
                k.op("dve", lambda v: v.reduce_max(gtmp[:, 0:1], relt16[:],
                                                   axis=mybir.AxisListType.X),
                     reads=[r_small], writes=[r_small])
                k.op("dve", lambda v: v.tensor_scalar(gtmp[:, 1:2], gtmp[:, 0:1],
                                                      -1.0 / S, None, ALU.mult),
                     reads=[r_small], writes=[r_small])
                k.op("act", lambda s_: s_.activation(relt16[:], relt16[:], AF.Exp,
                                                     bias=gtmp[:, 1:2], scale=1.0 / S),
                     reads=[r_small], writes=[r_small])
                k.op("dve", lambda v: v.reduce_sum(gtmp[:, 2:3], relt16[:],
                                                   axis=mybir.AxisListType.X),
                     reads=[r_small], writes=[r_small])
                k.op("dve", lambda v: v.reciprocal(gtmp[:, 3:4], gtmp[:, 2:3]),
                     reads=[r_small], writes=[r_small])
                k.op("dve", lambda v: v.tensor_scalar(relt16[:], relt16[:], gtmp[:, 3:4],
                                                      None, ALU.mult),
                     reads=[r_small], writes=[r_small])
                k.op("dve", lambda v: v.tensor_copy(coef16[:, 0:1], omg16[:]),
                     reads=[r_small], writes=[r_small])
                k.op("dve", lambda v: v.tensor_scalar(coef16[:, 1:1 + M], relt16[:],
                                                      g16[:], None, ALU.mult),
                     reads=[r_small], writes=[r_small])
                if dbg:
                    def fn_dw(g, bi=bi):
                        return g.dma_start(
                            out=bass.AP(dbg_outs["dbg_w"].tensor, bi * NH * 5,
                                        [[5, NH], [1, 5]]),
                            in_=bass.AP(coef16, 0, [[5, NH], [1, 5]]))
                    k.op("sp", fn_dw, reads=[r_small], inc=16)
                def fn_row(g):
                    return g.dma_start(
                        out=bass.AP(rowbuf, 0, [[NH * 5, 1], [5, NH], [1, 5]]),
                        in_=bass.AP(coef16, 0, [[5, NH], [1, 5]]))
                k.op("pool", fn_row, reads=[r_small], writes=[r_small], inc=16)
                k.op("dve", lambda v: v.tensor_copy(rowbuf_bf[:], rowbuf[:]),
                     reads=[r_small], writes=[r_small])
                k.op("pe", lambda t_: t_.matmul(PA[:, 0:NH * 5], ones_row[:], rowbuf_bf[:],
                                                start=True, stop=True),
                     reads=[r_consts, r_small], writes=[r_PA])
                k.op("act", lambda s_: s_.copy(coefb[:], PA[:, 0:NH * 5]),
                     reads=[r_PA], writes=[r_coefb])
            rel_tail[0] = emit_rel_tail

            if f"main{bi}" not in PH:
                continue
            if "state" not in PH:
                k.op("pool", lambda g: g.memset(ssb[:], 0.0), writes=[r_ssb])
                def fn_srst(g):
                    return g.memset(
                        rhscat.rearrange("p (a b) -> p a b", b=325)[:, :, 0:65], 0.0)
                k.op("pool", fn_srst, writes=[r_rhs_S])

            def emit_main_loads(t, bi=bi):
                tb = t % 2
                t0 = t * TILE
                def fn_hsT(g, bi=bi, t0=t0, tb=tb):
                    return g.dma_start(
                        out=hsT_b[tb][:],
                        in_=bass.AP(hsT[bi].tensor, t0,
                                    [[SO, 128], [128 * SO, FC], [1, TILE]]))
                k.op("pool", fn_hsT, writes=[r_hsT[tb]], inc=16)
                def fn_sqTl(g, bi=bi, t0=t0, tb=tb):
                    return g.dma_start(out=sqT_b[tb][:], in_=sq_scr_ap(bi, t0))
                k.op("sp", fn_sqTl, reads=[r_sqscr], writes=[r_sqT[tb]], inc=16)

            emit_main_loads(0)
            for t in range(NT):
                tb = t % 2
                t0 = t * TILE
                if t + 1 < NT:
                    emit_main_loads(t + 1)

                for ci in range(CPT):
                    cbp = ci % 2
                    ch0 = t0 + ci * CH
                    def fn_hs(g, bi=bi, ch0=ch0, cbp=cbp):
                        return g.dma_start(
                            out=hs_b[cbp][:],
                            in_=bass.AP(hstok[bi].tensor, ch0 * H, [[H, 128], [1, H]]))
                    k.op("sp", fn_hs, writes=[r_hs[cbp]], inc=16)

                    for wofs in ((1, 2) if "kv" in MP else ()):
                        for half in range(NFO):
                            pw = half % 2
                            bank = PW if wofs == 1 else PN
                            rbank = r_PW[pw] if wofs == 1 else r_PN[pw]
                            for fic in range(FC):
                                def fn_mm(t_, wofs=wofs, half=half, fic=fic,
                                          tb=tb, ci=ci, pw=pw, bank=bank):
                                    return t_.matmul(
                                        bank[:, pw * 512:pw * 512 + FOW],
                                        hsT_b[tb][:, fic * TILE + ci * 128:
                                                  fic * TILE + ci * 128 + 128],
                                        W_AR[:, wofs * FC * H + fic * H + half * FOW:
                                             wofs * FC * H + fic * H + half * FOW + FOW],
                                        start=(fic == 0), stop=(fic == FC - 1))
                                k.op("pe", fn_mm, reads=[r_war, r_hsT[tb]],
                                     writes=[rbank])
                            if wofs == 1:
                                def fn_nrelu(s_, pw=pw):
                                    return s_.activation(
                                        elu_t1[:, 0:FOW],
                                        PW[:, pw * 512:pw * 512 + FOW],
                                        AF.Relu, scale=-1.0)
                                k.op("act", fn_nrelu, reads=[r_PW[pw]],
                                     writes=[r_elu1])
                                k.op("act",
                                     lambda s_: s_.activation(elu_t2[:, 0:FOW],
                                                              elu_t1[:, 0:FOW],
                                                              AF.Exp, scale=-1.0),
                                     reads=[r_elu1], writes=[r_elu2])
                                def fn_comb(v, pw=pw, half=half, cbp=cbp):
                                    return v.scalar_tensor_tensor(
                                        skt_b[cbp][:, half * FOW:half * FOW + FOW],
                                        PW[:, pw * 512:pw * 512 + FOW], 0.0,
                                        elu_t2[:, 0:FOW],
                                        ALU.max, ALU.add)
                                k.op("dve", fn_comb, reads=[r_PW[pw], r_elu2],
                                     writes=[r_skt[cbp]])
                            else:
                                def fn_ve(v, pw=pw, half=half, cbp=cbp):
                                    nh_g = FOW // 64
                                    return v.tensor_copy(
                                        vaug_b[cbp].rearrange(
                                            "p (h c) -> p h c", c=65)[
                                            :, half * nh_g:(half + 1) * nh_g, 0:64],
                                        PN[:, pw * 512:pw * 512 + FOW])
                                k.op("dve", fn_ve, reads=[r_PN[pw]], writes=[r_vaug[cbp]])

                    for fic in (range(FC) if "skt" in MP else ()):
                        def fn_tp(t_, fic=fic, cbp=cbp):
                            return t_.transpose(PT[:, fic * 128:fic * 128 + 128],
                                                skt_b[cbp][:, fic * 128:fic * 128 + 128],
                                                identb[:])
                        k.op("pe", fn_tp, reads=[r_skt[cbp], r_consts], writes=[r_PT])
                    if "skt" in MP:
                        def fn_tpe(s_, cbp=cbp):
                            return s_.copy(skT_b[cbp][:, 0:H], PT[:, 0:H])
                        k.op("act", fn_tpe, reads=[r_PT], writes=[r_skT[cbp]])

                    emit_tail(0)

                    evens = [h for h in range(NH) if h % 2 == 0]
                    odds = [h for h in range(NH) if h % 2 == 1]
                    groups = []
                    for lst in (evens, odds):
                        for gi in range(0, len(lst), 4):
                            groups.append(lst[gi:gi + 4])
                    if "heads" not in MP:
                        groups = []
                    for hg, heads in enumerate(groups):
                        p2 = heads[0] % 2
                        pr = p2 * 64
                        gl = len(heads)
                        sup = heads[0] // 8
                        hat_list = list(enumerate(heads)) if "hat" in MP else []
                        if "hat1" in MP:
                            hat_list = hat_list[:1]
                        for j, h in hat_list:
                            fic_h = h // 2
                            def fn_at(t_, j=j, fic_h=fic_h, pr=pr, cbp=cbp,
                                      tb=tb, ci=ci):
                                return t_.matmul(
                                    PA[:, j * 128:j * 128 + 128],
                                    skT_b[cbp][pr:pr + 64,
                                               fic_h * 128:fic_h * 128 + 128],
                                    sqT_b[tb][pr:pr + 64,
                                              fic_h * TILE + ci * 128:
                                              fic_h * TILE + ci * 128 + 128],
                                    start=True, stop=True)
                            k.op("pe", fn_at, reads=[r_skT[cbp], r_sqT[tb]],
                                 writes=[r_PA])
                        if "hatm" in MP:
                            def fn_mask(v, hg=hg, gl=gl):
                                return v.tensor_tensor(am_b[hg % 2][:, 0:gl * 128],
                                                       PA[:, 0:gl * 128],
                                                       trib[:, 0:gl * 128], ALU.mult)
                            k.op("dve", fn_mask, reads=[r_PA, r_consts],
                                 writes=[r_am[hg % 2]])

                        for j, h in enumerate(heads):
                            hb = j % 2
                            fic_h = h // 2
                            if "hcm" in MP:
                                def fn_cm(t_, h=h, hb=hb, fic_h=fic_h, pr=pr,
                                          tb=tb, ci=ci):
                                    return t_.matmul(
                                        PN[:, hb * 512:hb * 512 + 325],
                                        sqT_b[tb][pr:pr + 64,
                                                  fic_h * TILE + ci * 128:
                                                  fic_h * TILE + ci * 128 + 128],
                                        rhscat[pr:pr + 64,
                                               (h // 2) * 325:(h // 2) * 325 + 325],
                                        start=True, stop=("hni" not in MP))
                                k.op("pe", fn_cm, reads=[r_sqT[tb], r_rhs_S, r_rhs_mem],
                                     writes=[r_PN[hb]])
                            if "hni" in MP:
                                def fn_ni(t_, h=h, j=j, hb=hb, hg=hg, cbp=cbp):
                                    return t_.matmul(
                                        PN[:, hb * 512:hb * 512 + 65],
                                        am_b[hg % 2][:, j * 128:j * 128 + 128],
                                        vaug_b[cbp][:, h * 65:h * 65 + 65],
                                        start=("hcm" not in MP), stop=True)
                                k.op("pe", fn_ni, reads=[r_am[hg % 2], r_vaug[cbp]],
                                     writes=[r_PN[hb]])
                            if "hcm" in MP or "hni" in MP:
                                def fn_ne(s_, h=h, hb=hb, cbp=cbp):
                                    return s_.copy(
                                        cb_b[cbp][:, h * 325:h * 325 + 325],
                                        PN[:, hb * 512:hb * 512 + 325])
                                k.op("act", fn_ne, reads=[r_PN[hb]],
                                     writes=[r_cb[cbp]])
                            if "hst" not in MP:
                                continue
                            def fn_st(t_, h=h, j=j, p2=p2, cbp=cbp, gl=gl):
                                kw = {}
                                bank = PS if p2 == 0 else PS2
                                if p2 == 1:
                                    kw["tile_position"] = (0, 64)
                                return t_.matmul(
                                    bank[p2 * 64:p2 * 64 + 64,
                                         j * 65:j * 65 + 65],
                                    skt_b[cbp][:, h * 64:h * 64 + 64],
                                    vaug_b[cbp][:, h * 65:h * 65 + 65],
                                    start=(j == 0), stop=(j == gl - 1), **kw)
                            k.op("pe", fn_st, reads=[r_skt[cbp], r_vaug[cbp]],
                                 writes=[r_PS if p2 == 0 else r_PS2])
                        if "hst" not in MP:
                            continue
                        def fn_sa(v, p2=p2, sup=sup, gl=gl):
                            bank = PS if p2 == 0 else PS2
                            return v.tensor_add(
                                ssb[p2 * 64:p2 * 64 + 64,
                                    sup * 260:sup * 260 + gl * 65],
                                ssb[p2 * 64:p2 * 64 + 64,
                                    sup * 260:sup * 260 + gl * 65],
                                bank[p2 * 64:p2 * 64 + 64, 0:gl * 65])
                        k.op("dve", fn_sa, reads=[r_PS if p2 == 0 else r_PS2],
                             writes=[r_ssb])
                    if "hst" not in MP:
                        continue
                    def fn_sc(s_):
                        return s_.copy(
                            rhscat.rearrange("p (a b) -> p a b", b=325)[:, :, 0:65],
                            ssb.rearrange("p (a b) -> p a b", b=65))
                    k.op("act", fn_sc, reads=[r_ssb], writes=[r_rhs_S])

                    if "epi" not in MP:
                        continue
                    if rel_tail[0] is not None:
                        rel_tail[0]()
                        rel_tail[0] = None
                    def fn_g1(v, cbp=cbp):
                        return v.tensor_copy(
                            sc_raw[:],
                            cb_b[cbp].rearrange("p (h j c) -> p h j c",
                                                j=5, c=65)[:, :, :, 64])
                    k.op("dve", fn_g1, reads=[r_cb[cbp]], writes=[r_scraw])
                    k.op("dve", lambda v: v.tensor_scalar_max(sc_clip[:], sc_raw[:],
                                                              EPS_ATT),
                         reads=[r_scraw], writes=[r_scclip])
                    k.op("dve", lambda v: v.reciprocal(sc_raw[:], sc_clip[:]),
                         reads=[r_scclip], writes=[r_scraw])
                    k.op("dve", lambda v: v.tensor_tensor(sc_scale[:], sc_raw[:],
                                                          coefb[:], ALU.mult),
                         reads=[r_scraw, r_coefb], writes=[r_scscale])
                    engs = ["dve", "dve", "pool", "pool", "dve"]
                    for j in range(5):
                        def fn_mj(x, j=j, cbp=cbp):
                            dst = ep_t[j][:] if j < 4 else outpre_b[:]
                            return x.tensor_tensor(
                                dst,
                                cb_b[cbp].rearrange("p (h j c) -> p h j c",
                                                    j=5, c=65)[:, :, j, 0:64],
                                bass.AP(sc_scale, j,
                                        [[NH * 5, 128], [5, NH], [0, 64]]),
                                ALU.mult)
                        tgt = r_ept[j] if j < 4 else r_outpre
                        k.op(engs[j], fn_mj, reads=[r_cb[cbp], r_scscale], writes=[tgt])
                    k.op("dve", lambda x: x.tensor_add(ep_t[0][:], ep_t[0][:],
                                                       ep_t[1][:]),
                         reads=[r_ept[1]], writes=[r_ept[0]])
                    k.op("pool", lambda x: x.tensor_add(ep_t[2][:], ep_t[2][:],
                                                        ep_t[3][:]),
                         reads=[r_ept[3]], writes=[r_ept[2]])
                    k.op("pool", lambda x: x.tensor_add(outpre_b[:], outpre_b[:],
                                                        ep_t[0][:]),
                         reads=[r_ept[0]], writes=[r_outpre])
                    k.op("pool", lambda x: x.tensor_add(outpre_b[:], outpre_b[:],
                                                        ep_t[2][:]),
                         reads=[r_ept[2]], writes=[r_outpre])

                    def tail1(bi=bi, t=t, tb=tb, ci=ci, cbp=cbp, ch0=ch0, t0=t0):
                        if "wo" not in MP:
                            return
                        for fic in range(FC):
                            def fn_tp(t_, fic=fic):
                                return t_.transpose(
                                    PT[:, fic * 128:fic * 128 + 128],
                                    outpre_b[:, fic * 128:fic * 128 + 128],
                                    identb[:])
                            k.op("pe", fn_tp, reads=[r_outpre, r_consts],
                                 writes=[r_PT])
                        k.op("act", lambda s_: s_.copy(opT_b[:, 0:H], PT[:, 0:H]),
                             reads=[r_PT], writes=[r_opT])
                        for half in range(NFO):
                            pwh = half % 2
                            for fic in range(FC):
                                def fn_wo(t_, half=half, fic=fic, pwh=pwh):
                                    return t_.matmul(
                                        PW[:, pwh * 512:pwh * 512 + FOW],
                                        opT_b[:, fic * 128:fic * 128 + 128],
                                        W_AR[:, 3 * FC * H + fic * H + half * FOW:
                                             3 * FC * H + fic * H + half * FOW + FOW],
                                        start=(fic == 0), stop=(fic == FC - 1))
                                k.op("pe", fn_wo, reads=[r_opT, r_war],
                                     writes=[r_PW[pwh]])
                        if "ln" not in MP:
                            return
                        for half in range(NFO):
                            pwh = half % 2
                            def fn_res(v, half=half, cbp=cbp, pwh=pwh):
                                return v.scalar_tensor_tensor(
                                    x1r_b[:, half * FOW:half * FOW + FOW],
                                    PW[:, pwh * 512:pwh * 512 + FOW], 1.0,
                                    hs_b[cbp][:, half * FOW:half * FOW + FOW],
                                    ALU.mult, ALU.add,
                                    accum_out=lncol[:, half:half + 1])
                            k.op("dve", fn_res, reads=[r_PW[pwh], r_hs[cbp]],
                                 writes=[r_x1r, r_lncol])
                        if NFO == 2:
                            k.op("dve", lambda v: v.tensor_add(lncol[:, 2:3],
                                                               lncol[:, 0:1],
                                                               lncol[:, 1:2]),
                                 reads=[r_lncol], writes=[r_lncol])
                        else:
                            k.op("dve", lambda v: v.tensor_copy(lncol[:, 2:3],
                                                                lncol[:, 0:1]),
                                 reads=[r_lncol], writes=[r_lncol])
                        def fn_xsq(v):
                            return v.scalar_tensor_tensor(xsq_scr[:], x1r_b[:], 1.0,
                                                          x1r_b[:], ALU.mult,
                                                          ALU.mult,
                                                          accum_out=lncol[:, 3:4])
                        k.op("dve", fn_xsq, reads=[r_x1r], writes=[r_xsq, r_lncol])
                        k.op("dve", lambda v: v.tensor_scalar(lncol[:, 4:5],
                                                              lncol[:, 2:3],
                                                              -1.0 / H, None,
                                                              ALU.mult),
                             reads=[r_lncol], writes=[r_lncol])
                        k.op("dve", lambda v: v.tensor_scalar(lncol[:, 5:6],
                                                              lncol[:, 3:4],
                                                              1.0 / H, None,
                                                              ALU.mult),
                             reads=[r_lncol], writes=[r_lncol])
                        k.op("dve", lambda v: v.tensor_scalar(lncol[:, 6:7],
                                                              lncol[:, 4:5],
                                                              lncol[:, 4:5], None,
                                                              ALU.mult),
                             reads=[r_lncol], writes=[r_lncol])
                        k.op("dve", lambda v: v.tensor_sub(lncol[:, 7:8],
                                                           lncol[:, 5:6],
                                                           lncol[:, 6:7]),
                             reads=[r_lncol], writes=[r_lncol])
                        k.op("act", lambda s_: s_.activation(lncol[:, 6:7],
                                                             lncol[:, 7:8],
                                                             AF.Ln,
                                                             bias=epsb[:, 0:1]),
                             reads=[r_lncol], writes=[r_lncol])
                        k.op("act", lambda s_: s_.activation(lncol[:, 7:8],
                                                             lncol[:, 6:7],
                                                             AF.Exp, scale=-0.5),
                             reads=[r_lncol], writes=[r_lncol])
                        def fn_x1(v):
                            return v.tensor_scalar(x1c_b[:], x1r_b[:], lncol[:, 4:5],
                                                   lncol[:, 7:8], ALU.add, ALU.mult)
                        k.op("dve", fn_x1, reads=[r_x1r, r_lncol], writes=[r_x1c])
                        if dbg:
                            def fn_dx1f(v):
                                return v.tensor_scalar(dbg_xout[:], x1r_b[:],
                                                       lncol[:, 4:5],
                                                       lncol[:, 7:8], ALU.add,
                                                       ALU.mult)
                            k.op("dve", fn_dx1f, reads=[r_x1r, r_lncol],
                                 writes=[r_xout])
                            def fn_dx1(g, bi=bi, ch0=ch0):
                                return g.dma_start(
                                    out=bass.AP(dbg_outs["dbg_x1"].tensor,
                                                (bi * S + ch0) * H,
                                                [[H, 128], [1, H]]),
                                    in_=dbg_xout[:])
                            k.op("sp", fn_dx1, reads=[r_xout], inc=16)

                    def tail2(bi=bi, t=t, tb=tb, ci=ci, cbp=cbp, ch0=ch0, t0=t0):
                        if "wo" not in MP or "ln" not in MP or "x1t" not in MP:
                            return
                        for fic in range(FC):
                            def fn_tp(t_, fic=fic):
                                return t_.transpose(
                                    PT[:, fic * 128:fic * 128 + 128],
                                    x1c_b[:, fic * 128:fic * 128 + 128],
                                    identb[:])
                            k.op("pe", fn_tp, reads=[r_x1c, r_consts],
                                 writes=[r_PT])
                        def fn_x1te(s_, tb=tb, ci=ci):
                            return s_.copy(
                                x1t_tile[tb].rearrange("p (f t) -> p f t", t=TILE)[
                                    :, :, ci * 128:ci * 128 + 128],
                                PT[:, 0:FC * 128].rearrange("p (f t) -> p f t",
                                                            t=128))
                        k.op("act", fn_x1te, reads=[r_PT], writes=[r_x1tt[tb]])

                        if "rout" in MP:
                            def fn_lgb(t_):
                                return t_.matmul(PW[:, 0:E], ones_row[0:1, :],
                                                 lbb_bf[:],
                                                 start=True, stop=False)
                            k.op("pe", fn_lgb, reads=[r_consts, r_small],
                                 writes=[r_PW[0]])
                            for fic in range(FC):
                                def fn_lg(t_, fic=fic, tb=tb, ci=ci):
                                    return t_.matmul(
                                        PW[:, 0:E],
                                        x1t_tile[tb][:, fic * TILE + ci * 128:
                                                     fic * TILE + ci * 128 + 128],
                                        gwt_sb[:, fic * E:fic * E + E],
                                        start=False, stop=(fic == FC - 1))
                                k.op("pe", fn_lg, reads=[r_x1tt[tb], r_consts],
                                     writes=[r_PW[0]])
                            k.op("act", lambda s_: s_.copy(sco_p[:], PW[:, 0:E]),
                                 reads=[r_PW[0]], writes=[r_scop])
                            k.op("act", lambda s_: s_.activation(sco_e[:], sco_p[:],
                                                                 AF.Exp,
                                                                 scale=-1.0),
                                 reads=[r_scop], writes=[r_scoe])
                            k.op("dve", lambda v: v.tensor_scalar_add(sco_e[:],
                                                                      sco_e[:], 1.0),
                                 reads=[r_scoe], writes=[r_scoe])
                            k.op("dve", lambda v: v.reciprocal(scores[:], sco_e[:]),
                                 reads=[r_scoe], writes=[r_scores])
                            k.op("dve", lambda v: v.reduce_max(
                                rmx[:, 0:1], scores[:], axis=mybir.AxisListType.X),
                                 reads=[r_scores], writes=[r_rmx])
                            k.op("dve", lambda v: v.tensor_scalar(
                                ismax[:], scores[:], rmx[:, 0:1], None, ALU.is_ge),
                                 reads=[r_scores, r_rmx], writes=[r_ismax])
                            def fn_smk(v):
                                return v.scalar_tensor_tensor(smask[:], ismax[:],
                                                              -1e30, scores[:],
                                                              ALU.mult, ALU.add)
                            k.op("dve", fn_smk, reads=[r_ismax, r_scores],
                                 writes=[r_smask])
                            k.op("dve", lambda v: v.reduce_max(
                                rmx[:, 1:2], smask[:], axis=mybir.AxisListType.X),
                                 reads=[r_smask], writes=[r_rmx])
                            k.op("dve", lambda v: v.tensor_scalar(
                                self32[:], scores[:], rmx[:, 1:2], None, ALU.is_ge),
                                 reads=[r_scores, r_rmx], writes=[r_self32])
                            k.op("dve", lambda v: v.tensor_add(rmx[:, 2:3],
                                                               rmx[:, 0:1],
                                                               rmx[:, 1:2]),
                                 reads=[r_rmx], writes=[r_rmx])
                            k.op("dve", lambda v: v.tensor_scalar_add(rmx[:, 2:3],
                                                                      rmx[:, 2:3],
                                                                      EPS_TOP),
                                 reads=[r_rmx], writes=[r_rmx])
                            k.op("dve", lambda v: v.reciprocal(rmx[:, 3:4],
                                                               rmx[:, 2:3]),
                                 reads=[r_rmx], writes=[r_rmx])
                            k.op("dve", lambda v: v.tensor_tensor(smask[:],
                                                                  scores[:],
                                                                  self32[:],
                                                                  ALU.mult),
                                 reads=[r_scores, r_self32], writes=[r_smask])
                            k.op("dve", lambda v: v.tensor_scalar(
                                wsel[:, 0:E], smask[:], rmx[:, 3:4], None,
                                ALU.mult),
                                 reads=[r_smask, r_rmx], writes=[r_wsel])
                            k.op("dve", lambda v: v.tensor_copy(wsel[:, E:2 * E],
                                                                self32[:]),
                                 reads=[r_self32], writes=[r_wsel])
                            def fn_selc(v, tb=tb, ci=ci):
                                return v.tensor_copy(
                                    sel_b[tb][:, ci * E:ci * E + E], self32[:])
                            k.op("dve", fn_selc, reads=[r_self32],
                                 writes=[r_sel_b[tb]])
                            k.op("dve", lambda v: v.tensor_add(wselacc[:],
                                                               wselacc[:],
                                                               wsel[:]),
                                 reads=[r_wsel], writes=[r_wselacc])

                        if ci == CPT - 1:
                            def fn_x1s(g, bi=bi, t0=t0, tb=tb):
                                return g.dma_start(out=x1_scr_ap(bi, t0),
                                                   in_=x1t_tile[tb][:])
                            k.op("sp", fn_x1s, reads=[r_x1tt[tb]],
                                 writes=[r_x1scr_t0 if t0 == 0 else
                                         r_x1scr_h[1 if t0 >= OH else 0]],
                                 inc=16)
                            def fn_sels(g, bi=bi, t0=t0, tb=tb):
                                return g.dma_start(
                                    out=bass.AP(sel_scr_t, (bi * S + t0) * E,
                                                [[E, 128], [128 * E, CPT], [1, E]]),
                                    in_=bass.AP(sel_b[tb], 0,
                                                [[CPT * E, 128], [E, CPT], [1, E]]))
                            k.op("sp", fn_sels, reads=[r_sel_b[tb]],
                                 writes=[r_selscr], inc=16)
                    emit_tail(1)
                    pending_tail[0] = tail1
                    pending_tail[1] = tail2
            emit_tail(0)
            emit_tail(1)
            if "rout" in MP:
                def fn_stm(t_):
                    return t_.matmul(PW[0:16, 1023:1024], wselacc[:],
                                     ones_col_f[:], start=True, stop=True)
                k.op("pe", fn_stm, reads=[r_wselacc, r_consts],
                     writes=[r_PW[1]])
                k.op("act", lambda s_: s_.copy(statacc[:],
                                               PW[0:16, 1023:1024]),
                     reads=[r_PW[1]], writes=[r_statacc])

        if rel_tail[0] is not None:
            rel_tail[0]()
            rel_tail[0] = None

        def mb_op(*a, **kw):
            if "mbox" in PH:
                return k.op(*a, **kw)
            return None
        def fn_statcc_a(g, sem, psem):
            g.dma_start(bass.AP(cstat_in_t, 0, [[16, 1], [1, 1], [1, 16]]),
                        bass.AP(statacc, 0, [[1, 16], [1, 1]])).then_inc(sem, 16)
            g.wait_ge(sem, 16)
            g.collective_compute(
                "AllReduce", ALU.add,
                replica_groups=[list(range(8))],
                ins=[cstat_in_t.ap().opt()], outs=[cstat_out_t.ap().opt()],
            ).then_inc(sem)
        def fn_statcc_b(g, sem, psem):
            g.dma_start(out=mrow[:, 0:16], in_=cstat_out_t[:, :]).then_inc(sem, 16)
        def emit_stats_cc():
            if "mbox" in PH:
                k.raw("pool", fn_statcc_a, reads=[r_statacc, r_cstat],
                      writes=[r_cstat], incs=17)
        def emit_stats_tail():
            if "mbox" not in PH:
                k.op("pool", lambda g: g.memset(wmeanb[:], 0.5),
                     writes=[r_wmeanb])
                return
            k.raw("pool", fn_statcc_b, reads=[r_cstat],
                  writes=[r_mrow], incs=16)
            mb_op("dve", lambda v: v.tensor_scalar_max(mrow[:, 24:24 + E],
                                                      mrow[:, E:2 * E], 1.0),
                 reads=[r_mrow], writes=[r_mrow])
            mb_op("dve", lambda v: v.reciprocal(mrow[:, 32:32 + E],
                                               mrow[:, 24:24 + E]),
                 reads=[r_mrow], writes=[r_mrow])
            mb_op("dve", lambda v: v.tensor_tensor(wm_row[:], mrow[:, 0:E],
                                                  mrow[:, 32:32 + E], ALU.mult),
                 reads=[r_mrow], writes=[r_wmrow])
            mb_op("dve", lambda v: v.tensor_copy(wm_bf[:], wm_row[:]),
                 reads=[r_wmrow], writes=[r_wmrow])
            mb_op("pe", lambda t_: t_.matmul(PA[:, 0:E], ones_row[:], wm_bf[:],
                                            start=True, stop=True),
                 reads=[r_consts, r_wmrow], writes=[r_PA])
            mb_op("act", lambda s_: s_.copy(wmeanb[:], PA[:, 0:E]),
                 reads=[r_PA], writes=[r_wmeanb])
        if "c" not in PH:
            emit_stats_cc()
            emit_stats_tail()

        att_alias = ([r_hsT[0], r_hsT[1], r_sqT[0], r_sqT[1], r_x1tt[0],
                      r_x1tt[1], r_cb[0], r_cb[1], r_rhs_S, r_rhs_mem,
                      r_skt[0], r_skt[1], r_skT[0], r_skT[1], r_vaug[0],
                      r_vaug[1], r_am[0], r_am[1], r_outpre, r_opT, r_x1c,
                      r_xsq, r_x1r, r_hs[0], r_hs[1], r_elu1, r_elu2, r_ssb]
                     + r_ept)
        if "c" in PH:
            k.op("pool", lambda g: g.engine_nop(),
                 reads=att_alias,
                 writes=[r_exp[0], r_exp[1], r_acc, r_hb[0], r_silu,
                         r_selown, r_xout, r_x2r, r_x1tok, r_war])
            emit_stats_cc()
            def fn_clr(base, ln):
                def fn(g, sem, psem, base=base, ln=ln):
                    for fic in range(FC):
                        g.dma_start(
                            bass.AP(W_AR, fic * OWN + base,
                                    [[4 * FC * H, 128], [1, ln]]),
                            bass.AP(x1t_scr_t, fic * 128 * NBS + base,
                                    [[NBS, 128], [1, ln]])).then_inc(sem, 16)
                return fn
            k.raw("pool", fn_clr(0, CT), reads=[r_x1scr_t0, r_war],
                  writes=[r_war_t0], incs=FC * 16)
            k.raw("sp", fn_clr(CT, OH - CT), reads=[r_x1scr_h[0], r_war],
                  writes=[r_war_h[0]], incs=FC * 16)
            k.raw("sp", fn_clr(OH, OH), reads=[r_x1scr_h[1], r_war],
                  writes=[r_war_h[1]], incs=FC * 16)
            def fn_cls(g):
                return g.dma_start(
                    sel_own.rearrange("p (a b) -> p a b", b=E),
                    bass.AP(sel_scr_t, 0, [[E, 128], [128 * E, NOC], [1, E]]))
            k.op("pool", fn_cls, reads=[r_selscr], writes=[r_selown], inc=16)

        accv = accv_full
        acc_cols = (OH // 128) * H

        def emit_ln2(half, tc):
            oc_own = half * (OH // 128) + tc
            col0 = tc * H
            rwl = (r_war_t0 if (half == 0 and tc < CT // 128)
                   else r_war_h[half])
            for fic in range(FC):
                def fn_tp(t_, fic=fic, oc_own=oc_own):
                    return t_.transpose(
                        PT[:, fic * 128:fic * 128 + 128],
                        W_AR[:, fic * OWN + oc_own * 128:
                             fic * OWN + oc_own * 128 + 128],
                        identb[:])
                k.op("pe", fn_tp, reads=[rwl, r_consts],
                     writes=[r_PT])
            k.op("act", lambda s_: s_.copy(x1tok_b[:, 0:H], PT[:, 0:H]),
                 reads=[r_PT], writes=[r_x1tok])
            for hh in range(NFO):
                def fn_r(v, hh=hh, col0=col0):
                    return v.scalar_tensor_tensor(
                        x2r_b[:, hh * FOW:hh * FOW + FOW],
                        x1tok_b[:, hh * FOW:hh * FOW + FOW], 1.0,
                        accv[:, col0 + hh * FOW:col0 + hh * FOW + FOW],
                        ALU.mult, ALU.add,
                        accum_out=lncol[:, hh:hh + 1])
                k.op("dve", fn_r, reads=[r_x1tok, r_acc],
                     writes=[r_x2r, r_lncol])
            if NFO == 2:
                k.op("dve", lambda v: v.tensor_add(lncol[:, 2:3], lncol[:, 0:1],
                                                   lncol[:, 1:2]),
                     reads=[r_lncol], writes=[r_lncol])
            else:
                k.op("dve", lambda v: v.tensor_copy(lncol[:, 2:3], lncol[:, 0:1]),
                     reads=[r_lncol], writes=[r_lncol])
            def fn_xsq(v, col0=col0):
                return v.scalar_tensor_tensor(accv[:, col0:col0 + H],
                                              x2r_b[:], 1.0,
                                              x2r_b[:], ALU.mult, ALU.mult,
                                              accum_out=lncol[:, 3:4])
            k.op("dve", fn_xsq, reads=[r_x2r], writes=[r_acc, r_lncol])
            k.op("dve", lambda v: v.tensor_scalar(lncol[:, 4:5], lncol[:, 2:3],
                                                  -1.0 / H, None, ALU.mult),
                 reads=[r_lncol], writes=[r_lncol])
            k.op("dve", lambda v: v.tensor_scalar(lncol[:, 5:6], lncol[:, 3:4],
                                                  1.0 / H, None, ALU.mult),
                 reads=[r_lncol], writes=[r_lncol])
            k.op("dve", lambda v: v.tensor_scalar(lncol[:, 6:7], lncol[:, 4:5],
                                                  lncol[:, 4:5], None, ALU.mult),
                 reads=[r_lncol], writes=[r_lncol])
            k.op("dve", lambda v: v.tensor_sub(lncol[:, 7:8], lncol[:, 5:6],
                                               lncol[:, 6:7]),
                 reads=[r_lncol], writes=[r_lncol])
            k.op("act", lambda s_: s_.activation(lncol[:, 6:7], lncol[:, 7:8],
                                                 AF.Ln, bias=epsb[:, 0:1]),
                 reads=[r_lncol], writes=[r_lncol])
            k.op("act", lambda s_: s_.activation(lncol[:, 7:8], lncol[:, 6:7],
                                                 AF.Exp, scale=-0.5),
                 reads=[r_lncol], writes=[r_lncol])
            def fn_outop(v):
                return v.tensor_scalar(xout[:], x2r_b[:], lncol[:, 4:5],
                                       lncol[:, 7:8], ALU.add, ALU.mult)
            k.op("dve", fn_outop, reads=[r_x2r, r_lncol], writes=[r_xout])
            def fn_os(g, half=half, tc=tc):
                row0 = half * OH + tc * 128
                return g.dma_start(
                    out=bass.AP(out.tensor, row0 * H, [[H, 128], [1, H]]),
                    in_=xout[:])
            k.op("sp", fn_os, reads=[r_xout], inc=16)
        if "c" not in PH:
            for tc in range(OWN // 128):
                def fn_os0(g, tc=tc):
                    return g.dma_start(
                        out=bass.AP(out.tensor, tc * 128 * H, [[H, 128], [1, H]]),
                        in_=xout[:])
                k.op("sp", fn_os0, reads=[r_xout], inc=16)
        for half in range(2):
            if "c" not in PH:
                break
            k.op("pool", lambda g: g.memset(accv[:, 0:acc_cols], 0.0),
                 writes=[r_acc])

            experts = (list(range(E)) + ["sh"]) if half == 0 \
                else (["sh"] + list(range(E)))
            for eidx, e in enumerate(experts):
                if e == "sh":
                    n_ic = SIC
                    II = SI
                    sofs = 0
                    u_ofs, d_ofs = FC * SI, 2 * FC * SI
                    wrs = [r_exp[0], r_exp[1]]
                    wrs_g = [r_exp[0]]
                    wrs_d = [r_exp[1]]
                    def fn_sh_g(g):
                        return g.dma_start(
                            out=EXP_AR[:, 0:FC * SI].rearrange(
                                "p (f i) -> p f i", i=SI),
                            in_=bass.AP(sgT.tensor, 0,
                                        [[SI, 128], [128 * SI, FC], [1, SI]]))
                    def fn_sh_u1(g):
                        return g.dma_start(
                            out=EXP_AR[:, FC * SI:FC * SI + 4 * SI].rearrange(
                                "p (f i) -> p f i", i=SI),
                            in_=bass.AP(suT.tensor, 0,
                                        [[SI, 128], [128 * SI, 4], [1, SI]]))
                    def fn_sh_u2(g):
                        return g.dma_start(
                            out=EXP_AR[:, FC * SI + 4 * SI:2 * FC * SI].rearrange(
                                "p (f i) -> p f i", i=SI),
                            in_=bass.AP(suT.tensor, 4 * 128 * SI,
                                        [[SI, 128], [128 * SI, 4], [1, SI]]))
                    def fn_sh_d(g):
                        return g.dma_start(
                            out=EXP_AR[:, 2 * FC * SI:2 * FC * SI + SIC * H].rearrange(
                                "p (f i) -> p f i", i=H),
                            in_=bass.AP(sdT.tensor, 0,
                                        [[H, 128], [128 * H, SIC], [1, H]]))
                else:
                    n_ic = EIC
                    II = EI
                    sofs = (eidx % 2) * EXPCOLS
                    u_ofs, d_ofs = FC * EI, 2 * FC * EI
                    wrs = [r_exp[eidx % 2]]
                    wrs_g = wrs
                    wrs_d = wrs
                    def fn_wl1(g, e=e, sofs=sofs):
                        return g.dma_start(
                            out=EXP_AR[:, sofs:sofs + FC * EI].rearrange(
                                "p (f i) -> p f i", i=EI),
                            in_=bass.AP(egT.tensor, e * H * EI,
                                        [[EI, 128], [128 * EI, FC], [1, EI]]))
                    def fn_wl2(g, e=e, sofs=sofs):
                        return g.dma_start(
                            out=EXP_AR[:, sofs + FC * EI:sofs + 2 * FC * EI].rearrange(
                                "p (f i) -> p f i", i=EI),
                            in_=bass.AP(euT.tensor, e * H * EI,
                                        [[EI, 128], [128 * EI, FC], [1, EI]]))
                    def fn_wl3(g, e=e, sofs=sofs):
                        return g.dma_start(
                            out=EXP_AR[:, sofs + 2 * FC * EI:
                                       sofs + 2 * FC * EI + EIC * H].rearrange(
                                "p (f i) -> p f i", i=H),
                            in_=bass.AP(edT.tensor, e * EI * H,
                                        [[H, 128], [128 * H, EIC], [1, H]]))
                if e == "sh":
                    if half == 0:
                        def fn_sh_a(g, sem, psem, f1=fn_sh_g, f2=fn_sh_u1):
                            f1(g).then_inc(sem, 16)
                            f2(g).then_inc(sem, 16)
                        def fn_sh_b(g, sem, psem, f1=fn_sh_u2, f2=fn_sh_d):
                            f1(g).then_inc(sem, 16)
                            f2(g).then_inc(sem, 16)
                        k.raw("pool", fn_sh_a, writes=[r_exp[0]], incs=32)
                        k.raw("pool", fn_sh_b, writes=[r_exp[1]], incs=32)
                else:
                    def fn_wl123(g, sem, psem, f1=fn_wl1, f2=fn_wl2, f3=fn_wl3):
                        f1(g).then_inc(sem, 16)
                        f2(g).then_inc(sem, 16)
                        f3(g).then_inc(sem, 16)
                    k.raw("pool", fn_wl123, writes=wrs, incs=48)

                for t in range(TPH):
                    th0 = half * OH + t * CT
                    hbp = t % 2
                    r_warx = (r_war_t0 if (half == 0 and t == 0)
                              else r_war_h[half])
                    for ioc in range(n_ic):
                        pn = ioc % 2
                        for fic in range(FC):
                            def fn_gm(t_, fic=fic, ioc=ioc, pn=pn, sofs=sofs,
                                      th0=th0, II=II):
                                return t_.matmul(
                                    PN[:, pn * 512:pn * 512 + CT],
                                    EXP_AR[:, sofs + fic * II + ioc * 128:
                                           sofs + fic * II + ioc * 128 + 128],
                                    W_AR[:, fic * OWN + th0:fic * OWN + th0 + CT],
                                    start=(fic == 0), stop=(fic == FC - 1))
                            k.op("pe", fn_gm, reads=[r_warx] + wrs_g,
                                 writes=[r_PN[pn]])
                        def fn_silu(s_, pn=pn):
                            return s_.activation(silu_b[:, 0:CT],
                                                 PN[:, pn * 512:pn * 512 + CT],
                                                 AF.Silu)
                        k.op("act", fn_silu, reads=[r_PN[pn]], writes=[r_silu])
                        for fic in range(FC):
                            def fn_um(t_, fic=fic, ioc=ioc, pn=pn, sofs=sofs,
                                      th0=th0, II=II, u_ofs=u_ofs):
                                return t_.matmul(
                                    PW[:, pn * 512:pn * 512 + CT],
                                    EXP_AR[:, sofs + u_ofs + fic * II + ioc * 128:
                                           sofs + u_ofs + fic * II + ioc * 128 + 128],
                                    W_AR[:, fic * OWN + th0:fic * OWN + th0 + CT],
                                    start=(fic == 0), stop=(fic == FC - 1))
                            wu = (wrs if e != "sh"
                                  else ([r_exp[0]] if fic < 4 else [r_exp[1]]))
                            k.op("pe", fn_um, reads=[r_warx] + wu,
                                 writes=[r_PW[pn]])
                        def fn_h(v, pn=pn, ioc=ioc, hbp=hbp):
                            return v.tensor_tensor(
                                h_b[hbp][:, ioc * CT:ioc * CT + CT],
                                silu_b[:, 0:CT], PW[:, pn * 512:pn * 512 + CT],
                                ALU.mult)
                        k.op("dve", fn_h, reads=[r_silu, r_PW[pn]], writes=[r_hb[hbp]])
                    if half == 0 and eidx == 0 and t == 0:
                        emit_stats_tail()
                    for tc in range(CTC):
                        oc = (half * OH + t * CT + tc * 128) // 128
                        if e != "sh":
                            def fn_wt(v, oc=oc, e=e):
                                return v.tensor_tensor(
                                    wtil[:],
                                    sel_own[:, oc * E + e:oc * E + e + 1],
                                    wmeanb[:, e:e + 1], ALU.mult)
                            k.op("dve", fn_wt, reads=[r_selown, r_wmeanb],
                                 writes=[r_wtil])
                        for foh in range(NFO):
                            pw = foh % 2
                            for ioc in range(n_ic):
                                def fn_dm(t_, ioc=ioc, tc=tc, foh=foh, pw=pw,
                                          sofs=sofs, d_ofs=d_ofs, hbp=hbp, n_ic=n_ic):
                                    return t_.matmul(
                                        PW[:, pw * 512:pw * 512 + FOW],
                                        h_b[hbp][:, ioc * CT + tc * 128:
                                                 ioc * CT + tc * 128 + 128],
                                        EXP_AR[:, sofs + d_ofs + ioc * H + foh * FOW:
                                               sofs + d_ofs + ioc * H + foh * FOW + FOW],
                                        start=(ioc == 0), stop=(ioc == n_ic - 1))
                                k.op("pe", fn_dm, reads=[r_hb[hbp]] + wrs_d,
                                     writes=[r_PW[pw]])
                            col0 = ((t * CT + tc * 128) // 128) * H + foh * FOW
                            if e == "sh":
                                def fn_acc(v, pw=pw, col0=col0):
                                    return v.tensor_add(
                                        accv[:, col0:col0 + FOW],
                                        accv[:, col0:col0 + FOW],
                                        PW[:, pw * 512:pw * 512 + FOW])
                                k.op("dve", fn_acc, reads=[r_PW[pw]], writes=[r_acc])
                            else:
                                def fn_acc(v, pw=pw, col0=col0):
                                    return v.scalar_tensor_tensor(
                                        accv[:, col0:col0 + FOW],
                                        PW[:, pw * 512:pw * 512 + FOW], wtil[:],
                                        accv[:, col0:col0 + FOW],
                                        ALU.mult, ALU.add)
                                k.op("dve", fn_acc, reads=[r_PW[pw], r_wtil],
                                     writes=[r_acc])
                        if eidx == len(experts) - 1:
                            emit_ln2(half, t * CTC + tc)

        emit_all(nc, plan, es)
    return nc, c, dbg_outs



def prep_inputs(cin, hs_full, w_q, w_k, w_v, w_o, gate, memories, memory_norms,
                gate_w, lb_bias, egw, euw, edw, sgw, suw, sdw):
    c = derive(cin)
    S, NH, M, E, H = c["S"], c["NH"], c["M"], c["E"], c["H"]

    wqT = np.ascontiguousarray(w_q.T)
    wkT = np.ascontiguousarray(w_k.T)
    wvT = np.ascontiguousarray(w_v.T)
    woT = np.ascontiguousarray(w_o.T)
    memcat = np.zeros((NH, 64, M * 65), np.float32)
    for m in range(M):
        memcat[:, :, 65 * m:65 * m + 64] = memories[m]
        memcat[:, :, 65 * m + 64] = memory_norms[m]
    wmn = np.zeros((H, NH * M), np.float32)
    for h in range(NH):
        for m in range(M):
            wmn[64 * h:64 * h + 64, h * M + m] = memory_norms[m, h]
    gate_in = gate.reshape(NH, 1).astype(np.float32)
    gwT = np.ascontiguousarray(gate_w.T)
    lbbv = lb_bias.reshape(1, E).astype(np.float32)
    egTv = np.ascontiguousarray(egw.transpose(0, 2, 1))
    euTv = np.ascontiguousarray(euw.transpose(0, 2, 1))
    edTv = np.ascontiguousarray(edw.transpose(0, 2, 1))
    sgTv = np.ascontiguousarray(sgw.T)
    suTv = np.ascontiguousarray(suw.T)
    sdTv = np.ascontiguousarray(sdw.T)

    SO = c["SO"]
    n_batch = hs_full.shape[0]
    hsT_first = []
    hsT_second = []
    for b in range(n_batch):
        hsTb = hs_full[b].T
        hsT_first.append(np.ascontiguousarray(hsTb[:, 0:SO]))
        hsT_second.append(np.ascontiguousarray(hsTb[:, SO:2 * SO]))

    in_maps = []
    placements = []
    for cid in range(8):
        b = cid // 2
        half = cid % 2
        m = {
            "hsT_own": hsT_second[b] if half else hsT_first[b],
            "hsT_state": hsT_first[b],
            "hs_own": np.ascontiguousarray(hs_full[b, half * SO:(half + 1) * SO]),
            "wqT": wqT, "wkT": wkT, "wvT": wvT, "woT": woT,
            "memcat": memcat, "wmn": wmn, "gate_in": gate_in,
            "gwT": gwT, "lbb": lbbv,
            "egT": egTv, "euT": euTv, "edT": edTv,
            "sgT": sgTv, "suT": suTv, "sdT": sdTv,
            "cfgv": np.array([[half, 0, 0, 0, 0, 0, 0, 0]], np.int32),
        }
        in_maps.append(m)
        placements.append((b, half * SO))
    return in_maps, placements


_BUILD_CACHE = {}


def kernel(L0_attn_hidden_states, L0_attn_w_q, L0_attn_w_k, L0_attn_w_v,
           L0_attn_w_o, L0_attn_gate, L0_attn_memories, L0_attn_memory_norms,
           L0_ffn_gate_w, L0_ffn_lb_bias, L0_ffn_expert_gate_w,
           L0_ffn_expert_up_w, L0_ffn_expert_down_w, L0_ffn_shared_gate_w,
           L0_ffn_shared_up_w, L0_ffn_shared_down_w):
    cfg = real_cfg()
    c = derive(cfg)
    hs = np.asarray(L0_attn_hidden_states, np.float32)
    in_maps, placements = prep_inputs(
        cfg, hs,
        np.asarray(L0_attn_w_q, np.float32), np.asarray(L0_attn_w_k, np.float32),
        np.asarray(L0_attn_w_v, np.float32), np.asarray(L0_attn_w_o, np.float32),
        np.asarray(L0_attn_gate, np.float32),
        np.asarray(L0_attn_memories, np.float32),
        np.asarray(L0_attn_memory_norms, np.float32),
        np.asarray(L0_ffn_gate_w, np.float32),
        np.asarray(L0_ffn_lb_bias, np.float32),
        np.asarray(L0_ffn_expert_gate_w, np.float32),
        np.asarray(L0_ffn_expert_up_w, np.float32),
        np.asarray(L0_ffn_expert_down_w, np.float32),
        np.asarray(L0_ffn_shared_gate_w, np.float32),
        np.asarray(L0_ffn_shared_up_w, np.float32),
        np.asarray(L0_ffn_shared_down_w, np.float32))

    if "real" not in _BUILD_CACHE:
        _BUILD_CACHE["real"] = build(cfg, dbg=False)
    nc, _, _ = _BUILD_CACHE["real"]

    res = run_bass_kernel_spmd(nc, in_maps, core_ids=list(range(8)))

    outv = np.zeros((4, c["S"], c["H"]), np.float32)
    for cid in range(8):
        own_b, seg_in = placements[cid]
        outv[own_b, seg_in:seg_in + c["OWN"]] = res.results[cid]["out"]
    return outv
